# version 30
# baseline (speedup 1.0000x reference)
import numpy as np
import concourse.bass as bass
import concourse.mybir as mybir
from concourse.bass_utils import run_bass_kernel_spmd
import ml_dtypes

BFNP = ml_dtypes.bfloat16
F32 = mybir.dt.float32
BF16 = mybir.dt.bfloat16
AF = mybir.ActivationFunctionType
ALU = mybir.AluOpType
AX = mybir.AxisListType

B, T, C, H, D = 4, 4096, 1024, 16, 64
NCORE = 8


def build(TM=2048, TC=2048, SUB=512, debug=(), ph=99, trunc=None):
    NSUB = TM // SUB
    NT_S = SUB // 128
    NTB_S = SUB // 512
    NCH_S = NT_S
    NT_C = TC // 128
    NG = 32 * NT_S
    nc = bass.Bass("TRN2", target_bir_lowering=False)

    def din(name, shape, dt=F32):
        return nc.dram_tensor(name, shape, dt, kind="ExternalInput").ap()

    xm = din("xm", [TM, C])
    xc = din("xc", [TC, C])
    flag = din("flag", [128, 1])
    wqT = din("wqT", [C, C]); wkT = din("wkT", [C, C])
    wv = din("wv", [C, C])
    projP = din("projP", [128, 1024])
    bqc = din("bqc", [128, 8]); bkc = din("bkc", [128, 8])
    bv = din("bv", [1, C])
    wout = din("wout", [C, C]); bout = din("bout", [1, C])
    protosP = din("protosP", [128, 256])
    ball = din("ball", [C, 512])
    afl = din("afl", [512, C])
    fbias = din("fbias", [32, C])
    temp = din("temp", [1, 1])
    tri = din("tri", [128, 512], BF16)
    idn64b = din("idn64b", [64, 64], BF16)
    idn128b = din("idn128b", [128, 128], BF16)
    idn32f = din("idn32f", [32, 32])
    ones_r512f = din("ones_r512f", [1, 512])
    ones_r512b = din("ones_r512b", [1, 512], BF16)
    ones_c128b = din("ones_c128b", [1, 128], BF16)
    ones_r128f = din("ones_r128f", [1, 128])
    ones_col128f = din("ones_col128f", [128, 1])
    E2c = din("E2c", [32, 512], BF16)
    E3c = din("E3c", [16, C], BF16)

    y_out = nc.dram_tensor("y", [TM, C], F32, kind="ExternalOutput").ap()

    scr_hi = nc.dram_tensor("scr_hi", [TM, C], BF16, kind="Internal").ap()
    scr_lo = nc.dram_tensor("scr_lo", [TM, C], BF16, kind="Internal").ap()
    scr_c = nc.dram_tensor("scr_c", [TC, C], BF16, kind="Internal").ap()

    def sb(name, shape, dt):
        return nc.alloc_sbuf_tensor(name, shape, dt).ap()

    xt = [sb(f"xt{i}", [128, C], F32) for i in range(4)]
    xn32 = [sb(f"xn32_{i}", [128, C], F32) for i in range(2)]
    hitm = [sb(f"hitm{i}", [128, C], BF16) for i in range(2)]
    lotm = [sb(f"lotm{i}", [128, C], BF16) for i in range(2)]
    stat6 = [sb(f"stat6_{i}", [128, 12], F32) for i in range(4)]
    mv = [sb(f"mv{i}", [128, 2], F32) for i in range(4)]
    msr = [sb(f"msr{i}", [128, 2], F32) for i in range(4)]
    ln1 = [sb(f"ln1_{i}", [128, 1], F32) for i in range(4)]
    x2cb = [sb(f"x2c{i}", [128, NT_S], F32) for i in range(2)]

    hi_fm = [sb(f"hi_fm{cb}", [128, SUB], BF16) for cb in range(8)]
    lo_fm = [sb(f"lo_fm{cb}", [128, SUB], BF16) for cb in range(8)]
    ctxu = lo_fm
    qp_fm = [sb(f"qp_fm{cb}", [128, SUB], BF16) for cb in range(8)]
    kp_fm = [sb(f"kp_fm{cb}", [128, SUB], BF16) for cb in range(8)]

    kt = [sb(f"kt{i}", [128, C], BF16) for i in range(NT_S)]
    ctxn = [kt[cb // 2][:, (cb % 2) * 512:(cb % 2) * 512 + 512] for cb in range(8)]
    vt = [sb(f"vt{i}", [128, 1040], BF16) for i in range(NT_S)]
    hgt = [sb(f"hgt{p}", [128, SUB], BF16) for p in range(4)]
    grb = sb("grb", [128, 512], BF16)

    W_QP = [sb(f"W_QP{cb}", [128, C], BF16) for cb in range(8)]
    W_KP = [sb(f"W_KP{cb}", [128, C], BF16) for cb in range(8)]
    wvb = [sb(f"wvb{cb}", [128, C], BF16) for cb in range(8)]
    woutb = [sb(f"woutb{cb}", [128, C], BF16) for cb in range(8)]
    ballb = [sb(f"ballb{cb}", [128, 512], BF16) for cb in range(8)]
    aflb = [sb(f"aflb{p}", [128, C], BF16) for p in range(4)]
    fbiasb = sb("fbiasb", [32, C], BF16)
    boutr = sb("boutr", [1, C], BF16)
    bvr = sb("bvr", [1, C], BF16)
    bqpr = sb("bqpr", [1, C], BF16)
    bkpr = sb("bkpr", [1, C], BF16)
    projs = sb("projs", [128, 1024], BF16)
    bqcb = sb("bqcb", [128, 8], BF16)
    bkcb = sb("bkcb", [128, 8], BF16)
    qkst = [sb(f"qkst{i}", [64, 1024], BF16) for i in range(2)]

    phTa = sb("phTa", [128, 256], BF16)
    plTa = sb("plTa", [128, 256], BF16)
    p2r = sb("p2r", [1, 32], F32)
    p2h = sb("p2h", [1, 32], BF16)
    p2l = sb("p2l", [1, 32], BF16)
    tprep = sb("tprep", [1, 4], F32)
    tcol = sb("tcol", [128, 1], F32)
    flagc = sb("flagc", [128, 1], F32)
    S_acc = sb("S_acc", [128, 520], F32)
    S_seed = sb("S_seed", [128, 520], BF16)
    Cend = sb("Cend", [128, 520], BF16)
    Csnap = sb("Csnap", [64, 260], BF16)
    Ssb = [sb(f"Ssb{i}", [64, 260], BF16) for i in range(2)]
    cendt = sb("cendt", [64, 65], F32)

    da = sb("da", [32, SUB], F32)
    dt1 = sb("dt1", [128, NG], F32)

    Lg = sb("Lg", [128, NG], F32)
    m1g = sb("m1g", [128, NT_S], F32)
    m2g = sb("m2g", [128, NT_S], F32)
    sel1 = sb("sel1", [128, NG], F32)
    l2b = sb("l2b", [128, NG], F32)
    dt2 = sel1
    dt3 = l2b
    eex = sb("eex", [128, NG], F32)
    egb = eex
    deng = sb("deng", [128, NT_S], F32)
    rdeng = sb("rdeng", [128, NT_S], F32)
    gtm = sb("gtm", [128, NG], BF16)
    gatesT = sb("gatesT", [32, SUB], BF16)

    Asm = [sb(f"Asm{i}", [128, 512], BF16) for i in range(NTB_S)]
    hps_dbg = nc.dram_tensor("d_hps", [128, 512], BF16, kind="ExternalOutput").ap() if "d_hps" in debug else None
    grb_dbg = nc.dram_tensor("d_grb", [128, 512], BF16, kind="ExternalOutput").ap() if "d_hps" in debug else None
    nall = sb("nall", [16, SUB], BF16)
    rnb = sb("rnb", [16, SUB], BF16)
    stage = [sb(f"stage{i}", [1, 512], BF16) for i in range(2)]
    outb = xn32

    tri_s = sb("tri_s", [128, 512], BF16)
    idn128b_s = sb("idn128b_s", [128, 128], BF16)
    idn32f_s = sb("idn32f_s", [32, 32], F32)
    onesr512b_s = sb("onesr512b_s", [1, 512], BF16)
    onesc128b_s = sb("onesc128b_s", [1, 128], BF16)
    onesr128f_s = sb("onesr128f_s", [1, 128], F32)
    onescol128f_s = sb("onescol128f_s", [128, 1], F32)
    E2s = sb("E2s", [32, 512], BF16)
    E3s = sb("E3s", [16, C], BF16)
    tempf = sb("tempf", [1, 1], F32)

    ktA = [kt[0], kt[1]]
    vtA = [vt[0], vt[1]]

    def psn(name, shape, dt=F32):
        return nc.alloc_psum_tensor(name, shape, dt).ap()

    p_mm = [psn("p_mm0", [128, 512]), psn("p_mm1", [128, 512])]
    p_sc = psn("p_sc", [128, 512])
    p_ctx = [psn("p_ctx0", [128, 512]), psn("p_ctx1", [128, 512])]
    p_st = psn("p_st", [128, 512])
    p_bc = psn("p_bc", [128, 512])
    p_trb = psn("p_trb", [128, 1024], BF16)

    ops = {k: [] for k in ("sync", "tensor", "vector", "scalar", "gpsimd")}
    seqc = [0]

    def em(eng, fn):
        seqc[0] += 1
        ops[eng].append((seqc[0], fn))

    class Sem:
        def __init__(self, name):
            self.h = nc.alloc_semaphore(name)
            self.n = 0

        def inc(self, k=1):
            self.n += k
            return self.n

    s_w0 = Sem("s_w0")
    s_wcA = Sem("s_wcA")
    s_wcB = Sem("s_wcB")
    s_wcC = Sem("s_wcC")
    s_wr = [Sem("s_wrA"), Sem("s_wrB")]
    s_xb = [Sem(f"s_x{i}") for i in range(8)]
    s_scr = Sem("s_scr")
    s_scrA = Sem("s_scrA")
    s_fmA = [Sem("s_fmA0"), Sem("s_fmA1")]
    s_fm = Sem("s_fm")
    s_pe = Sem("s_pe")
    s_ve = Sem("s_ve")
    s_se = Sem("s_se")
    s_gp = Sem("s_gp")
    s_gd = Sem("s_gd")
    s_out = Sem("s_out")

    ENG_SEM = {"tensor": s_pe, "vector": s_ve, "scalar": s_se, "gpsimd": s_gp}

    def W(eng, *deps):
        for sem, val in deps:
            if val and val > 0:
                em(eng, lambda e, s=sem.h, v=val: e.wait_ge(s, v))

    def dma(eng_name, sem, out, in_, transpose=False):
        v = sem.inc(16)
        em(eng_name, lambda e, o=out, i=in_, s=sem.h, tr=transpose:
           e.dma_start(out=o, in_=i, transpose=tr).then_inc(s, 16))
        return v

    def mark(eng):
        sem = ENG_SEM[eng]
        v = sem.inc(1)
        _, prev = ops[eng].pop()
        em(eng, lambda e, p=prev, s=sem.h: p(e).then_inc(s, 1))
        return v

    def op(eng, fn):
        em(eng, fn)

    def edge(eng):
        sem = ENG_SEM[eng]
        v = mark(eng)
        W(eng, (sem, v))
        return v

    bank_war = {}

    def use_bank(name):
        for sem, v in bank_war.get(name, ()):
            W("tensor", (sem, v))

    def done_bank(name, *marks):
        bank_war[name] = [(ENG_SEM[eng], v) for eng, v in marks]

    cpool = sb("cpool", [128, 2], F32)
    nc.const_aps.register(F32, 1e-5, cpool[:, 0:1])

    for dst, src in [(tri_s, tri), (idn128b_s, idn128b),
                     (idn32f_s, idn32f),
                     (onesr512b_s, ones_r512b), (onesc128b_s, ones_c128b),
                     (onesr128f_s, ones_r128f),
                     (onescol128f_s, ones_col128f),
                     (E2s, E2c), (E3s, E3c), (tempf, temp),
                     (flagc, flag)]:
        dma("sync", s_w0, dst, src)
    ld_small = s_w0.n

    op("vector", lambda e: e.memset(cpool[:, 0:1], 1e-5))
    ve_cpool = mark("vector")

    W("scalar", (s_w0, ld_small), (s_ve, ve_cpool))
    op("scalar", lambda e: e.activation(tprep[0:1, 0:1], tempf, AF.Abs))
    se_abs = mark("scalar")
    W("vector", (s_se, se_abs))
    op("vector", lambda e: e.tensor_scalar_max(tprep[0:1, 1:2], tprep[0:1, 0:1], 0.1))
    edge("vector")
    op("vector", lambda e: e.reciprocal(tprep[0:1, 2:3], tprep[0:1, 1:2]))
    edge("vector")
    op("vector", lambda e: e.tensor_scalar_mul(tprep[0:1, 3:4], tprep[0:1, 2:3], -0.5))
    ve_t = mark("vector")
    W("tensor", (s_ve, ve_t))
    use_bank("p_bc")
    op("tensor", lambda e: e.matmul(p_bc[0:128, 0:1], onesr128f_s, tprep[0:1, 3:4], start=True, stop=True))
    pe_t = mark("tensor")
    W("vector", (s_pe, pe_t))
    op("vector", lambda e: e.tensor_copy(tcol, p_bc[0:128, 0:1]))
    ve_tcol = mark("vector")
    done_bank("p_bc", ("vector", ve_tcol))

    op("gpsimd", lambda e: e.engine_nop())
    gp_smallcast = mark("gpsimd")

    def cast_loads(pairs, sem):
        for dst_ap, src_sl in pairs:
            dma("gpsimd", sem, dst_ap, src_sl)
        return sem.n

    pairs = [(projs, projP), (bqcb, bqc), (bkcb, bkc), (bvr, bv)]
    for cb in range(8):
        pairs.append((wvb[cb][:, :], wv[cb * 128:(cb + 1) * 128, :]))
    gp_wcast_n = cast_loads(pairs, s_wcA)

    pe_wdone = [0]

    def emit_wfusion(qk):
        Wdst = W_QP if qk == 0 else W_KP
        wsrc = wqT if qk == 0 else wkT
        if pe_wdone[0]:
            W("gpsimd", (s_pe, pe_wdone[0]))
        stg = []
        for cb in range(8):
            stg.append((qp_fm[cb][:, 0:512], wsrc[cb * 128:(cb + 1) * 128, 0:512]))
            stg.append((kp_fm[cb][:, 0:512], wsrc[cb * 128:(cb + 1) * 128, 512:1024]))
        v_stg = cast_loads(stg, s_wcB)
        W("tensor", (s_wcB, v_stg))
        for cbc in range(8):
            buf = qp_fm if cbc < 4 else kp_fm
            for gq in range(2):
                rb = (cbc * 2 + gq) % 2
                bname = f"p_mm{rb}"
                use_bank(bname)
                for gi in range(4):
                    g = gq * 4 + gi
                    lhsT = buf[g][:, (cbc % 4) * 128:(cbc % 4) * 128 + 128]
                    rhs = projs[:, g * 128:(g + 1) * 128]
                    op("tensor", lambda e, o=p_mm[rb][:, gi * 128:gi * 128 + 128], l=lhsT, rr=rhs:
                       e.matmul(o, l, rr, start=True, stop=True))
                pe_w = mark("tensor")
                W("vector", (s_pe, pe_w))
                op("vector", lambda e, d=Wdst[cbc][:, gq * 512:gq * 512 + 512], s=p_mm[rb][:, 0:512]:
                   e.tensor_copy(d, s))
                ve_w = mark("vector")
                done_bank(bname, ("vector", ve_w))
        pe_wdone[0] = pe_w
        bcb = bqcb if qk == 0 else bkcb
        dstr = bqpr if qk == 0 else bkpr
        ve_b = 0
        for half in range(2):
            use_bank("p_mm1")
            for gg in range(4):
                g = half * 4 + gg
                lhsT = bcb[:, g:g + 1]
                rhs = projs[:, g * 128:(g + 1) * 128]
                o_ap = p_mm[1][0:1, gg * 128:gg * 128 + 128]
                op("tensor", lambda e, o=o_ap, l=lhsT, rr=rhs:
                   e.matmul(o, l, rr, start=True, stop=True))
            pe_b = mark("tensor")
            W("vector", (s_pe, pe_b))
            op("vector", lambda e, d=dstr[0:1, half * 512:half * 512 + 512]:
               e.tensor_copy(d, p_mm[1][0:1, 0:512]))
            ve_b = mark("vector")
            done_bank("p_mm1", ("vector", ve_b))
        return ve_b

    psall = grb.bitcast(F32)[:, 0:256]
    sqall = Asm[0].bitcast(F32)[:, 0:256]
    use_bank("p_bc")
    v = dma("sync", s_wr[0], psall, protosP)
    W("vector", (s_wr[0], v))
    op("vector", lambda e: e.tensor_scalar_mul(psall, psall, -2.0))
    edge("vector")
    op("vector", lambda e: e.tensor_copy(phTa, psall))
    edge("vector")
    op("vector", lambda e: e.tensor_tensor(plTa, psall, phTa, ALU.subtract))
    ve_lo2 = mark("vector")
    W("scalar", (s_ve, ve_lo2))
    op("scalar", lambda e: e.activation(sqall, psall, AF.Square, scale=-0.5))
    se_sq = mark("scalar")
    W("tensor", (s_se, se_sq))
    op("tensor", lambda e: e.matmul(p_bc[0:1, 0:256], onescol128f_s, sqall, start=True, stop=True))
    pe_p2 = mark("tensor")
    gp_protos = s_gp.n
    W("vector", (s_pe, pe_p2))
    op("vector", lambda e: e.tensor_reduce(p2r.rearrange("o (c u) -> o c u", u=1),
                                           p_bc[0:1, 0:256].rearrange("o (cb c) -> o c cb", c=32),
                                           AX.X, ALU.add))
    ve_p2a = mark("vector")
    W("gpsimd", (s_ve, ve_p2a))
    op("gpsimd", lambda e: e.tensor_copy(p2h, p2r))
    gp_p2h = mark("gpsimd")
    W("vector", (s_gp, gp_p2h))
    op("vector", lambda e: e.tensor_tensor(p2l, p2r, p2h, ALU.subtract))
    ve_p2 = mark("vector")
    done_bank("p_bc", ("vector", ve_p2))

    W("tensor", (s_wcA, gp_wcast_n), (s_gp, gp_smallcast))
    ve_bias_rows = emit_wfusion(1)



    scratch_war = [0, 0]
    xt_war = [0, 0, 0, 0]
    ln_pend = {}

    ln_vx = [0, 0, 0, 0]

    def emit_ln_load(x_src_rows, rot):
        if xt_war[rot]:
            W("scalar", (s_ve, xt_war[rot]))
        ln_vx[rot] = dma("scalar", s_xb[rot], xt[rot][:, :], x_src_rows)

    def emit_ln_stats(rot, want_x2=None, pre_wait=None):
        W("vector", (s_xb[rot], ln_vx[rot]))
        op("vector", lambda e, r=rot: e.bn_stats(stat6[r][:, 0:6], xt[r][:, 0:512]))
        op("vector", lambda e, r=rot: e.bn_stats(stat6[r][:, 6:12], xt[r][:, 512:1024]))
        edge("vector")
        op("vector", lambda e, r=rot: e.bn_aggr(mv[r], stat6[r]))
        ve_mv = mark("vector")
        W("scalar", (s_ve, ve_mv))
        op("scalar", lambda e, r=rot: e.activation(ln1[r], mv[r][:, 1:2], AF.Ln, bias=1e-5))
        edge("scalar")
        op("scalar", lambda e, r=rot: e.activation(msr[r][:, 1:2], ln1[r], AF.Exp, scale=-0.5))
        edge("scalar")
        op("scalar", lambda e, r=rot: e.copy(msr[r][:, 0:1], mv[r][:, 0:1]))
        se_msr = mark("scalar")
        if want_x2 is not None:
            op("scalar", lambda e, r=rot: e.activation(ln1[r], msr[r][:, 1:2], AF.Square))
            se_sq2 = mark("scalar")
        else:
            se_sq2 = se_msr
        ln_pend[rot] = (se_msr, se_sq2, want_x2, pre_wait)

    def emit_ln_A(x_src_rows, rot, want_x2=None, pre_wait=None):
        emit_ln_load(x_src_rows, rot)
        emit_ln_stats(rot, want_x2=want_x2, pre_wait=pre_wait)

    ln_bseq = [0]

    def emit_ln_B(rot, want_lo=True):
        se_msr, se_sq2, want_x2, pre_wait = ln_pend.pop(rot)
        brot = ln_bseq[0] % 2
        ln_bseq[0] += 1
        if pre_wait is not None:
            W("vector", *pre_wait)
        W("vector", (s_se, se_sq2))
        if want_x2 is not None:
            op("vector", lambda e, r=rot, d=want_x2: e.tensor_tensor(d, mv[r][:, 1:2], ln1[r], ALU.mult))
            edge("vector")
            op("vector", lambda e, d=want_x2: e.tensor_scalar_mul(d, d, 1024.0))
        op("vector", lambda e, r=rot, b=brot: e.tensor_scalar(xn32[b], xt[r], msr[r][:, 0:1], msr[r][:, 1:2],
                                                              ALU.subtract, ALU.mult))
        ve_xn = mark("vector")
        xt_war[rot] = ve_xn
        W("vector", (s_ve, ve_xn))
        if scratch_war[brot]:
            W("vector", (s_scr, s_scr.n), (s_scrA, s_scrA.n))
        op("vector", lambda e, b=brot: e.tensor_copy(hitm[b], xn32[b]))
        if want_lo:
            edge("vector")
            op("vector", lambda e, b=brot: e.tensor_tensor(lotm[b], xn32[b], hitm[b], ALU.subtract))
        ve_lo = mark("vector")
        return ve_lo, brot

    def emit_kv_tile(fm_slices, kdst, vdst):
        marks = {}
        v3 = vdst.rearrange("p (h e) -> p h e", e=65)
        for mh in range(2):
            bname = f"p_mm{mh}"
            rot = p_mm[mh]
            use_bank(bname)
            for cb in range(8):
                l = fm_slices[cb]
                rr = W_KP[cb][:, mh * 512:mh * 512 + 512]
                op("tensor", lambda e, o=rot, l_=l, r_=rr, st=(cb == 0): e.matmul(o, l_, r_, start=st, stop=False))
            bslice = bkpr[0:1, mh * 512:mh * 512 + 512]
            op("tensor", lambda e, o=rot, b_=bslice: e.matmul(o, onesc128b_s, b_, start=False, stop=True))
            pe_k = mark("tensor")
            W("scalar", (s_pe, pe_k))
            op("scalar", lambda e, d=kdst[:, mh * 512:mh * 512 + 512], s=rot:
               e.activation(d, s, AF.Relu, scale=0.125))
            se_k = mark("scalar")
            done_bank(bname, ("scalar", se_k))
            marks[("k", mh)] = se_k
        for mh in range(2):
            bname = f"p_mm{mh}"
            rot = p_mm[mh]
            use_bank(bname)
            for cb in range(8):
                l = fm_slices[cb]
                rr = wvb[cb][:, mh * 512:mh * 512 + 512]
                op("tensor", lambda e, o=rot, l_=l, r_=rr, st=(cb == 0): e.matmul(o, l_, r_, start=st, stop=False))
            bslice = bvr[0:1, mh * 512:mh * 512 + 512]
            op("tensor", lambda e, o=rot, b_=bslice: e.matmul(o, onesc128b_s, b_, start=False, stop=True))
            pe_v = mark("tensor")
            W("scalar", (s_pe, pe_v))
            dst8 = v3[:, mh * 8:mh * 8 + 8, 0:64]
            src8 = rot.rearrange("p (h d) -> p h d", d=64)
            op("scalar", lambda e, d=dst8, s=src8: e.copy(d, s))
            se_v = mark("scalar")
            done_bank(bname, ("scalar", se_v))
            marks[("v", mh)] = se_v
        return marks

    PH_DONE = [False]

    def s_banks(tt):
        return (p_ctx[0], p_ctx[1]) if tt % 2 == 0 else (p_sc, p_bc)

    def s_bname(tt):
        return ("p_ctx0", "p_ctx1") if tt % 2 == 0 else ("p_sc", "p_bc")

    def s_region(tt, h):
        g = h // 2
        half = (h % 2) * 64
        b0, b1 = s_banks(tt)
        if g < 7:
            return b0[half:half + 64, 65 * g:65 * g + 65]
        return b1[half:half + 64, 0:65]

    if ph <= 0:
        PH_DONE[0] = True
    for i in range(NT_S):
        v3i = vt[i].rearrange("p (h e) -> p h e", e=65)
        op("gpsimd", lambda e, d=v3i[:, :, 64:65]: e.memset(d, 1.0))
    op("gpsimd", lambda e: e.memset(S_acc, 0.0))
    gp_vones = mark("gpsimd")
    if PH_DONE[0]:
        NT_C_eff = 0
    else:
        NT_C_eff = NT_C
    fmA_war = [0, 0]
    scrA_done = []
    sacc_m = [0]
    peS_m = [0]

    def emit_A_group(g2):
        fmbuf = hi_fm if (g2 % 2) == 0 else lo_fm
        if fmA_war[g2 % 2]:
            W("sync", (s_pe, fmA_war[g2 % 2]))
        W("sync", (s_scrA, scrA_done[g2]))
        for cb in range(8):
            src = scr_c[g2 * 512:(g2 + 1) * 512, cb * 128:(cb + 1) * 128]
            dma("sync", s_fmA[g2 % 2], fmbuf[cb][:, 0:512], src, transpose=True)
        v_fmrd = s_fmA[g2 % 2].n
        for tt in range(g2 * 4, g2 * 4 + 4):
            ro = tt % 2
            col = (tt % 4) * 128
            fm_slices = [fmbuf[cb][:, col:col + 128] for cb in range(8)]
            W("tensor", (s_fmA[g2 % 2], v_fmrd))
            marks = emit_kv_tile(fm_slices, ktA[ro], vtA[ro])
            W("tensor", (s_se, marks[("k", 1)]), (s_se, marks[("v", 1)]))
            n0, n1 = s_bname(tt)
            use_bank(n0)
            use_bank(n1)
            for h in range(H):
                l = ktA[ro][:, h * 64:h * 64 + 64]
                rr = vtA[ro].rearrange("p (hh e) -> p hh e", e=65)[:, h, :]
                o = s_region(tt, h)
                op("tensor", lambda e, o_=o, l_=l, r_=rr:
                   e.matmul(o_, l_, r_, start=True, stop=True))
            pe_S = mark("tensor")
            peS_m[0] = pe_S
            b0, b1 = s_banks(tt)
            W("vector", (s_pe, pe_S))
            if sacc_m[0]:
                W("vector", (s_ve, sacc_m[0]))
            op("vector", lambda e, b_=b0: e.tensor_tensor(S_acc[:, 0:455], S_acc[:, 0:455], b_[:, 0:455], ALU.add))
            op("vector", lambda e, b_=b1: e.tensor_tensor(S_acc[:, 455:520], S_acc[:, 455:520], b_[:, 0:65], ALU.add))
            sacc_m[0] = mark("vector")
            done_bank(n0, ("vector", sacc_m[0]))
            done_bank(n1, ("vector", sacc_m[0]))
        fmA_war[g2 % 2] = peS_m[0]

    b1a_gp = [0, 0]
    prev_store = [0, 0]

    def emit_b1a_A(spx, t):
        basex = spx * SUB
        pw = ((s_out, max(prev_store)),) if (spx > 0 and t < 2) else None
        emit_ln_A(xm[basex + t * 128: basex + (t + 1) * 128, :], t % 4,
                  want_x2=x2cb[spx % 2][:, t:t + 1], pre_wait=pw)

    def emit_b1a_B(spx, t):
        basex = spx * SUB
        ve_lo, brot = emit_ln_B(t % 4)
        b1a_gp[brot] = ve_lo
        W("gpsimd", (s_ve, ve_lo))
        v1 = dma("gpsimd", s_scr, scr_hi[basex + t * 128: basex + (t + 1) * 128, :], hitm[brot])
        v2 = dma("gpsimd", s_scr, scr_lo[basex + t * 128: basex + (t + 1) * 128, :], lotm[brot])
        scratch_war[brot] = v2

    jobsA = [(xc[t * 128:(t + 1) * 128, :], None) for t in range(NT_C_eff)]
    if not PH_DONE[0]:
        jobsA += [(xm[t * 128:(t + 1) * 128, :], x2cb[0][:, t:t + 1]) for t in range(NT_S)]
    NJ = len(jobsA)

    def job_load(j):
        if j < NJ:
            emit_ln_load(jobsA[j][0], j % 4)

    def job_stats(j):
        if j < NJ:
            emit_ln_stats(j % 4, want_x2=jobsA[j][1])

    ve_bias_rows = 0
    if NT_C_eff:
        W("tensor", (s_wcA, gp_wcast_n), (s_gp, gp_smallcast))
        ve_bias_rows = emit_wfusion(1)
        for j in range(min(4, NJ)):
            job_load(j)
        job_stats(0)
        job_stats(1)
        for t in range(NT_C_eff):
            ve_lo, brot = emit_ln_B(t % 4, want_lo=False)
            W("scalar", (s_ve, ve_lo))
            scratch_war[brot] = dma("scalar", s_scrA, scr_c[t * 128:(t + 1) * 128, :], hitm[brot])
            if t % 4 == 3:
                scrA_done.append(s_scrA.n)
            job_load(t + 4)
            job_stats(t + 2)
        for j in range(NT_C_eff + 2, NJ):
            job_stats(j)
        W("tensor", (s_ve, ve_bias_rows), (s_gp, gp_vones))
        for t in range(NT_S):
            emit_b1a_B(0, t)
        for g2 in range(NT_C_eff // 4):
            emit_A_group(g2)
    W("vector", (s_w0, ld_small), (s_ve, sacc_m[0]))
    op("vector", lambda e: e.tensor_scalar_mul(S_seed, S_acc, flagc[:, 0:1]))
    ve_seed = mark("vector")
    pe_phaseA_last = peS_m[0]

    if ph <= 1:
        PH_DONE[0] = True

    prev_scan_pe = 0
    gp_cend = [0, 0]
    prev_ctxn_ve = 0
    prev_y_pe = pe_phaseA_last
    prev_hg_ve = 0
    prev_gates_pe = 0
    prev_stage_gd = [0, 0]
    prev_out_ve = [0, 0]
    ve_final = 0
    se_scan_prev = 0

    def emit_b1b_hi(spx):
        basex = spx * SUB
        W("sync", (s_pe, s_pe.n))
        W("sync", (s_scr, s_scr.n))
        for g4 in range(NT_S // 4):
            for cb in range(8):
                src = scr_hi[basex + g4 * 512: basex + (g4 + 1) * 512, cb * 128:(cb + 1) * 128]
                dma("sync", s_fm, hi_fm[cb][:, g4 * 512:g4 * 512 + 512], src, transpose=True)

    def emit_b1b_lo(spx, ve_guard):
        basex = spx * SUB
        if ve_guard:
            W("sync", (s_ve, ve_guard))
        W("sync", (s_scr, s_scr.n))
        for g4 in range(NT_S // 4):
            for cb in range(8):
                src = scr_lo[basex + g4 * 512: basex + (g4 + 1) * 512, cb * 128:(cb + 1) * 128]
                dma("sync", s_fm, lo_fm[cb][:, g4 * 512:g4 * 512 + 512], src, transpose=True)
        return s_fm.n

    if not PH_DONE[0]:
        emit_b1b_hi(0)
        v_fm_next = emit_b1b_lo(0, 0)

    W("gpsimd", (s_scrA, s_scrA.n), (s_scr, s_scr.n))
    emit_wfusion(0)
    ve_wq = s_ve.n
    pairs2 = [(fbiasb, fbias), (boutr, bout)]
    for cb in range(8):
        pairs2.append((woutb[cb][:, :], wout[cb * 128:(cb + 1) * 128, :]))
        pairs2.append((ballb[cb][:, :], ball[cb * 128:(cb + 1) * 128, :]))
    for p in range(4):
        pairs2.append((aflb[p][:, :], afl[p * 128:(p + 1) * 128, :]))
    gp_wcast2_n = cast_loads(pairs2, s_wcC)

    for sp in range(NSUB if not PH_DONE[0] else 0):
        base = sp * SUB
        v_fm_all = v_fm_next
        op("vector", lambda e, x2=x2cb[sp % 2]: e.tensor_copy(deng[:, 0:1], x2[:, 0:1]))
        ve_x2 = mark("vector")

        if ph <= 2:
            continue
        for tb in range(NTB_S):
            use_bank("p_sc")
            W("tensor", (s_fm, v_fm_all), (s_ve, ve_p2), (s_gp, gp_protos))
            first = True
            for cb in range(8):
                phs = phTa[:, cb * 32:(cb + 1) * 32]
                pls = plTa[:, cb * 32:(cb + 1) * 32]
                for (lh, rh) in ((phs, hi_fm[cb]), (pls, hi_fm[cb]), (phs, lo_fm[cb])):
                    rsl = rh[:, tb * 512:tb * 512 + 512]
                    op("tensor", lambda e, l=lh, rr=rsl, st=first:
                       e.matmul(p_sc[0:32, :], l, rr, start=st, stop=False))
                    first = False
            op("tensor", lambda e: e.matmul(p_sc[0:32, :], p2h, onesr512b_s, start=False, stop=False))
            op("tensor", lambda e: e.matmul(p_sc[0:32, :], p2l, onesr512b_s, start=False, stop=True))
            pe_d = mark("tensor")
            W("scalar", (s_pe, pe_d))
            op("scalar", lambda e, t_=tb: e.copy(da[:, t_ * 512:t_ * 512 + 512], p_sc[0:32, :]))
            se_dist = mark("scalar")
            done_bank("p_sc", ("scalar", se_dist))

        W("tensor", (s_se, se_dist))
        if sp > 0:
            W("tensor", (s_pe, prev_gates_pe))
        for r in range(NT_S // 4):
            use_bank("p_bc")
            for j in range(4):
                tt = r * 4 + j
                src = da[:, tt * 128:tt * 128 + 128]
                o = p_bc[0:128, j * 32:j * 32 + 32]
                op("tensor", lambda e, o_=o, s=src: e.transpose(o_, s, idn32f_s))
            pe_tr = mark("tensor")
            W("vector", (s_pe, pe_tr))
            op("vector", lambda e, r_=r: e.tensor_copy(dt1[:, r_ * 128:r_ * 128 + 128], p_bc[0:128, 0:128]))
            ve_d2 = mark("vector")
            done_bank("p_bc", ("vector", ve_d2))

        d3 = dt1.rearrange("p (j c) -> p j c", c=32)
        x2e = x2cb[sp % 2][:, 0:NT_S].rearrange("p (j o) -> p j o", o=1).broadcast_to([128, NT_S, 32])
        W("vector", (s_ve, ve_x2), (s_ve, ve_d2))
        op("vector", lambda e, d_=d3, x_=x2e: e.tensor_tensor(d_, d_, x_, ALU.add))
        ve_d2f = mark("vector")
        W("scalar", (s_ve, ve_d2f))
        op("scalar", lambda e: e.activation(dt2, dt1, AF.Relu))
        edge("scalar")
        op("scalar", lambda e: e.activation(dt3, dt2, AF.Sqrt))
        se_s0 = mark("scalar")
        W("vector", (s_se, se_s0), (s_ve, ve_tcol))
        op("vector", lambda e: e.reciprocal(dt1, dt3))
        edge("vector")
        op("vector", lambda e: e.tensor_tensor(dt1, dt2, dt1, ALU.mult))
        edge("vector")
        op("vector", lambda e: e.tensor_tensor(dt1, dt3, dt1, ALU.add))
        edge("vector")
        op("vector", lambda e: e.tensor_scalar_mul(Lg, dt1, tcol[:, 0:1]))
        ve_log = mark("vector")
        if sp + 1 < NSUB:
            emit_b1a_A(sp + 1, 0)
            emit_b1a_A(sp + 1, 1)

        L3 = Lg.rearrange("p (j c) -> p j c", c=32)
        m1o = m1g.rearrange("p (j o) -> p j o", o=1)
        m2o = m2g.rearrange("p (j o) -> p j o", o=1)
        dno = deng.rearrange("p (j o) -> p j o", o=1)
        m1e = m1o.broadcast_to([128, NT_S, 32])
        m2e = m2o.broadcast_to([128, NT_S, 32])
        rde = rdeng.rearrange("p (j o) -> p j o", o=1).broadcast_to([128, NT_S, 32])
        sel3 = sel1.rearrange("p (j c) -> p j c", c=32)
        l23 = l2b.rearrange("p (j c) -> p j c", c=32)
        ee3 = eex.rearrange("p (j c) -> p j c", c=32)
        eg3 = egb.rearrange("p (j c) -> p j c", c=32)
        gt3 = gtm.rearrange("p (j c) -> p j c", c=32)
        W("vector", (s_ve, ve_log))
        op("vector", lambda e: e.tensor_reduce(m1o, L3, AX.X, ALU.max))
        edge("vector")
        op("vector", lambda e: e.tensor_tensor(sel3, L3, m1e, ALU.is_ge))
        edge("vector")
        op("vector", lambda e: e.scalar_tensor_tensor(l23, sel3, -1e30, L3, ALU.mult, ALU.add))
        edge("vector")
        op("vector", lambda e: e.tensor_reduce(m2o, l23, AX.X, ALU.max))
        edge("vector")
        op("vector", lambda e: e.tensor_tensor(l23, L3, m1e, ALU.subtract))
        ve_g1 = mark("vector")
        W("scalar", (s_ve, ve_g1))
        op("scalar", lambda e: e.activation(eex, l2b, AF.Exp))
        se_ex = mark("scalar")
        W("vector", (s_se, se_ex))
        op("vector", lambda e: e.tensor_tensor(sel3, L3, m2e, ALU.is_ge))
        edge("vector")
        op("vector", lambda e: e.tensor_tensor(eg3, ee3, sel3, ALU.mult))
        edge("vector")
        op("vector", lambda e: e.tensor_reduce(dno, eg3, AX.X, ALU.add))
        edge("vector")
        op("vector", lambda e: e.reciprocal(rdeng, deng))
        edge("vector")
        op("vector", lambda e: e.tensor_tensor(gt3, eg3, rde, ALU.mult))
        ve_g = mark("vector")

        W("tensor", (s_ve, ve_g))
        use_bank("p_trb")

        def _bf16_transpose(e, o_, s):
            with nc.allow_low_precision(reason="bf16 gates transpose, values are bf16 by design"):
                return e.transpose(o_, s, idn128b_s)

        for tt in range(NT_S):
            src = gtm[:, tt * 32:tt * 32 + 32]
            o = p_trb[0:32, tt * 128:tt * 128 + 128]
            op("tensor", lambda e, o_=o, s=src: _bf16_transpose(e, o_, s))
        pe_gt = mark("tensor")
        prev_gates_pe = pe_gt
        W("vector", (s_pe, pe_gt))
        op("vector", lambda e: e.tensor_copy(gatesT[0:32, 0:SUB], p_trb[0:32, 0:SUB]))
        ve_gT = mark("vector")
        done_bank("p_trb", ("vector", ve_gT))
        if sp + 1 < NSUB:
            emit_b1a_B(sp + 1, 0)
            emit_b1a_A(sp + 1, 2)

        if ph <= 3:
            continue
        if sp == 0:
            W("tensor", (s_ve, ve_wq))
        for qk in range(2):
            Wsrc = W_QP if qk == 0 else W_KP
            bsrc = bqpr if qk == 0 else bkpr
            dstl = qp_fm if qk == 0 else kp_fm
            for jb in range(8):
                for tb in range(NTB_S):
                    rb = (jb * NTB_S + tb + qk * 8) % 2
                    bname = f"p_mm{rb}"
                    rot = p_mm[rb]
                    use_bank(bname)
                    for cb in range(8):
                        l = Wsrc[cb][:, jb * 128:jb * 128 + 128]
                        rr = hi_fm[cb][:, tb * 512:tb * 512 + 512]
                        op("tensor", lambda e, o=rot, l_=l, r_=rr, st=(cb == 0):
                           e.matmul(o, l_, r_, start=st, stop=False))
                    bsl = bsrc[0:1, jb * 128:jb * 128 + 128]
                    op("tensor", lambda e, o=rot, b_=bsl: e.matmul(o, b_, onesr512b_s, start=False, stop=True))
                    pe_q = mark("tensor")
                    W("scalar", (s_pe, pe_q))
                    d = dstl[jb][:, tb * 512:tb * 512 + 512]
                    op("scalar", lambda e, d_=d, s=rot: e.activation(d_, s, AF.Relu, scale=0.125))
                    se_qk = mark("scalar")
                    done_bank(bname, ("scalar", se_qk))

        if sp + 1 < NSUB:
            emit_b1a_B(sp + 1, 1)
            emit_b1a_A(sp + 1, 3)
        if sp > 0:
            W("tensor", (s_pe, prev_scan_pe))
        for t in range(NT_S):
            g4 = t // 4
            col = g4 * 512 + (t % 4) * 128
            fm_slices = [hi_fm[cb][:, col:col + 128] for cb in range(8)]
            emit_kv_tile(fm_slices, kt[t], vt[t])
        gp_ones_last = s_gp.n
        se_kv_last = s_se.n

        if sp + 1 < NSUB:
            emit_b1a_B(sp + 1, 2)
            emit_b1a_B(sp + 1, 3)
        if sp == 0:
            W("tensor", (s_wcC, gp_wcast2_n))
        ve_hg = 0
        for prc in range(4):
            for tb in range(NTB_S):
                rb = (prc * NTB_S + tb) % 2
                bname = f"p_mm{rb}"
                rot = p_mm[rb]
                use_bank(bname)
                for cb in range(8):
                    l = ballb[cb][:, prc * 128:prc * 128 + 128]
                    rr = hi_fm[cb][:, tb * 512:tb * 512 + 512]
                    op("tensor", lambda e, o=rot, l_=l, r_=rr, st=(cb == 0), sp_=(cb == 7):
                       e.matmul(o, l_, r_, start=st, stop=sp_))
                pe_h = mark("tensor")
                W("tensor", (s_ve, ve_gT))
                use_bank("p_bc")
                l2 = E2s[:, prc * 128:prc * 128 + 128]
                r2 = gatesT[0:32, tb * 512:tb * 512 + 512]
                op("tensor", lambda e, l_=l2, r_=r2: e.matmul(p_bc, l_, r_, start=True, stop=True))
                pe_gr = mark("tensor")
                W("vector", (s_pe, pe_gr))
                if ve_hg:
                    W("vector", (s_ve, ve_hg))
                if hps_dbg is not None and prc == 0 and tb == 0 and sp == 0:
                    op("vector", lambda e, s=rot: e.tensor_copy(Asm[1], s))
                    ve_dbg = mark("vector")
                    W("gpsimd", (s_ve, ve_dbg))
                    dma("gpsimd", s_gd, hps_dbg, Asm[1])
                op("vector", lambda e: e.tensor_copy(grb, p_bc))
                ve_grb = mark("vector")
                done_bank("p_bc", ("vector", ve_grb))
                W("vector", (s_ve, ve_grb))
                if grb_dbg is not None and prc == 0 and tb == 0 and sp == 0:
                    edge("vector")
                    op("vector", lambda e: e.tensor_copy(Asm[0], grb))
                    ve_dbg2 = mark("vector")
                    W("gpsimd", (s_ve, ve_dbg2))
                    dma("gpsimd", s_gd, grb_dbg, Asm[0])
                    W("vector", (s_gd, s_gd.n))
                d = hgt[prc][:, tb * 512:tb * 512 + 512]
                op("vector", lambda e, d_=d, s=rot: e.tensor_tensor(d_, s, grb, ALU.mult))
                ve_hg = mark("vector")
                done_bank(bname, ("vector", ve_hg))
        prev_hg_ve = ve_hg
        if sp + 1 < NSUB:
            emit_b1b_hi(sp + 1)

        if ph <= 4:
            continue
        seed_src = S_seed if sp == 0 else Cend
        if sp == 0:
            W("tensor", (s_ve, ve_seed))
        else:
            W("tensor", (s_se, se_scan_prev))
        ve_asm = 0
        se_snap = 0
        pe_ctx_group = [0, 0]
        ve_ctx_evac = [0, 0]
        se_stage = [0, 0]
        pe_head_last = [0, 0]
        se_head_last = [0, 0]
        W("vector", (s_se, s_se.n))
        for h in range(H):
            hb = h // 2
            hp = (h % 2) * 64
            hr = h % 2
            ssrc = seed_src[hp:hp + 64, (h // 2) * 65:(h // 2) * 65 + 65]
            if pe_head_last[hr]:
                W("vector", (s_pe, pe_head_last[hr]))
            if sp > 0:
                W("vector", (s_se, se_scan_prev))
            if gp_cend[hr]:
                W("vector", (s_gp, gp_cend[hr]))
            op("vector", lambda e, s=qp_fm[hb][hp:hp + 64, :], d=qkst[hr][0:64, 0:SUB]:
               e.tensor_copy(d, s))
            op("vector", lambda e, s=kp_fm[hb][hp:hp + 64, :], d=qkst[hr][0:64, SUB:2 * SUB]:
               e.tensor_copy(d, s))
            op("vector", lambda e, s=ssrc, c0=hr * 130: e.tensor_copy(Csnap[0:64, c0:c0 + 65], s))
            ve_stage0 = mark("vector")
            qsl = qkst[hr][0:64, 0:SUB]
            ksl = qkst[hr][0:64, SUB:2 * SUB]
            use_bank("p_st")
            W("tensor", (s_ve, ve_stage0))
            for i in range(NCH_S):
                l6 = kt[i][:, h * 64:h * 64 + 64]
                r6 = vt[i].rearrange("p (hh e) -> p hh e", e=65)[:, h, :]
                op("tensor", lambda e, o_=p_st[0:64, i * 65:(i + 1) * 65], l_=l6, r_=r6:
                   e.matmul(o_, l_, r_, start=True, stop=True))
            pe_S4 = mark("tensor")
            W("scalar", (s_pe, pe_S4))
            if gp_cend[hr]:
                W("scalar", (s_gp, gp_cend[hr]))
            op("scalar", lambda e, d=Ssb[hr][0:64, 0:260]: e.copy(d, p_st[0:64, 0:260]))
            se_Ssb = mark("scalar")
            done_bank("p_st", ("scalar", se_Ssb))
            W("gpsimd", (s_se, se_Ssb), (s_gp, max(gp_cend)))
            op("gpsimd", lambda e, s=Csnap[0:64, hr * 130:hr * 130 + 65]:
               e.tensor_copy(cendt[0:64, 0:65], s))
            edge("gpsimd")
            for i in range(NCH_S):
                op("gpsimd", lambda e, s=Ssb[hr][0:64, i * 65:(i + 1) * 65]:
                   e.tensor_tensor(cendt[0:64, 0:65], cendt[0:64, 0:65], s, ALU.add))
                edge("gpsimd")
            dcend = Cend[hp:hp + 64, (h // 2) * 65:(h // 2) * 65 + 65]
            op("gpsimd", lambda e, d=dcend: e.tensor_copy(d, cendt[0:64, 0:65]))
            gp_cend[hr] = mark("gpsimd")
            for tb in range(NTB_S):
                use_bank("p_sc")
                for j in range(4):
                    ccol = tb * 512 + j * 128
                    l = ksl[:, ccol:ccol + 128]
                    rr = qsl[:, ccol:ccol + 128]
                    o = p_sc[:, j * 128:j * 128 + 128]
                    op("tensor", lambda e, o_=o, l_=l, r_=rr:
                       e.matmul(o_, l_, r_, start=True, stop=True))
                pe_sc = mark("tensor")
                W("vector", (s_pe, pe_sc))
                ar = Asm[tb]
                op("vector", lambda e, a=ar: e.tensor_tensor(a, p_sc, tri_s, ALU.mult))
                ve_asm = mark("vector")
                done_bank("p_sc", ("vector", ve_asm))
                cname = f"p_ctx{tb}"
                use_bank(cname)
                pc = p_ctx[tb]
                snp = Csnap[0:64, hr * 130:hr * 130 + 65]
                op("tensor", lambda e, l_=snp, r_=qsl[:, tb * 512:tb * 512 + 512]:
                   e.matmul(pc[0:65, 0:512], l_, r_, start=True, stop=False))
                W("tensor", (s_se, se_Ssb))
                for i in range(3):
                    sl = Ssb[hr][0:64, i * 65:(i + 1) * 65]
                    rtail = qsl[:, tb * 512 + (i + 1) * 128: tb * 512 + 512]
                    otail = pc[0:65, (i + 1) * 128:512]
                    op("tensor", lambda e, o_=otail, l_=sl, r_=rtail:
                       e.matmul(o_, l_, r_, start=False, stop=False))
                W("tensor", (s_ve, ve_asm))
                for j in range(4):
                    jj = tb * 4 + j
                    l4 = vt[jj].rearrange("p (hh e) -> p hh e", e=65)[:, h, :]
                    r4 = ar[:, j * 128:j * 128 + 128]
                    o4 = pc[0:65, j * 128:j * 128 + 128]
                    op("tensor", lambda e, o_=o4, l_=l4, r_=r4, sp_=(j == 3):
                       e.matmul(o_, l_, r_, start=False, stop=sp_))
                pe_ctx_mms = mark("tensor")
                pe_ctx_group[tb] = pe_ctx_mms
                W("scalar", (s_gd, s_gd.n), (s_pe, pe_ctx_group[tb]))
                dctx = ctxu[hb][hp:hp + 64, tb * 512:tb * 512 + 512]
                op("scalar", lambda e, d=dctx, t_=tb: e.copy(d, p_ctx[t_][0:64, :]))
                op("scalar", lambda e, t_=tb, r=tb % 2: e.copy(stage[r], p_ctx[t_][64:65, :]))
                se_stage[tb] = mark("scalar")
                done_bank(cname, ("scalar", se_stage[tb]))
                W("sync", (s_se, se_stage[tb]))
                dn = nall[h:h + 1, tb * 512:tb * 512 + 512]
                prev_stage_gd[tb] = dma("sync", s_gd, dn, stage[tb % 2])
            pe_head_last[hr] = s_pe.n
            se_head_last[hr] = s_se.n
        prev_scan_pe = s_pe.n
        se_scan_prev = s_se.n

        W("vector", (s_gd, prev_stage_gd[0]), (s_gd, prev_stage_gd[1]))

        def _lp(e, fn):
            with nc.allow_low_precision(reason="bf16 norm reciprocal, consistent with bf16 matmul path"):
                return fn(e)

        op("vector", lambda e: _lp(e, lambda e2: e2.tensor_scalar_add(nall, nall, 1e-6)))
        edge("vector")
        op("vector", lambda e: _lp(e, lambda e2: e2.reciprocal(rnb, nall)))
        ve_rn = mark("vector")
        ve_ctxn = 0
        for cb in range(8):
            for tb in range(NTB_S):
                W("tensor", (s_ve, ve_rn))
                use_bank("p_bc")
                l = E3s[:, cb * 128:cb * 128 + 128]
                rr = rnb[0:16, tb * 512:tb * 512 + 512]
                op("tensor", lambda e, l_=l, r_=rr: e.matmul(p_bc, l_, r_, start=True, stop=True))
                pe_rn = mark("tensor")
                W("vector", (s_pe, pe_rn))
                d = ctxn[cb][:, tb * 512:tb * 512 + 512]
                s2 = ctxu[cb][:, tb * 512:tb * 512 + 512]
                op("vector", lambda e, d_=d, s_=s2: e.tensor_tensor(d_, s_, p_bc, ALU.mult))
                ve_ctxn = mark("vector")
                done_bank("p_bc", ("vector", ve_ctxn))
        prev_ctxn_ve = ve_ctxn
        if sp + 1 < NSUB:
            v_fm_next = emit_b1b_lo(sp + 1, prev_ctxn_ve)

        if ph <= 5:
            continue
        W("tensor", (s_ve, prev_hg_ve), (s_ve, ve_gT))
        for t in range(NT_S):
            rot = t % 2
            xrot = t % 4
            if prev_out_ve[rot]:
                W("sync", (s_ve, prev_out_ve[rot]))
            if xt_war[xrot]:
                W("sync", (s_ve, xt_war[xrot]))
            vx = dma("sync", s_xb[4 + xrot], xt[xrot][:, :], xm[base + t * 128: base + (t + 1) * 128, :])
            for jh in range(2):
                bname = f"p_mm{jh}"
                pm = p_mm[jh]
                use_bank(bname)
                for prc in range(4):
                    l = hgt[prc][:, t * 128:t * 128 + 128]
                    rr = aflb[prc][:, jh * 512:jh * 512 + 512]
                    op("tensor", lambda e, o=pm, l_=l, r_=rr, st=(prc == 0):
                       e.matmul(o, l_, r_, start=st, stop=False))
                lgb = gatesT[0:32, t * 128:t * 128 + 128]
                rgb = fbiasb[0:32, jh * 512:jh * 512 + 512]
                op("tensor", lambda e, o=pm, l_=lgb, r_=rgb: e.matmul(o, l_, r_, start=False, stop=False))
                rb2 = boutr[0:1, jh * 512:jh * 512 + 512]
                op("tensor", lambda e, o=pm, b_=rb2: e.matmul(o, onesc128b_s, b_, start=False, stop=False))
                W("tensor", (s_ve, ve_ctxn))
                for cb in range(8):
                    l = ctxn[cb][:, t * 128:t * 128 + 128]
                    rr = woutb[cb][:, jh * 512:jh * 512 + 512]
                    op("tensor", lambda e, o=pm, l_=l, r_=rr, sp_=(cb == 7):
                       e.matmul(o, l_, r_, start=False, stop=sp_))
                pe_y = mark("tensor")
                W("vector", (s_pe, pe_y), (s_xb[4 + xrot], vx), (s_out, s_out.n))
                d = outb[rot][:, jh * 512:jh * 512 + 512]
                xs = xt[xrot][:, jh * 512:jh * 512 + 512]
                op("vector", lambda e, d_=d, s_=xs, p=pm: e.tensor_tensor(d_, s_, p, ALU.add))
                ve_final = mark("vector")
                done_bank(bname, ("vector", ve_final))
            xt_war[xrot] = ve_final
            prev_y_pe = s_pe.n
            W("sync", (s_ve, ve_final))
            prev_store[rot] = dma("sync", s_out, y_out[base + t * 128: base + (t + 1) * 128, :], outb[rot])
            prev_out_ve[rot] = ve_final

    W("sync", (s_out, s_out.n))

    dbg_outs = {}
    if debug:
        dbg_srcs = {
            "d_hi_fm0": (hi_fm[0], [128, SUB], BF16),
            "d_qp_fm0": (qp_fm[0], [128, SUB], BF16),
            "d_kt0": (kt[0], [128, C], BF16),
            "d_vt0": (vt[0], [128, 1040], BF16),
            "d_gatesT": (gatesT, [32, SUB], BF16),
            "d_Lg": (Lg, [128, NG], F32),
            "d_Sseed": (S_seed, [128, 520], BF16),
            "d_nall": (nall, [16, SUB], BF16),
            "d_ctxu0": (ctxu[0], [128, SUB], BF16),
            "d_hgt0": (hgt[0], [128, SUB], BF16),
            "d_x2c": (x2c, [128, NT_S], F32),
            "d_Cend": (Cend, [128, 520], BF16),
        }
        for name in debug:
            if name not in dbg_srcs:
                continue
            src, shape, dt = dbg_srcs[name]
            dv = nc.dram_tensor(name, shape, dt, kind="ExternalOutput").ap()
            dma("sync", s_out, dv, src)
        W("sync", (s_out, s_out.n))

    tmax = trunc if trunc is not None else seqc[0] + 1

    with nc.Block() as blk:
        @blk.sync
        def _(eng):
            for sq, f in ops["sync"]:
                if sq <= tmax:
                    f(eng)

        @blk.tensor
        def _(eng):
            for sq, f in ops["tensor"]:
                if sq <= tmax:
                    f(eng)

        @blk.vector
        def _(eng):
            for sq, f in ops["vector"]:
                if sq <= tmax:
                    f(eng)

        @blk.scalar
        def _(eng):
            for sq, f in ops["scalar"]:
                if sq <= tmax:
                    f(eng)

        @blk.gpsimd
        def _(eng):
            for sq, f in ops["gpsimd"]:
                if sq <= tmax:
                    f(eng)

    return nc


def _host_inputs(inputs, TM=2048, TC=2048):
    x = np.ascontiguousarray(np.asarray(inputs["L0_attn_x"], dtype=np.float32))
    w_qkv = np.asarray(inputs["L0_attn_w_qkv"], dtype=np.float32)
    b_qkv = np.asarray(inputs["L0_attn_b_qkv"], dtype=np.float32)
    w_out = np.asarray(inputs["L0_attn_w_out"], dtype=np.float32)
    b_out = np.asarray(inputs["L0_attn_b_out"], dtype=np.float32)
    proj = np.asarray(inputs["L0_attn_proj"], dtype=np.float32)
    protos = np.asarray(inputs["L0_ffn_prototypes"], dtype=np.float32)
    Bm = np.asarray(inputs["L0_ffn_B"], dtype=np.float32)
    Am = np.asarray(inputs["L0_ffn_A"], dtype=np.float32)
    bias = np.asarray(inputs["L0_ffn_bias"], dtype=np.float32)
    tempv = np.asarray(inputs["L0_ffn_temp"], dtype=np.float32).reshape(1, 1)

    common = {}
    common["wqT"] = np.ascontiguousarray(w_qkv[:, 0:C].T)
    common["wkT"] = np.ascontiguousarray(w_qkv[:, C:2 * C].T)
    common["wv"] = np.ascontiguousarray(w_qkv[:, 2 * C:3 * C])
    projP = np.zeros((128, 1024), np.float32)
    bqc = np.zeros((128, 8), np.float32)
    bkc = np.zeros((128, 8), np.float32)
    for g in range(8):
        projP[0:64, g * 128:g * 128 + 64] = proj[2 * g]
        projP[64:128, g * 128 + 64:g * 128 + 128] = proj[2 * g + 1]
        bqc[0:64, g] = b_qkv[2 * g * 64:(2 * g + 1) * 64]
        bqc[64:128, g] = b_qkv[(2 * g + 1) * 64:(2 * g + 2) * 64]
        bkc[0:64, g] = b_qkv[C + 2 * g * 64: C + (2 * g + 1) * 64]
        bkc[64:128, g] = b_qkv[C + (2 * g + 1) * 64: C + (2 * g + 2) * 64]
    common["projP"] = projP
    common["bqc"] = bqc
    common["bkc"] = bkc
    common["bv"] = np.ascontiguousarray(b_qkv[2 * C:3 * C].reshape(1, C))
    common["wout"] = w_out
    common["bout"] = b_out.reshape(1, C)
    common["protosP"] = np.ascontiguousarray(
        protos.T.reshape(8, 128, 32).transpose(1, 0, 2).reshape(128, 256))
    common["ball"] = np.ascontiguousarray(Bm.transpose(2, 0, 1).reshape(C, 32 * 16))
    common["afl"] = np.ascontiguousarray(Am.transpose(0, 2, 1).reshape(32 * 16, C))
    common["fbias"] = bias
    common["temp"] = tempv
    common["tri"] = np.ascontiguousarray(np.tile(np.triu(np.ones((128, 128), np.float32)), (1, 4))).astype(BFNP)
    common["idn64b"] = np.eye(64, dtype=BFNP)
    common["idn128b"] = np.eye(128, dtype=BFNP)
    common["idn32f"] = np.eye(32, dtype=np.float32)
    common["ones_r512f"] = np.ones((1, 512), np.float32)
    common["ones_r512b"] = np.ones((1, 512), BFNP)
    common["ones_c128b"] = np.ones((1, 128), BFNP)
    common["ones_r128f"] = np.ones((1, 128), np.float32)
    common["ones_col128f"] = np.ones((128, 1), np.float32)
    E2 = np.zeros((32, 512), BFNP)
    for k in range(32):
        E2[k, k * 16:(k + 1) * 16] = 1
    common["E2c"] = E2
    E3 = np.zeros((16, C), BFNP)
    for r in range(8):
        E3[2 * r, r * 128:r * 128 + 64] = 1
        E3[2 * r + 1, r * 128 + 64:r * 128 + 128] = 1
    common["E3c"] = E3

    in_maps = []
    for core in range(NCORE):
        b = core // 2
        half = core % 2
        m = dict(common)
        m["xm"] = np.ascontiguousarray(x[b, half * TM:(half + 1) * TM, :])
        m["xc"] = np.ascontiguousarray(x[b, 0:TC, :])
        f = np.zeros((128, 1), np.float32)
        f[:, 0] = float(half)
        m["flag"] = f
        in_maps.append(m)
    return in_maps


def kernel(**inputs):
    TM, TC, SUB = 2048, 2048, 512
    nc = build(TM=TM, TC=TC, SUB=SUB)
    in_maps = _host_inputs(inputs, TM, TC)
    res = run_bass_kernel_spmd(nc, in_maps, core_ids=list(range(NCORE)))
    out = np.zeros((B, T, C), np.float32)
    for core in range(NCORE):
        b = core // 2
        half = core % 2
        out[b, half * TM:(half + 1) * TM, :] = res.results[core]["y"]
    return out


# version 31
# speedup vs baseline: 1.0088x; 1.0088x over previous
import numpy as np
import concourse.bass as bass
import concourse.mybir as mybir
from concourse.bass_utils import run_bass_kernel_spmd
import ml_dtypes

BFNP = ml_dtypes.bfloat16
F32 = mybir.dt.float32
BF16 = mybir.dt.bfloat16
AF = mybir.ActivationFunctionType
ALU = mybir.AluOpType
AX = mybir.AxisListType

B, T, C, H, D = 4, 4096, 1024, 16, 64
NCORE = 8


def build(TM=2048, TC=2048, SUB=512, debug=(), ph=99, trunc=None):
    NSUB = TM // SUB
    NT_S = SUB // 128
    NTB_S = SUB // 512
    NCH_S = NT_S
    NT_C = TC // 128
    NG = 32 * NT_S
    nc = bass.Bass("TRN2", target_bir_lowering=False)

    def din(name, shape, dt=F32):
        return nc.dram_tensor(name, shape, dt, kind="ExternalInput").ap()

    xm = din("xm", [TM, C])
    xc = din("xc", [TC, C])
    flag = din("flag", [128, 1])
    wqT = din("wqT", [C, C]); wkT = din("wkT", [C, C])
    wv = din("wv", [C, C])
    projP = din("projP", [128, 1024])
    bqc = din("bqc", [128, 8]); bkc = din("bkc", [128, 8])
    bv = din("bv", [1, C])
    wout = din("wout", [C, C]); bout = din("bout", [1, C])
    protosP = din("protosP", [128, 256])
    ball = din("ball", [C, 512])
    afl = din("afl", [512, C])
    fbias = din("fbias", [32, C])
    temp = din("temp", [1, 1])
    tri = din("tri", [128, 512], BF16)
    idn64b = din("idn64b", [64, 64], BF16)
    idn128b = din("idn128b", [128, 128], BF16)
    idn32f = din("idn32f", [32, 32])
    ones_r512f = din("ones_r512f", [1, 512])
    ones_r512b = din("ones_r512b", [1, 512], BF16)
    ones_c128b = din("ones_c128b", [1, 128], BF16)
    ones_r128f = din("ones_r128f", [1, 128])
    ones_col128f = din("ones_col128f", [128, 1])
    E2c = din("E2c", [32, 512], BF16)
    E3c = din("E3c", [16, C], BF16)

    y_out = nc.dram_tensor("y", [TM, C], F32, kind="ExternalOutput").ap()

    scr_hi = nc.dram_tensor("scr_hi", [TM, C], BF16, kind="Internal").ap()
    scr_lo = nc.dram_tensor("scr_lo", [TM, C], BF16, kind="Internal").ap()
    scr_c = nc.dram_tensor("scr_c", [TC, C], BF16, kind="Internal").ap()

    def sb(name, shape, dt):
        return nc.alloc_sbuf_tensor(name, shape, dt).ap()

    xt = [sb(f"xt{i}", [128, C], F32) for i in range(4)]
    xn32 = [sb(f"xn32_{i}", [128, C], F32) for i in range(2)]
    hitm = [sb(f"hitm{i}", [128, C], BF16) for i in range(2)]
    lotm = [sb(f"lotm{i}", [128, C], BF16) for i in range(2)]
    stat6 = [sb(f"stat6_{i}", [128, 12], F32) for i in range(4)]
    mv = [sb(f"mv{i}", [128, 2], F32) for i in range(4)]
    msr = [sb(f"msr{i}", [128, 2], F32) for i in range(4)]
    ln1 = [sb(f"ln1_{i}", [128, 1], F32) for i in range(4)]
    x2cb = [sb(f"x2c{i}", [128, NT_S], F32) for i in range(2)]

    hi_fm = [sb(f"hi_fm{cb}", [128, SUB], BF16) for cb in range(8)]
    lo_fm = [sb(f"lo_fm{cb}", [128, SUB], BF16) for cb in range(8)]
    ctxu = lo_fm
    qp_fm = [sb(f"qp_fm{cb}", [128, SUB], BF16) for cb in range(8)]
    kp_fm = [sb(f"kp_fm{cb}", [128, SUB], BF16) for cb in range(8)]

    kt = [sb(f"kt{i}", [128, C], BF16) for i in range(NT_S)]
    ctxn = [kt[cb // 2][:, (cb % 2) * 512:(cb % 2) * 512 + 512] for cb in range(8)]
    vt = [sb(f"vt{i}", [128, 1040], BF16) for i in range(NT_S)]
    hgt = [sb(f"hgt{p}", [128, SUB], BF16) for p in range(4)]
    grb = sb("grb", [128, 512], BF16)

    W_QP = [sb(f"W_QP{cb}", [128, C], BF16) for cb in range(8)]
    W_KP = [sb(f"W_KP{cb}", [128, C], BF16) for cb in range(8)]
    wvb = [sb(f"wvb{cb}", [128, C], BF16) for cb in range(8)]
    woutb = [sb(f"woutb{cb}", [128, C], BF16) for cb in range(8)]
    ballb = [sb(f"ballb{cb}", [128, 512], BF16) for cb in range(8)]
    aflb = [sb(f"aflb{p}", [128, C], BF16) for p in range(4)]
    fbiasb = sb("fbiasb", [32, C], BF16)
    boutr = sb("boutr", [1, C], BF16)
    bvr = sb("bvr", [1, C], BF16)
    bqpr = sb("bqpr", [1, C], BF16)
    bkpr = sb("bkpr", [1, C], BF16)
    projs = sb("projs", [128, 1024], BF16)
    bqcb = sb("bqcb", [128, 8], BF16)
    bkcb = sb("bkcb", [128, 8], BF16)
    qkst = [sb(f"qkst{i}", [64, 1024], BF16) for i in range(2)]

    phTa = sb("phTa", [128, 256], BF16)
    plTa = sb("plTa", [128, 256], BF16)
    p2r = sb("p2r", [1, 32], F32)
    p2h = sb("p2h", [1, 32], BF16)
    p2l = sb("p2l", [1, 32], BF16)
    tprep = sb("tprep", [1, 4], F32)
    tcol = sb("tcol", [128, 1], F32)
    flagc = sb("flagc", [128, 1], F32)
    S_acc = sb("S_acc", [128, 520], F32)
    S_seed = sb("S_seed", [128, 520], BF16)
    Cend = sb("Cend", [128, 520], BF16)
    Csnap = sb("Csnap", [64, 260], BF16)
    Ssb = [sb(f"Ssb{i}", [64, 260], BF16) for i in range(2)]
    cendt = sb("cendt", [64, 65], F32)

    da = sb("da", [32, SUB], F32)
    dt1 = sb("dt1", [128, NG], F32)

    Lg = sb("Lg", [128, NG], F32)
    m1g = sb("m1g", [128, NT_S], F32)
    m2g = sb("m2g", [128, NT_S], F32)
    sel1 = sb("sel1", [128, NG], F32)
    l2b = sb("l2b", [128, NG], F32)
    dt2 = sel1
    dt3 = l2b
    eex = sb("eex", [128, NG], F32)
    egb = eex
    deng = sb("deng", [128, NT_S], F32)
    rdeng = sb("rdeng", [128, NT_S], F32)
    gtm = sb("gtm", [128, NG], BF16)
    gatesT = sb("gatesT", [32, SUB], BF16)

    Asm = [sb(f"Asm{i}", [128, 512], BF16) for i in range(NTB_S)]
    hps_dbg = nc.dram_tensor("d_hps", [128, 512], BF16, kind="ExternalOutput").ap() if "d_hps" in debug else None
    grb_dbg = nc.dram_tensor("d_grb", [128, 512], BF16, kind="ExternalOutput").ap() if "d_hps" in debug else None
    nall = sb("nall", [16, SUB], BF16)
    rnb = sb("rnb", [16, SUB], BF16)
    stage = [sb(f"stage{i}", [1, 512], BF16) for i in range(2)]
    outb = xn32

    tri_s = sb("tri_s", [128, 512], BF16)
    idn128b_s = sb("idn128b_s", [128, 128], BF16)
    idn32f_s = sb("idn32f_s", [32, 32], F32)
    onesr512b_s = sb("onesr512b_s", [1, 512], BF16)
    onesc128b_s = sb("onesc128b_s", [1, 128], BF16)
    onesr128f_s = sb("onesr128f_s", [1, 128], F32)
    onescol128f_s = sb("onescol128f_s", [128, 1], F32)
    E2s = sb("E2s", [32, 512], BF16)
    E3s = sb("E3s", [16, C], BF16)
    tempf = sb("tempf", [1, 1], F32)

    ktA = [kt[0], kt[1]]
    vtA = [vt[0], vt[1]]

    def psn(name, shape, dt=F32):
        return nc.alloc_psum_tensor(name, shape, dt).ap()

    p_mm = [psn("p_mm0", [128, 512]), psn("p_mm1", [128, 512])]
    p_sc = psn("p_sc", [128, 512])
    p_ctx = [psn("p_ctx0", [128, 512]), psn("p_ctx1", [128, 512])]
    p_st = psn("p_st", [128, 512])
    p_bc = psn("p_bc", [128, 512])
    p_trb = psn("p_trb", [128, 1024], BF16)

    ops = {k: [] for k in ("sync", "tensor", "vector", "scalar", "gpsimd")}
    seqc = [0]

    def em(eng, fn):
        seqc[0] += 1
        ops[eng].append((seqc[0], fn))

    class Sem:
        def __init__(self, name):
            self.h = nc.alloc_semaphore(name)
            self.n = 0

        def inc(self, k=1):
            self.n += k
            return self.n

    s_w0 = Sem("s_w0")
    s_wcA = Sem("s_wcA")
    s_wcB = Sem("s_wcB")
    s_wcC = Sem("s_wcC")
    s_wr = [Sem("s_wrA"), Sem("s_wrB")]
    s_xb = [Sem(f"s_x{i}") for i in range(8)]
    s_scr = Sem("s_scr")
    s_scrA = Sem("s_scrA")
    s_fmA = [Sem("s_fmA0"), Sem("s_fmA1")]
    s_fm = Sem("s_fm")
    s_pe = Sem("s_pe")
    s_ve = Sem("s_ve")
    s_se = Sem("s_se")
    s_gp = Sem("s_gp")
    s_gd = Sem("s_gd")
    s_out = Sem("s_out")

    ENG_SEM = {"tensor": s_pe, "vector": s_ve, "scalar": s_se, "gpsimd": s_gp}

    def W(eng, *deps):
        for sem, val in deps:
            if val and val > 0:
                em(eng, lambda e, s=sem.h, v=val: e.wait_ge(s, v))

    def dma(eng_name, sem, out, in_, transpose=False):
        v = sem.inc(16)
        em(eng_name, lambda e, o=out, i=in_, s=sem.h, tr=transpose:
           e.dma_start(out=o, in_=i, transpose=tr).then_inc(s, 16))
        return v

    def mark(eng):
        sem = ENG_SEM[eng]
        v = sem.inc(1)
        _, prev = ops[eng].pop()
        em(eng, lambda e, p=prev, s=sem.h: p(e).then_inc(s, 1))
        return v

    def op(eng, fn):
        em(eng, fn)

    def edge(eng):
        sem = ENG_SEM[eng]
        v = mark(eng)
        W(eng, (sem, v))
        return v

    bank_war = {}

    def use_bank(name):
        for sem, v in bank_war.get(name, ()):
            W("tensor", (sem, v))

    def done_bank(name, *marks):
        bank_war[name] = [(ENG_SEM[eng], v) for eng, v in marks]

    cpool = sb("cpool", [128, 2], F32)
    nc.const_aps.register(F32, 1e-5, cpool[:, 0:1])

    for dst, src in [(tri_s, tri), (idn128b_s, idn128b),
                     (idn32f_s, idn32f),
                     (onesr512b_s, ones_r512b), (onesc128b_s, ones_c128b),
                     (onesr128f_s, ones_r128f),
                     (onescol128f_s, ones_col128f),
                     (E2s, E2c), (E3s, E3c), (tempf, temp),
                     (flagc, flag)]:
        dma("sync", s_w0, dst, src)
    ld_small = s_w0.n

    op("vector", lambda e: e.memset(cpool[:, 0:1], 1e-5))
    ve_cpool = mark("vector")

    W("scalar", (s_w0, ld_small), (s_ve, ve_cpool))
    op("scalar", lambda e: e.activation(tprep[0:1, 0:1], tempf, AF.Abs))
    se_abs = mark("scalar")
    W("vector", (s_se, se_abs))
    op("vector", lambda e: e.tensor_scalar_max(tprep[0:1, 1:2], tprep[0:1, 0:1], 0.1))
    edge("vector")
    op("vector", lambda e: e.reciprocal(tprep[0:1, 2:3], tprep[0:1, 1:2]))
    edge("vector")
    op("vector", lambda e: e.tensor_scalar_mul(tprep[0:1, 3:4], tprep[0:1, 2:3], -0.5))
    ve_t = mark("vector")
    W("tensor", (s_ve, ve_t))
    use_bank("p_bc")
    op("tensor", lambda e: e.matmul(p_bc[0:128, 0:1], onesr128f_s, tprep[0:1, 3:4], start=True, stop=True))
    pe_t = mark("tensor")
    W("vector", (s_pe, pe_t))
    op("vector", lambda e: e.tensor_copy(tcol, p_bc[0:128, 0:1]))
    ve_tcol = mark("vector")
    done_bank("p_bc", ("vector", ve_tcol))

    op("gpsimd", lambda e: e.engine_nop())
    gp_smallcast = mark("gpsimd")

    def cast_loads(pairs, sem):
        for dst_ap, src_sl in pairs:
            dma("gpsimd", sem, dst_ap, src_sl)
        return sem.n

    pairs = [(projs, projP), (bqcb, bqc), (bkcb, bkc), (bvr, bv)]
    for cb in range(8):
        pairs.append((wvb[cb][:, :], wv[cb * 128:(cb + 1) * 128, :]))
    gp_wcast_n = cast_loads(pairs, s_wcA)

    pe_wdone = [0]

    def emit_wfusion(qk):
        Wdst = W_QP if qk == 0 else W_KP
        wsrc = wqT if qk == 0 else wkT
        if pe_wdone[0]:
            W("gpsimd", (s_pe, pe_wdone[0]))
        stg = []
        for cb in range(8):
            stg.append((qp_fm[cb][:, 0:512], wsrc[cb * 128:(cb + 1) * 128, 0:512]))
            stg.append((kp_fm[cb][:, 0:512], wsrc[cb * 128:(cb + 1) * 128, 512:1024]))
        v_stg = cast_loads(stg, s_wcB)
        W("tensor", (s_wcB, v_stg))
        for cbc in range(8):
            buf = qp_fm if cbc < 4 else kp_fm
            for gq in range(2):
                rb = (cbc * 2 + gq) % 2
                bname = f"p_mm{rb}"
                use_bank(bname)
                for gi in range(4):
                    g = gq * 4 + gi
                    lhsT = buf[g][:, (cbc % 4) * 128:(cbc % 4) * 128 + 128]
                    rhs = projs[:, g * 128:(g + 1) * 128]
                    op("tensor", lambda e, o=p_mm[rb][:, gi * 128:gi * 128 + 128], l=lhsT, rr=rhs:
                       e.matmul(o, l, rr, start=True, stop=True))
                pe_w = mark("tensor")
                W("vector", (s_pe, pe_w))
                op("vector", lambda e, d=Wdst[cbc][:, gq * 512:gq * 512 + 512], s=p_mm[rb][:, 0:512]:
                   e.tensor_copy(d, s))
                ve_w = mark("vector")
                done_bank(bname, ("vector", ve_w))
        pe_wdone[0] = pe_w
        bcb = bqcb if qk == 0 else bkcb
        dstr = bqpr if qk == 0 else bkpr
        ve_b = 0
        for half in range(2):
            use_bank("p_mm1")
            for gg in range(4):
                g = half * 4 + gg
                lhsT = bcb[:, g:g + 1]
                rhs = projs[:, g * 128:(g + 1) * 128]
                o_ap = p_mm[1][0:1, gg * 128:gg * 128 + 128]
                op("tensor", lambda e, o=o_ap, l=lhsT, rr=rhs:
                   e.matmul(o, l, rr, start=True, stop=True))
            pe_b = mark("tensor")
            W("vector", (s_pe, pe_b))
            op("vector", lambda e, d=dstr[0:1, half * 512:half * 512 + 512]:
               e.tensor_copy(d, p_mm[1][0:1, 0:512]))
            ve_b = mark("vector")
            done_bank("p_mm1", ("vector", ve_b))
        return ve_b

    psall = grb.bitcast(F32)[:, 0:256]
    sqall = Asm[0].bitcast(F32)[:, 0:256]
    use_bank("p_bc")
    v = dma("sync", s_wr[0], psall, protosP)
    W("vector", (s_wr[0], v))
    op("vector", lambda e: e.tensor_scalar_mul(psall, psall, -2.0))
    edge("vector")
    op("vector", lambda e: e.tensor_copy(phTa, psall))
    edge("vector")
    op("vector", lambda e: e.tensor_tensor(plTa, psall, phTa, ALU.subtract))
    ve_lo2 = mark("vector")
    W("scalar", (s_ve, ve_lo2))
    op("scalar", lambda e: e.activation(sqall, psall, AF.Square, scale=-0.5))
    se_sq = mark("scalar")
    W("tensor", (s_se, se_sq))
    op("tensor", lambda e: e.matmul(p_bc[0:1, 0:256], onescol128f_s, sqall, start=True, stop=True))
    pe_p2 = mark("tensor")
    gp_protos = s_gp.n
    W("vector", (s_pe, pe_p2))
    op("vector", lambda e: e.tensor_reduce(p2r.rearrange("o (c u) -> o c u", u=1),
                                           p_bc[0:1, 0:256].rearrange("o (cb c) -> o c cb", c=32),
                                           AX.X, ALU.add))
    ve_p2a = mark("vector")
    W("gpsimd", (s_ve, ve_p2a))
    op("gpsimd", lambda e: e.tensor_copy(p2h, p2r))
    gp_p2h = mark("gpsimd")
    W("vector", (s_gp, gp_p2h))
    op("vector", lambda e: e.tensor_tensor(p2l, p2r, p2h, ALU.subtract))
    ve_p2 = mark("vector")
    done_bank("p_bc", ("vector", ve_p2))

    W("tensor", (s_wcA, gp_wcast_n), (s_gp, gp_smallcast))
    ve_bias_rows = emit_wfusion(1)



    scratch_war = [0, 0]
    xt_war = [0, 0, 0, 0]
    ln_pend = {}

    ln_vx = [0, 0, 0, 0]

    def emit_ln_load(x_src_rows, rot):
        if xt_war[rot]:
            W("sync", (s_ve, xt_war[rot]))
        ln_vx[rot] = dma("sync", s_xb[rot], xt[rot][:, :], x_src_rows)

    def emit_ln_stats(rot, want_x2=None, pre_wait=None):
        W("vector", (s_xb[rot], ln_vx[rot]))
        op("vector", lambda e, r=rot: e.bn_stats(stat6[r][:, 0:6], xt[r][:, 0:512]))
        op("vector", lambda e, r=rot: e.bn_stats(stat6[r][:, 6:12], xt[r][:, 512:1024]))
        edge("vector")
        op("vector", lambda e, r=rot: e.bn_aggr(mv[r], stat6[r]))
        ve_mv = mark("vector")
        W("scalar", (s_ve, ve_mv))
        op("scalar", lambda e, r=rot: e.activation(ln1[r], mv[r][:, 1:2], AF.Ln, bias=1e-5))
        edge("scalar")
        op("scalar", lambda e, r=rot: e.activation(msr[r][:, 1:2], ln1[r], AF.Exp, scale=-0.5))
        edge("scalar")
        op("scalar", lambda e, r=rot: e.copy(msr[r][:, 0:1], mv[r][:, 0:1]))
        se_msr = mark("scalar")
        if want_x2 is not None:
            op("scalar", lambda e, r=rot: e.activation(ln1[r], msr[r][:, 1:2], AF.Square))
            se_sq2 = mark("scalar")
        else:
            se_sq2 = se_msr
        ln_pend[rot] = (se_msr, se_sq2, want_x2, pre_wait)

    def emit_ln_A(x_src_rows, rot, want_x2=None, pre_wait=None):
        emit_ln_load(x_src_rows, rot)
        emit_ln_stats(rot, want_x2=want_x2, pre_wait=pre_wait)

    ln_bseq = [0]

    def emit_ln_B(rot, want_lo=True):
        se_msr, se_sq2, want_x2, pre_wait = ln_pend.pop(rot)
        brot = ln_bseq[0] % 2
        ln_bseq[0] += 1
        if pre_wait is not None:
            W("vector", *pre_wait)
        W("vector", (s_se, se_sq2))
        if want_x2 is not None:
            op("vector", lambda e, r=rot, d=want_x2: e.tensor_tensor(d, mv[r][:, 1:2], ln1[r], ALU.mult))
            edge("vector")
            op("vector", lambda e, d=want_x2: e.tensor_scalar_mul(d, d, 1024.0))
        op("vector", lambda e, r=rot, b=brot: e.tensor_scalar(xn32[b], xt[r], msr[r][:, 0:1], msr[r][:, 1:2],
                                                              ALU.subtract, ALU.mult))
        ve_xn = mark("vector")
        xt_war[rot] = ve_xn
        W("vector", (s_ve, ve_xn))
        if scratch_war[brot]:
            W("vector", (s_scr, s_scr.n), (s_scrA, s_scrA.n))
        op("vector", lambda e, b=brot: e.tensor_copy(hitm[b], xn32[b]))
        if want_lo:
            edge("vector")
            op("vector", lambda e, b=brot: e.tensor_tensor(lotm[b], xn32[b], hitm[b], ALU.subtract))
        ve_lo = mark("vector")
        return ve_lo, brot

    def emit_kv_tile(fm_slices, kdst, vdst):
        marks = {}
        v3 = vdst.rearrange("p (h e) -> p h e", e=65)
        for mh in range(2):
            bname = f"p_mm{mh}"
            rot = p_mm[mh]
            use_bank(bname)
            for cb in range(8):
                l = fm_slices[cb]
                rr = W_KP[cb][:, mh * 512:mh * 512 + 512]
                op("tensor", lambda e, o=rot, l_=l, r_=rr, st=(cb == 0): e.matmul(o, l_, r_, start=st, stop=False))
            bslice = bkpr[0:1, mh * 512:mh * 512 + 512]
            op("tensor", lambda e, o=rot, b_=bslice: e.matmul(o, onesc128b_s, b_, start=False, stop=True))
            pe_k = mark("tensor")
            W("scalar", (s_pe, pe_k))
            op("scalar", lambda e, d=kdst[:, mh * 512:mh * 512 + 512], s=rot:
               e.activation(d, s, AF.Relu, scale=0.125))
            se_k = mark("scalar")
            done_bank(bname, ("scalar", se_k))
            marks[("k", mh)] = se_k
        for mh in range(2):
            bname = f"p_mm{mh}"
            rot = p_mm[mh]
            use_bank(bname)
            for cb in range(8):
                l = fm_slices[cb]
                rr = wvb[cb][:, mh * 512:mh * 512 + 512]
                op("tensor", lambda e, o=rot, l_=l, r_=rr, st=(cb == 0): e.matmul(o, l_, r_, start=st, stop=False))
            bslice = bvr[0:1, mh * 512:mh * 512 + 512]
            op("tensor", lambda e, o=rot, b_=bslice: e.matmul(o, onesc128b_s, b_, start=False, stop=True))
            pe_v = mark("tensor")
            W("scalar", (s_pe, pe_v))
            dst8 = v3[:, mh * 8:mh * 8 + 8, 0:64]
            src8 = rot.rearrange("p (h d) -> p h d", d=64)
            op("scalar", lambda e, d=dst8, s=src8: e.copy(d, s))
            se_v = mark("scalar")
            done_bank(bname, ("scalar", se_v))
            marks[("v", mh)] = se_v
        return marks

    PH_DONE = [False]

    def s_banks(tt):
        return (p_ctx[0], p_ctx[1]) if tt % 2 == 0 else (p_sc, p_bc)

    def s_bname(tt):
        return ("p_ctx0", "p_ctx1") if tt % 2 == 0 else ("p_sc", "p_bc")

    def s_region(tt, h):
        g = h // 2
        half = (h % 2) * 64
        b0, b1 = s_banks(tt)
        if g < 7:
            return b0[half:half + 64, 65 * g:65 * g + 65]
        return b1[half:half + 64, 0:65]

    if ph <= 0:
        PH_DONE[0] = True
    for i in range(NT_S):
        v3i = vt[i].rearrange("p (h e) -> p h e", e=65)
        op("gpsimd", lambda e, d=v3i[:, :, 64:65]: e.memset(d, 1.0))
    op("gpsimd", lambda e: e.memset(S_acc, 0.0))
    gp_vones = mark("gpsimd")
    if PH_DONE[0]:
        NT_C_eff = 0
    else:
        NT_C_eff = NT_C
    fmA_war = [0, 0]
    scrA_done = []
    sacc_m = [0]
    peS_m = [0]

    fmrd = {}

    def emit_A_reads(g2):
        fmbuf = hi_fm if (g2 % 2) == 0 else lo_fm
        if fmA_war[g2 % 2]:
            W("sync", (s_pe, fmA_war[g2 % 2]))
        W("sync", (s_scrA, scrA_done[g2]))
        for cb in range(8):
            src = scr_c[g2 * 512:(g2 + 1) * 512, cb * 128:(cb + 1) * 128]
            dma("sync", s_fmA[g2 % 2], fmbuf[cb][:, 0:512], src, transpose=True)
        fmrd[g2] = s_fmA[g2 % 2].n

    def emit_A_group(g2):
        fmbuf = hi_fm if (g2 % 2) == 0 else lo_fm
        if g2 not in fmrd:
            emit_A_reads(g2)
        v_fmrd = fmrd[g2]
        for tt in range(g2 * 4, g2 * 4 + 4):
            ro = tt % 2
            col = (tt % 4) * 128
            fm_slices = [fmbuf[cb][:, col:col + 128] for cb in range(8)]
            W("tensor", (s_fmA[g2 % 2], v_fmrd))
            marks = emit_kv_tile(fm_slices, ktA[ro], vtA[ro])
            W("tensor", (s_se, marks[("k", 1)]), (s_se, marks[("v", 1)]))
            n0, n1 = s_bname(tt)
            use_bank(n0)
            use_bank(n1)
            for h in range(H):
                l = ktA[ro][:, h * 64:h * 64 + 64]
                rr = vtA[ro].rearrange("p (hh e) -> p hh e", e=65)[:, h, :]
                o = s_region(tt, h)
                op("tensor", lambda e, o_=o, l_=l, r_=rr:
                   e.matmul(o_, l_, r_, start=True, stop=True))
            pe_S = mark("tensor")
            peS_m[0] = pe_S
            b0, b1 = s_banks(tt)
            W("vector", (s_pe, pe_S))
            if sacc_m[0]:
                W("vector", (s_ve, sacc_m[0]))
            op("vector", lambda e, b_=b0: e.tensor_tensor(S_acc[:, 0:455], S_acc[:, 0:455], b_[:, 0:455], ALU.add))
            op("vector", lambda e, b_=b1: e.tensor_tensor(S_acc[:, 455:520], S_acc[:, 455:520], b_[:, 0:65], ALU.add))
            sacc_m[0] = mark("vector")
            done_bank(n0, ("vector", sacc_m[0]))
            done_bank(n1, ("vector", sacc_m[0]))
        fmA_war[g2 % 2] = peS_m[0]

    b1a_gp = [0, 0]
    prev_store = [0, 0]

    def emit_b1a_A(spx, t):
        basex = spx * SUB
        pw = ((s_out, max(prev_store)),) if (spx > 0 and t < 2) else None
        emit_ln_A(xm[basex + t * 128: basex + (t + 1) * 128, :], t % 4,
                  want_x2=x2cb[spx % 2][:, t:t + 1], pre_wait=pw)

    def emit_b1a_B(spx, t):
        basex = spx * SUB
        ve_lo, brot = emit_ln_B(t % 4)
        b1a_gp[brot] = ve_lo
        W("gpsimd", (s_ve, ve_lo))
        v1 = dma("gpsimd", s_scr, scr_hi[basex + t * 128: basex + (t + 1) * 128, :], hitm[brot])
        v2 = dma("gpsimd", s_scr, scr_lo[basex + t * 128: basex + (t + 1) * 128, :], lotm[brot])
        scratch_war[brot] = v2

    jobsA = [(xc[t * 128:(t + 1) * 128, :], None) for t in range(NT_C_eff)]
    if not PH_DONE[0]:
        jobsA += [(xm[t * 128:(t + 1) * 128, :], x2cb[0][:, t:t + 1]) for t in range(NT_S)]
    NJ = len(jobsA)

    def job_load(j):
        if j < NJ:
            emit_ln_load(jobsA[j][0], j % 4)

    def job_stats(j):
        if j < NJ:
            emit_ln_stats(j % 4, want_x2=jobsA[j][1])

    ve_bias_rows = 0
    if NT_C_eff:
        W("tensor", (s_wcA, gp_wcast_n), (s_gp, gp_smallcast))
        ve_bias_rows = emit_wfusion(1)
        for j in range(min(4, NJ)):
            job_load(j)
        job_stats(0)
        job_stats(1)
        for t in range(NT_C_eff):
            ve_lo, brot = emit_ln_B(t % 4, want_lo=False)
            W("scalar", (s_ve, ve_lo))
            scratch_war[brot] = dma("scalar", s_scrA, scr_c[t * 128:(t + 1) * 128, :], hitm[brot])
            if t % 4 == 3:
                scrA_done.append(s_scrA.n)
                if t // 4 < 2:
                    emit_A_reads(t // 4)
            job_load(t + 4)
            job_stats(t + 2)
        for j in range(NT_C_eff + 2, NJ):
            job_stats(j)
        W("tensor", (s_ve, ve_bias_rows), (s_gp, gp_vones))
        for t in range(NT_S):
            emit_b1a_B(0, t)
        for g2 in range(NT_C_eff // 4):
            emit_A_group(g2)
    W("vector", (s_w0, ld_small), (s_ve, sacc_m[0]))
    op("vector", lambda e: e.tensor_scalar_mul(S_seed, S_acc, flagc[:, 0:1]))
    ve_seed = mark("vector")
    pe_phaseA_last = peS_m[0]

    if ph <= 1:
        PH_DONE[0] = True

    prev_scan_pe = 0
    gp_cend = [0, 0]
    prev_ctxn_ve = 0
    prev_y_pe = pe_phaseA_last
    prev_hg_ve = 0
    prev_gates_pe = 0
    prev_stage_gd = [0, 0]
    prev_out_ve = [0, 0]
    ve_final = 0
    se_scan_prev = 0

    def emit_b1b_hi(spx):
        basex = spx * SUB
        W("sync", (s_pe, s_pe.n))
        W("sync", (s_scr, s_scr.n))
        for g4 in range(NT_S // 4):
            for cb in range(8):
                src = scr_hi[basex + g4 * 512: basex + (g4 + 1) * 512, cb * 128:(cb + 1) * 128]
                dma("sync", s_fm, hi_fm[cb][:, g4 * 512:g4 * 512 + 512], src, transpose=True)

    def emit_b1b_lo(spx, ve_guard):
        basex = spx * SUB
        if ve_guard:
            W("sync", (s_ve, ve_guard))
        W("sync", (s_scr, s_scr.n))
        for g4 in range(NT_S // 4):
            for cb in range(8):
                src = scr_lo[basex + g4 * 512: basex + (g4 + 1) * 512, cb * 128:(cb + 1) * 128]
                dma("sync", s_fm, lo_fm[cb][:, g4 * 512:g4 * 512 + 512], src, transpose=True)
        return s_fm.n

    if not PH_DONE[0]:
        emit_b1b_hi(0)
        v_fm_next = emit_b1b_lo(0, 0)

    W("gpsimd", (s_scrA, s_scrA.n), (s_scr, s_scr.n))
    emit_wfusion(0)
    ve_wq = s_ve.n
    pairs2 = [(fbiasb, fbias), (boutr, bout)]
    for cb in range(8):
        pairs2.append((woutb[cb][:, :], wout[cb * 128:(cb + 1) * 128, :]))
        pairs2.append((ballb[cb][:, :], ball[cb * 128:(cb + 1) * 128, :]))
    for p in range(4):
        pairs2.append((aflb[p][:, :], afl[p * 128:(p + 1) * 128, :]))
    gp_wcast2_n = cast_loads(pairs2, s_wcC)

    for sp in range(NSUB if not PH_DONE[0] else 0):
        base = sp * SUB
        v_fm_all = v_fm_next
        op("vector", lambda e, x2=x2cb[sp % 2]: e.tensor_copy(deng[:, 0:1], x2[:, 0:1]))
        ve_x2 = mark("vector")

        if ph <= 2:
            continue
        for tb in range(NTB_S):
            use_bank("p_sc")
            W("tensor", (s_fm, v_fm_all), (s_ve, ve_p2), (s_gp, gp_protos))
            first = True
            for cb in range(8):
                phs = phTa[:, cb * 32:(cb + 1) * 32]
                pls = plTa[:, cb * 32:(cb + 1) * 32]
                for (lh, rh) in ((phs, hi_fm[cb]), (pls, hi_fm[cb]), (phs, lo_fm[cb])):
                    rsl = rh[:, tb * 512:tb * 512 + 512]
                    op("tensor", lambda e, l=lh, rr=rsl, st=first:
                       e.matmul(p_sc[0:32, :], l, rr, start=st, stop=False))
                    first = False
            op("tensor", lambda e: e.matmul(p_sc[0:32, :], p2h, onesr512b_s, start=False, stop=False))
            op("tensor", lambda e: e.matmul(p_sc[0:32, :], p2l, onesr512b_s, start=False, stop=True))
            pe_d = mark("tensor")
            W("scalar", (s_pe, pe_d))
            op("scalar", lambda e, t_=tb: e.copy(da[:, t_ * 512:t_ * 512 + 512], p_sc[0:32, :]))
            se_dist = mark("scalar")
            done_bank("p_sc", ("scalar", se_dist))

        W("tensor", (s_se, se_dist))
        if sp > 0:
            W("tensor", (s_pe, prev_gates_pe))
        for r in range(NT_S // 4):
            use_bank("p_bc")
            for j in range(4):
                tt = r * 4 + j
                src = da[:, tt * 128:tt * 128 + 128]
                o = p_bc[0:128, j * 32:j * 32 + 32]
                op("tensor", lambda e, o_=o, s=src: e.transpose(o_, s, idn32f_s))
            pe_tr = mark("tensor")
            W("vector", (s_pe, pe_tr))
            op("vector", lambda e, r_=r: e.tensor_copy(dt1[:, r_ * 128:r_ * 128 + 128], p_bc[0:128, 0:128]))
            ve_d2 = mark("vector")
            done_bank("p_bc", ("vector", ve_d2))

        d3 = dt1.rearrange("p (j c) -> p j c", c=32)
        x2e = x2cb[sp % 2][:, 0:NT_S].rearrange("p (j o) -> p j o", o=1).broadcast_to([128, NT_S, 32])
        W("vector", (s_ve, ve_x2), (s_ve, ve_d2))
        op("vector", lambda e, d_=d3, x_=x2e: e.tensor_tensor(d_, d_, x_, ALU.add))
        ve_d2f = mark("vector")
        W("scalar", (s_ve, ve_d2f))
        op("scalar", lambda e: e.activation(dt2, dt1, AF.Relu))
        edge("scalar")
        op("scalar", lambda e: e.activation(dt3, dt2, AF.Sqrt))
        se_s0 = mark("scalar")
        W("vector", (s_se, se_s0), (s_ve, ve_tcol))
        op("vector", lambda e: e.reciprocal(dt1, dt3))
        edge("vector")
        op("vector", lambda e: e.tensor_tensor(dt1, dt2, dt1, ALU.mult))
        edge("vector")
        op("vector", lambda e: e.tensor_tensor(dt1, dt3, dt1, ALU.add))
        edge("vector")
        op("vector", lambda e: e.tensor_scalar_mul(Lg, dt1, tcol[:, 0:1]))
        ve_log = mark("vector")
        if sp + 1 < NSUB:
            emit_b1a_A(sp + 1, 0)
            emit_b1a_A(sp + 1, 1)

        L3 = Lg.rearrange("p (j c) -> p j c", c=32)
        m1o = m1g.rearrange("p (j o) -> p j o", o=1)
        m2o = m2g.rearrange("p (j o) -> p j o", o=1)
        dno = deng.rearrange("p (j o) -> p j o", o=1)
        m1e = m1o.broadcast_to([128, NT_S, 32])
        m2e = m2o.broadcast_to([128, NT_S, 32])
        rde = rdeng.rearrange("p (j o) -> p j o", o=1).broadcast_to([128, NT_S, 32])
        sel3 = sel1.rearrange("p (j c) -> p j c", c=32)
        l23 = l2b.rearrange("p (j c) -> p j c", c=32)
        ee3 = eex.rearrange("p (j c) -> p j c", c=32)
        eg3 = egb.rearrange("p (j c) -> p j c", c=32)
        gt3 = gtm.rearrange("p (j c) -> p j c", c=32)
        W("vector", (s_ve, ve_log))
        op("vector", lambda e: e.tensor_reduce(m1o, L3, AX.X, ALU.max))
        edge("vector")
        op("vector", lambda e: e.tensor_tensor(sel3, L3, m1e, ALU.is_ge))
        edge("vector")
        op("vector", lambda e: e.scalar_tensor_tensor(l23, sel3, -1e30, L3, ALU.mult, ALU.add))
        edge("vector")
        op("vector", lambda e: e.tensor_reduce(m2o, l23, AX.X, ALU.max))
        edge("vector")
        op("vector", lambda e: e.tensor_tensor(l23, L3, m1e, ALU.subtract))
        ve_g1 = mark("vector")
        W("scalar", (s_ve, ve_g1))
        op("scalar", lambda e: e.activation(eex, l2b, AF.Exp))
        se_ex = mark("scalar")
        W("vector", (s_se, se_ex))
        op("vector", lambda e: e.tensor_tensor(sel3, L3, m2e, ALU.is_ge))
        edge("vector")
        op("vector", lambda e: e.tensor_tensor(eg3, ee3, sel3, ALU.mult))
        edge("vector")
        op("vector", lambda e: e.tensor_reduce(dno, eg3, AX.X, ALU.add))
        edge("vector")
        op("vector", lambda e: e.reciprocal(rdeng, deng))
        edge("vector")
        op("vector", lambda e: e.tensor_tensor(gt3, eg3, rde, ALU.mult))
        ve_g = mark("vector")

        W("tensor", (s_ve, ve_g))
        use_bank("p_trb")

        def _bf16_transpose(e, o_, s):
            with nc.allow_low_precision(reason="bf16 gates transpose, values are bf16 by design"):
                return e.transpose(o_, s, idn128b_s)

        for tt in range(NT_S):
            src = gtm[:, tt * 32:tt * 32 + 32]
            o = p_trb[0:32, tt * 128:tt * 128 + 128]
            op("tensor", lambda e, o_=o, s=src: _bf16_transpose(e, o_, s))
        pe_gt = mark("tensor")
        prev_gates_pe = pe_gt
        W("vector", (s_pe, pe_gt))
        op("vector", lambda e: e.tensor_copy(gatesT[0:32, 0:SUB], p_trb[0:32, 0:SUB]))
        ve_gT = mark("vector")
        done_bank("p_trb", ("vector", ve_gT))
        if sp + 1 < NSUB:
            emit_b1a_B(sp + 1, 0)
            emit_b1a_A(sp + 1, 2)

        if ph <= 3:
            continue
        if sp == 0:
            W("tensor", (s_ve, ve_wq))
        for qk in range(2):
            Wsrc = W_QP if qk == 0 else W_KP
            bsrc = bqpr if qk == 0 else bkpr
            dstl = qp_fm if qk == 0 else kp_fm
            for jb in range(8):
                for tb in range(NTB_S):
                    rb = (jb * NTB_S + tb + qk * 8) % 2
                    bname = f"p_mm{rb}"
                    rot = p_mm[rb]
                    use_bank(bname)
                    for cb in range(8):
                        l = Wsrc[cb][:, jb * 128:jb * 128 + 128]
                        rr = hi_fm[cb][:, tb * 512:tb * 512 + 512]
                        op("tensor", lambda e, o=rot, l_=l, r_=rr, st=(cb == 0):
                           e.matmul(o, l_, r_, start=st, stop=False))
                    bsl = bsrc[0:1, jb * 128:jb * 128 + 128]
                    op("tensor", lambda e, o=rot, b_=bsl: e.matmul(o, b_, onesr512b_s, start=False, stop=True))
                    pe_q = mark("tensor")
                    W("scalar", (s_pe, pe_q))
                    d = dstl[jb][:, tb * 512:tb * 512 + 512]
                    op("scalar", lambda e, d_=d, s=rot: e.activation(d_, s, AF.Relu, scale=0.125))
                    se_qk = mark("scalar")
                    done_bank(bname, ("scalar", se_qk))

        if sp + 1 < NSUB:
            emit_b1a_B(sp + 1, 1)
            emit_b1a_A(sp + 1, 3)
        if sp > 0:
            W("tensor", (s_pe, prev_scan_pe))
        for t in range(NT_S):
            g4 = t // 4
            col = g4 * 512 + (t % 4) * 128
            fm_slices = [hi_fm[cb][:, col:col + 128] for cb in range(8)]
            emit_kv_tile(fm_slices, kt[t], vt[t])
        gp_ones_last = s_gp.n
        se_kv_last = s_se.n

        if sp + 1 < NSUB:
            emit_b1a_B(sp + 1, 2)
            emit_b1a_B(sp + 1, 3)
        if sp == 0:
            W("tensor", (s_wcC, gp_wcast2_n))
        ve_hg = 0
        for prc in range(4):
            for tb in range(NTB_S):
                rb = (prc * NTB_S + tb) % 2
                bname = f"p_mm{rb}"
                rot = p_mm[rb]
                use_bank(bname)
                for cb in range(8):
                    l = ballb[cb][:, prc * 128:prc * 128 + 128]
                    rr = hi_fm[cb][:, tb * 512:tb * 512 + 512]
                    op("tensor", lambda e, o=rot, l_=l, r_=rr, st=(cb == 0), sp_=(cb == 7):
                       e.matmul(o, l_, r_, start=st, stop=sp_))
                pe_h = mark("tensor")
                W("tensor", (s_ve, ve_gT))
                use_bank("p_bc")
                l2 = E2s[:, prc * 128:prc * 128 + 128]
                r2 = gatesT[0:32, tb * 512:tb * 512 + 512]
                op("tensor", lambda e, l_=l2, r_=r2: e.matmul(p_bc, l_, r_, start=True, stop=True))
                pe_gr = mark("tensor")
                W("vector", (s_pe, pe_gr))
                if ve_hg:
                    W("vector", (s_ve, ve_hg))
                if hps_dbg is not None and prc == 0 and tb == 0 and sp == 0:
                    op("vector", lambda e, s=rot: e.tensor_copy(Asm[1], s))
                    ve_dbg = mark("vector")
                    W("gpsimd", (s_ve, ve_dbg))
                    dma("gpsimd", s_gd, hps_dbg, Asm[1])
                op("vector", lambda e: e.tensor_copy(grb, p_bc))
                ve_grb = mark("vector")
                done_bank("p_bc", ("vector", ve_grb))
                W("vector", (s_ve, ve_grb))
                if grb_dbg is not None and prc == 0 and tb == 0 and sp == 0:
                    edge("vector")
                    op("vector", lambda e: e.tensor_copy(Asm[0], grb))
                    ve_dbg2 = mark("vector")
                    W("gpsimd", (s_ve, ve_dbg2))
                    dma("gpsimd", s_gd, grb_dbg, Asm[0])
                    W("vector", (s_gd, s_gd.n))
                d = hgt[prc][:, tb * 512:tb * 512 + 512]
                op("vector", lambda e, d_=d, s=rot: e.tensor_tensor(d_, s, grb, ALU.mult))
                ve_hg = mark("vector")
                done_bank(bname, ("vector", ve_hg))
        prev_hg_ve = ve_hg
        if sp + 1 < NSUB:
            emit_b1b_hi(sp + 1)

        if ph <= 4:
            continue
        seed_src = S_seed if sp == 0 else Cend
        if sp == 0:
            W("tensor", (s_ve, ve_seed))
        else:
            W("tensor", (s_se, se_scan_prev))
        ve_asm = 0
        se_snap = 0
        pe_ctx_group = [0, 0]
        ve_ctx_evac = [0, 0]
        se_stage = [0, 0]
        pe_head_last = [0, 0]
        se_head_last = [0, 0]
        W("vector", (s_se, s_se.n))
        for h in range(H):
            hb = h // 2
            hp = (h % 2) * 64
            hr = h % 2
            ssrc = seed_src[hp:hp + 64, (h // 2) * 65:(h // 2) * 65 + 65]
            if pe_head_last[hr]:
                W("vector", (s_pe, pe_head_last[hr]))
            if sp > 0:
                W("vector", (s_se, se_scan_prev))
            if gp_cend[hr]:
                W("vector", (s_gp, gp_cend[hr]))
            op("vector", lambda e, s=qp_fm[hb][hp:hp + 64, :], d=qkst[hr][0:64, 0:SUB]:
               e.tensor_copy(d, s))
            op("vector", lambda e, s=kp_fm[hb][hp:hp + 64, :], d=qkst[hr][0:64, SUB:2 * SUB]:
               e.tensor_copy(d, s))
            op("vector", lambda e, s=ssrc, c0=hr * 130: e.tensor_copy(Csnap[0:64, c0:c0 + 65], s))
            ve_stage0 = mark("vector")
            qsl = qkst[hr][0:64, 0:SUB]
            ksl = qkst[hr][0:64, SUB:2 * SUB]
            use_bank("p_st")
            W("tensor", (s_ve, ve_stage0))
            for i in range(NCH_S):
                l6 = kt[i][:, h * 64:h * 64 + 64]
                r6 = vt[i].rearrange("p (hh e) -> p hh e", e=65)[:, h, :]
                op("tensor", lambda e, o_=p_st[0:64, i * 65:(i + 1) * 65], l_=l6, r_=r6:
                   e.matmul(o_, l_, r_, start=True, stop=True))
            pe_S4 = mark("tensor")
            W("scalar", (s_pe, pe_S4))
            if gp_cend[hr]:
                W("scalar", (s_gp, gp_cend[hr]))
            op("scalar", lambda e, d=Ssb[hr][0:64, 0:260]: e.copy(d, p_st[0:64, 0:260]))
            se_Ssb = mark("scalar")
            done_bank("p_st", ("scalar", se_Ssb))
            W("gpsimd", (s_se, se_Ssb), (s_gp, max(gp_cend)))
            op("gpsimd", lambda e, s=Csnap[0:64, hr * 130:hr * 130 + 65]:
               e.tensor_copy(cendt[0:64, 0:65], s))
            edge("gpsimd")
            for i in range(NCH_S):
                op("gpsimd", lambda e, s=Ssb[hr][0:64, i * 65:(i + 1) * 65]:
                   e.tensor_tensor(cendt[0:64, 0:65], cendt[0:64, 0:65], s, ALU.add))
                edge("gpsimd")
            dcend = Cend[hp:hp + 64, (h // 2) * 65:(h // 2) * 65 + 65]
            op("gpsimd", lambda e, d=dcend: e.tensor_copy(d, cendt[0:64, 0:65]))
            gp_cend[hr] = mark("gpsimd")
            for tb in range(NTB_S):
                use_bank("p_sc")
                for j in range(4):
                    ccol = tb * 512 + j * 128
                    l = ksl[:, ccol:ccol + 128]
                    rr = qsl[:, ccol:ccol + 128]
                    o = p_sc[:, j * 128:j * 128 + 128]
                    op("tensor", lambda e, o_=o, l_=l, r_=rr:
                       e.matmul(o_, l_, r_, start=True, stop=True))
                pe_sc = mark("tensor")
                W("vector", (s_pe, pe_sc))
                ar = Asm[tb]
                op("vector", lambda e, a=ar: e.tensor_tensor(a, p_sc, tri_s, ALU.mult))
                ve_asm = mark("vector")
                done_bank("p_sc", ("vector", ve_asm))
                cname = f"p_ctx{tb}"
                use_bank(cname)
                pc = p_ctx[tb]
                snp = Csnap[0:64, hr * 130:hr * 130 + 65]
                op("tensor", lambda e, l_=snp, r_=qsl[:, tb * 512:tb * 512 + 512]:
                   e.matmul(pc[0:65, 0:512], l_, r_, start=True, stop=False))
                W("tensor", (s_se, se_Ssb))
                for i in range(3):
                    sl = Ssb[hr][0:64, i * 65:(i + 1) * 65]
                    rtail = qsl[:, tb * 512 + (i + 1) * 128: tb * 512 + 512]
                    otail = pc[0:65, (i + 1) * 128:512]
                    op("tensor", lambda e, o_=otail, l_=sl, r_=rtail:
                       e.matmul(o_, l_, r_, start=False, stop=False))
                W("tensor", (s_ve, ve_asm))
                for j in range(4):
                    jj = tb * 4 + j
                    l4 = vt[jj].rearrange("p (hh e) -> p hh e", e=65)[:, h, :]
                    r4 = ar[:, j * 128:j * 128 + 128]
                    o4 = pc[0:65, j * 128:j * 128 + 128]
                    op("tensor", lambda e, o_=o4, l_=l4, r_=r4, sp_=(j == 3):
                       e.matmul(o_, l_, r_, start=False, stop=sp_))
                pe_ctx_mms = mark("tensor")
                pe_ctx_group[tb] = pe_ctx_mms
                W("scalar", (s_gd, s_gd.n), (s_pe, pe_ctx_group[tb]))
                dctx = ctxu[hb][hp:hp + 64, tb * 512:tb * 512 + 512]
                op("scalar", lambda e, d=dctx, t_=tb: e.copy(d, p_ctx[t_][0:64, :]))
                op("scalar", lambda e, t_=tb, r=tb % 2: e.copy(stage[r], p_ctx[t_][64:65, :]))
                se_stage[tb] = mark("scalar")
                done_bank(cname, ("scalar", se_stage[tb]))
                W("sync", (s_se, se_stage[tb]))
                dn = nall[h:h + 1, tb * 512:tb * 512 + 512]
                prev_stage_gd[tb] = dma("sync", s_gd, dn, stage[tb % 2])
            pe_head_last[hr] = s_pe.n
            se_head_last[hr] = s_se.n
        prev_scan_pe = s_pe.n
        se_scan_prev = s_se.n

        W("vector", (s_gd, prev_stage_gd[0]), (s_gd, prev_stage_gd[1]))

        def _lp(e, fn):
            with nc.allow_low_precision(reason="bf16 norm reciprocal, consistent with bf16 matmul path"):
                return fn(e)

        op("vector", lambda e: _lp(e, lambda e2: e2.tensor_scalar_add(nall, nall, 1e-6)))
        edge("vector")
        op("vector", lambda e: _lp(e, lambda e2: e2.reciprocal(rnb, nall)))
        ve_rn = mark("vector")
        ve_ctxn = 0
        for cb in range(8):
            for tb in range(NTB_S):
                W("tensor", (s_ve, ve_rn))
                use_bank("p_bc")
                l = E3s[:, cb * 128:cb * 128 + 128]
                rr = rnb[0:16, tb * 512:tb * 512 + 512]
                op("tensor", lambda e, l_=l, r_=rr: e.matmul(p_bc, l_, r_, start=True, stop=True))
                pe_rn = mark("tensor")
                W("vector", (s_pe, pe_rn))
                d = ctxn[cb][:, tb * 512:tb * 512 + 512]
                s2 = ctxu[cb][:, tb * 512:tb * 512 + 512]
                op("vector", lambda e, d_=d, s_=s2: e.tensor_tensor(d_, s_, p_bc, ALU.mult))
                ve_ctxn = mark("vector")
                done_bank("p_bc", ("vector", ve_ctxn))
        prev_ctxn_ve = ve_ctxn
        if sp + 1 < NSUB:
            v_fm_next = emit_b1b_lo(sp + 1, prev_ctxn_ve)

        if ph <= 5:
            continue
        W("tensor", (s_ve, prev_hg_ve), (s_ve, ve_gT))
        for t in range(NT_S):
            rot = t % 2
            xrot = t % 4
            if prev_out_ve[rot]:
                W("sync", (s_ve, prev_out_ve[rot]))
            if xt_war[xrot]:
                W("sync", (s_ve, xt_war[xrot]))
            vx = dma("sync", s_xb[4 + xrot], xt[xrot][:, :], xm[base + t * 128: base + (t + 1) * 128, :])
            for jh in range(2):
                bname = f"p_mm{jh}"
                pm = p_mm[jh]
                use_bank(bname)
                for prc in range(4):
                    l = hgt[prc][:, t * 128:t * 128 + 128]
                    rr = aflb[prc][:, jh * 512:jh * 512 + 512]
                    op("tensor", lambda e, o=pm, l_=l, r_=rr, st=(prc == 0):
                       e.matmul(o, l_, r_, start=st, stop=False))
                lgb = gatesT[0:32, t * 128:t * 128 + 128]
                rgb = fbiasb[0:32, jh * 512:jh * 512 + 512]
                op("tensor", lambda e, o=pm, l_=lgb, r_=rgb: e.matmul(o, l_, r_, start=False, stop=False))
                rb2 = boutr[0:1, jh * 512:jh * 512 + 512]
                op("tensor", lambda e, o=pm, b_=rb2: e.matmul(o, onesc128b_s, b_, start=False, stop=False))
                W("tensor", (s_ve, ve_ctxn))
                for cb in range(8):
                    l = ctxn[cb][:, t * 128:t * 128 + 128]
                    rr = woutb[cb][:, jh * 512:jh * 512 + 512]
                    op("tensor", lambda e, o=pm, l_=l, r_=rr, sp_=(cb == 7):
                       e.matmul(o, l_, r_, start=False, stop=sp_))
                pe_y = mark("tensor")
                W("vector", (s_pe, pe_y), (s_xb[4 + xrot], vx), (s_out, s_out.n))
                d = outb[rot][:, jh * 512:jh * 512 + 512]
                xs = xt[xrot][:, jh * 512:jh * 512 + 512]
                op("vector", lambda e, d_=d, s_=xs, p=pm: e.tensor_tensor(d_, s_, p, ALU.add))
                ve_final = mark("vector")
                done_bank(bname, ("vector", ve_final))
            xt_war[xrot] = ve_final
            prev_y_pe = s_pe.n
            W("sync", (s_ve, ve_final))
            prev_store[rot] = dma("sync", s_out, y_out[base + t * 128: base + (t + 1) * 128, :], outb[rot])
            prev_out_ve[rot] = ve_final

    W("sync", (s_out, s_out.n))

    dbg_outs = {}
    if debug:
        dbg_srcs = {
            "d_hi_fm0": (hi_fm[0], [128, SUB], BF16),
            "d_qp_fm0": (qp_fm[0], [128, SUB], BF16),
            "d_kt0": (kt[0], [128, C], BF16),
            "d_vt0": (vt[0], [128, 1040], BF16),
            "d_gatesT": (gatesT, [32, SUB], BF16),
            "d_Lg": (Lg, [128, NG], F32),
            "d_Sseed": (S_seed, [128, 520], BF16),
            "d_nall": (nall, [16, SUB], BF16),
            "d_ctxu0": (ctxu[0], [128, SUB], BF16),
            "d_hgt0": (hgt[0], [128, SUB], BF16),
            "d_x2c": (x2c, [128, NT_S], F32),
            "d_Cend": (Cend, [128, 520], BF16),
        }
        for name in debug:
            if name not in dbg_srcs:
                continue
            src, shape, dt = dbg_srcs[name]
            dv = nc.dram_tensor(name, shape, dt, kind="ExternalOutput").ap()
            dma("sync", s_out, dv, src)
        W("sync", (s_out, s_out.n))

    tmax = trunc if trunc is not None else seqc[0] + 1

    with nc.Block() as blk:
        @blk.sync
        def _(eng):
            for sq, f in ops["sync"]:
                if sq <= tmax:
                    f(eng)

        @blk.tensor
        def _(eng):
            for sq, f in ops["tensor"]:
                if sq <= tmax:
                    f(eng)

        @blk.vector
        def _(eng):
            for sq, f in ops["vector"]:
                if sq <= tmax:
                    f(eng)

        @blk.scalar
        def _(eng):
            for sq, f in ops["scalar"]:
                if sq <= tmax:
                    f(eng)

        @blk.gpsimd
        def _(eng):
            for sq, f in ops["gpsimd"]:
                if sq <= tmax:
                    f(eng)

    return nc


def _host_inputs(inputs, TM=2048, TC=2048):
    x = np.ascontiguousarray(np.asarray(inputs["L0_attn_x"], dtype=np.float32))
    w_qkv = np.asarray(inputs["L0_attn_w_qkv"], dtype=np.float32)
    b_qkv = np.asarray(inputs["L0_attn_b_qkv"], dtype=np.float32)
    w_out = np.asarray(inputs["L0_attn_w_out"], dtype=np.float32)
    b_out = np.asarray(inputs["L0_attn_b_out"], dtype=np.float32)
    proj = np.asarray(inputs["L0_attn_proj"], dtype=np.float32)
    protos = np.asarray(inputs["L0_ffn_prototypes"], dtype=np.float32)
    Bm = np.asarray(inputs["L0_ffn_B"], dtype=np.float32)
    Am = np.asarray(inputs["L0_ffn_A"], dtype=np.float32)
    bias = np.asarray(inputs["L0_ffn_bias"], dtype=np.float32)
    tempv = np.asarray(inputs["L0_ffn_temp"], dtype=np.float32).reshape(1, 1)

    common = {}
    common["wqT"] = np.ascontiguousarray(w_qkv[:, 0:C].T)
    common["wkT"] = np.ascontiguousarray(w_qkv[:, C:2 * C].T)
    common["wv"] = np.ascontiguousarray(w_qkv[:, 2 * C:3 * C])
    projP = np.zeros((128, 1024), np.float32)
    bqc = np.zeros((128, 8), np.float32)
    bkc = np.zeros((128, 8), np.float32)
    for g in range(8):
        projP[0:64, g * 128:g * 128 + 64] = proj[2 * g]
        projP[64:128, g * 128 + 64:g * 128 + 128] = proj[2 * g + 1]
        bqc[0:64, g] = b_qkv[2 * g * 64:(2 * g + 1) * 64]
        bqc[64:128, g] = b_qkv[(2 * g + 1) * 64:(2 * g + 2) * 64]
        bkc[0:64, g] = b_qkv[C + 2 * g * 64: C + (2 * g + 1) * 64]
        bkc[64:128, g] = b_qkv[C + (2 * g + 1) * 64: C + (2 * g + 2) * 64]
    common["projP"] = projP
    common["bqc"] = bqc
    common["bkc"] = bkc
    common["bv"] = np.ascontiguousarray(b_qkv[2 * C:3 * C].reshape(1, C))
    common["wout"] = w_out
    common["bout"] = b_out.reshape(1, C)
    common["protosP"] = np.ascontiguousarray(
        protos.T.reshape(8, 128, 32).transpose(1, 0, 2).reshape(128, 256))
    common["ball"] = np.ascontiguousarray(Bm.transpose(2, 0, 1).reshape(C, 32 * 16))
    common["afl"] = np.ascontiguousarray(Am.transpose(0, 2, 1).reshape(32 * 16, C))
    common["fbias"] = bias
    common["temp"] = tempv
    common["tri"] = np.ascontiguousarray(np.tile(np.triu(np.ones((128, 128), np.float32)), (1, 4))).astype(BFNP)
    common["idn64b"] = np.eye(64, dtype=BFNP)
    common["idn128b"] = np.eye(128, dtype=BFNP)
    common["idn32f"] = np.eye(32, dtype=np.float32)
    common["ones_r512f"] = np.ones((1, 512), np.float32)
    common["ones_r512b"] = np.ones((1, 512), BFNP)
    common["ones_c128b"] = np.ones((1, 128), BFNP)
    common["ones_r128f"] = np.ones((1, 128), np.float32)
    common["ones_col128f"] = np.ones((128, 1), np.float32)
    E2 = np.zeros((32, 512), BFNP)
    for k in range(32):
        E2[k, k * 16:(k + 1) * 16] = 1
    common["E2c"] = E2
    E3 = np.zeros((16, C), BFNP)
    for r in range(8):
        E3[2 * r, r * 128:r * 128 + 64] = 1
        E3[2 * r + 1, r * 128 + 64:r * 128 + 128] = 1
    common["E3c"] = E3

    in_maps = []
    for core in range(NCORE):
        b = core // 2
        half = core % 2
        m = dict(common)
        m["xm"] = np.ascontiguousarray(x[b, half * TM:(half + 1) * TM, :])
        m["xc"] = np.ascontiguousarray(x[b, 0:TC, :])
        f = np.zeros((128, 1), np.float32)
        f[:, 0] = float(half)
        m["flag"] = f
        in_maps.append(m)
    return in_maps


def kernel(**inputs):
    TM, TC, SUB = 2048, 2048, 512
    nc = build(TM=TM, TC=TC, SUB=SUB)
    in_maps = _host_inputs(inputs, TM, TC)
    res = run_bass_kernel_spmd(nc, in_maps, core_ids=list(range(NCORE)))
    out = np.zeros((B, T, C), np.float32)
    for core in range(NCORE):
        b = core // 2
        half = core % 2
        out[b, half * TM:(half + 1) * TM, :] = res.results[core]["y"]
    return out


# version 32 (final)
# speedup vs baseline: 1.0410x; 1.0319x over previous
import numpy as np
import concourse.bass as bass
import concourse.mybir as mybir
from concourse.bass_utils import run_bass_kernel_spmd
import ml_dtypes

BFNP = ml_dtypes.bfloat16
F32 = mybir.dt.float32
BF16 = mybir.dt.bfloat16
AF = mybir.ActivationFunctionType
ALU = mybir.AluOpType
AX = mybir.AxisListType

B, T, C, H, D = 4, 4096, 1024, 16, 64
NCORE = 8


def build(TM=2048, TC=2048, SUB=512, debug=(), ph=99, trunc=None):
    NSUB = TM // SUB
    NT_S = SUB // 128
    NTB_S = SUB // 512
    NCH_S = NT_S
    NT_C = TC // 128
    NG = 32 * NT_S
    nc = bass.Bass("TRN2", target_bir_lowering=False)

    def din(name, shape, dt=F32):
        return nc.dram_tensor(name, shape, dt, kind="ExternalInput").ap()

    xm = din("xm", [TM, C])
    xc = din("xc", [TC, C])
    flag = din("flag", [128, 1])
    wqT = din("wqT", [C, C]); wkT = din("wkT", [C, C])
    wv = din("wv", [C, C])
    projP = din("projP", [128, 1024])
    bqc = din("bqc", [128, 8]); bkc = din("bkc", [128, 8])
    bv = din("bv", [1, C])
    wout = din("wout", [C, C]); bout = din("bout", [1, C])
    protosT = din("protosT", [C, 32])
    ball = din("ball", [C, 512])
    afl = din("afl", [512, C])
    fbias = din("fbias", [32, C])
    temp = din("temp", [1, 1])
    tri = din("tri", [128, 512], BF16)
    idn64b = din("idn64b", [64, 64], BF16)
    idn128b = din("idn128b", [128, 128], BF16)
    idn32f = din("idn32f", [32, 32])
    ones_r512f = din("ones_r512f", [1, 512])
    ones_r512b = din("ones_r512b", [1, 512], BF16)
    ones_c128b = din("ones_c128b", [1, 128], BF16)
    ones_r128f = din("ones_r128f", [1, 128])
    ones_col128f = din("ones_col128f", [128, 1])
    E2c = din("E2c", [32, 512], BF16)
    E3c = din("E3c", [16, C], BF16)

    y_out = nc.dram_tensor("y", [TM, C], F32, kind="ExternalOutput").ap()

    scr_hi = nc.dram_tensor("scr_hi", [TM, C], BF16, kind="Internal").ap()
    scr_lo = nc.dram_tensor("scr_lo", [TM, C], BF16, kind="Internal").ap()
    scr_c = nc.dram_tensor("scr_c", [TC, C], BF16, kind="Internal").ap()

    def sb(name, shape, dt):
        return nc.alloc_sbuf_tensor(name, shape, dt).ap()

    xt = [sb(f"xt{i}", [128, C], F32) for i in range(4)]
    xn32 = [sb(f"xn32_{i}", [128, C], F32) for i in range(2)]
    hitm = [sb(f"hitm{i}", [128, C], BF16) for i in range(2)]
    lotm = [sb(f"lotm{i}", [128, C], BF16) for i in range(2)]
    stat6 = [sb(f"stat6_{i}", [128, 12], F32) for i in range(4)]
    mv = [sb(f"mv{i}", [128, 2], F32) for i in range(4)]
    msr = [sb(f"msr{i}", [128, 2], F32) for i in range(4)]
    ln1 = [sb(f"ln1_{i}", [128, 1], F32) for i in range(4)]
    x2cb = [sb(f"x2c{i}", [128, NT_S], F32) for i in range(2)]

    hi_fm = [sb(f"hi_fm{cb}", [128, SUB], BF16) for cb in range(8)]
    lo_fm = [sb(f"lo_fm{cb}", [128, SUB], BF16) for cb in range(8)]
    ctxu = lo_fm
    qp_fm = [sb(f"qp_fm{cb}", [128, SUB], BF16) for cb in range(8)]
    kp_fm = [sb(f"kp_fm{cb}", [128, SUB], BF16) for cb in range(8)]

    kt = [sb(f"kt{i}", [128, C], BF16) for i in range(NT_S)]
    ctxn = [kt[cb // 2][:, (cb % 2) * 512:(cb % 2) * 512 + 512] for cb in range(8)]
    vt = [sb(f"vt{i}", [128, 1040], BF16) for i in range(NT_S)]
    hgt = [sb(f"hgt{p}", [128, SUB], BF16) for p in range(4)]
    grb = sb("grb", [128, 512], BF16)

    W_QP = [sb(f"W_QP{cb}", [128, C], BF16) for cb in range(8)]
    W_KP = [sb(f"W_KP{cb}", [128, C], BF16) for cb in range(8)]
    wvb = [sb(f"wvb{cb}", [128, C], BF16) for cb in range(8)]
    woutb = [sb(f"woutb{cb}", [128, C], BF16) for cb in range(8)]
    ballb = [sb(f"ballb{cb}", [128, 512], BF16) for cb in range(8)]
    aflb = [sb(f"aflb{p}", [128, C], BF16) for p in range(4)]
    fbiasb = sb("fbiasb", [32, C], BF16)
    boutr = sb("boutr", [1, C], BF16)
    bvr = sb("bvr", [1, C], BF16)
    bqpr = sb("bqpr", [1, C], BF16)
    bkpr = sb("bkpr", [1, C], BF16)
    projs = sb("projs", [128, 1024], BF16)
    bqcb = sb("bqcb", [128, 8], BF16)
    bkcb = sb("bkcb", [128, 8], BF16)
    qkst = [sb(f"qkst{i}", [64, 1024], BF16) for i in range(2)]

    phTa = sb("phTa", [128, 256], BF16)
    plTa = sb("plTa", [128, 256], BF16)
    p2r = sb("p2r", [1, 32], F32)
    p2h = sb("p2h", [1, 32], BF16)
    p2l = sb("p2l", [1, 32], BF16)
    tprep = sb("tprep", [1, 4], F32)
    tcol = sb("tcol", [128, 1], F32)
    flagc = sb("flagc", [128, 1], F32)
    S_acc = sb("S_acc", [128, 520], F32)
    S_seed = sb("S_seed", [128, 520], BF16)
    Cend = sb("Cend", [128, 520], BF16)
    Csnap = sb("Csnap", [64, 260], BF16)
    Ssb = [sb(f"Ssb{i}", [64, 260], BF16) for i in range(2)]
    cendt = sb("cendt", [64, 65], F32)

    da = sb("da", [32, SUB], F32)
    dt1 = sb("dt1", [128, NG], F32)

    Lg = sb("Lg", [128, NG], F32)
    m1g = sb("m1g", [128, NT_S], F32)
    m2g = sb("m2g", [128, NT_S], F32)
    sel1 = sb("sel1", [128, NG], F32)
    l2b = sb("l2b", [128, NG], F32)
    dt2 = sel1
    dt3 = l2b
    eex = sb("eex", [128, NG], F32)
    egb = eex
    deng = sb("deng", [128, NT_S], F32)
    rdeng = sb("rdeng", [128, NT_S], F32)
    gtm = sb("gtm", [128, NG], BF16)
    gatesT = sb("gatesT", [32, SUB], BF16)

    Asm = [sb(f"Asm{i}", [128, 512], BF16) for i in range(NTB_S)]
    hps_dbg = nc.dram_tensor("d_hps", [128, 512], BF16, kind="ExternalOutput").ap() if "d_hps" in debug else None
    grb_dbg = nc.dram_tensor("d_grb", [128, 512], BF16, kind="ExternalOutput").ap() if "d_hps" in debug else None
    nall = sb("nall", [16, SUB], BF16)
    rnb = sb("rnb", [16, SUB], BF16)
    stage = [sb(f"stage{i}", [1, 512], BF16) for i in range(2)]
    outb = xn32

    tri_s = sb("tri_s", [128, 512], BF16)
    idn128b_s = sb("idn128b_s", [128, 128], BF16)
    idn32f_s = sb("idn32f_s", [32, 32], F32)
    onesr512b_s = sb("onesr512b_s", [1, 512], BF16)
    onesc128b_s = sb("onesc128b_s", [1, 128], BF16)
    onesr128f_s = sb("onesr128f_s", [1, 128], F32)
    onescol128f_s = sb("onescol128f_s", [128, 1], F32)
    E2s = sb("E2s", [32, 512], BF16)
    E3s = sb("E3s", [16, C], BF16)
    tempf = sb("tempf", [1, 1], F32)

    ktA = [kt[0], kt[1]]
    vtA = [vt[0], vt[1]]

    def psn(name, shape, dt=F32):
        return nc.alloc_psum_tensor(name, shape, dt).ap()

    p_mm = [psn("p_mm0", [128, 512]), psn("p_mm1", [128, 512])]
    p_sc = psn("p_sc", [128, 512])
    p_ctx = [psn("p_ctx0", [128, 512]), psn("p_ctx1", [128, 512])]
    p_st = psn("p_st", [128, 512])
    p_bc = psn("p_bc", [128, 512])
    p_trb = psn("p_trb", [128, 1024], BF16)

    ops = {k: [] for k in ("sync", "tensor", "vector", "scalar", "gpsimd")}
    seqc = [0]

    def em(eng, fn):
        seqc[0] += 1
        ops[eng].append((seqc[0], fn))

    class Sem:
        def __init__(self, name):
            self.h = nc.alloc_semaphore(name)
            self.n = 0

        def inc(self, k=1):
            self.n += k
            return self.n

    s_w0 = Sem("s_w0")
    s_wcA = Sem("s_wcA")
    s_wcB = Sem("s_wcB")
    s_wcC = Sem("s_wcC")
    s_wr = [Sem("s_wrA"), Sem("s_wrB")]
    s_xb = [Sem(f"s_x{i}") for i in range(8)]
    s_scr = Sem("s_scr")
    s_scrA = Sem("s_scrA")
    s_fmA = [Sem("s_fmA0"), Sem("s_fmA1")]
    s_fm = Sem("s_fm")
    s_pe = Sem("s_pe")
    s_ve = Sem("s_ve")
    s_se = Sem("s_se")
    s_gp = Sem("s_gp")
    s_gd = Sem("s_gd")
    s_out = Sem("s_out")

    ENG_SEM = {"tensor": s_pe, "vector": s_ve, "scalar": s_se, "gpsimd": s_gp}

    def W(eng, *deps):
        for sem, val in deps:
            if val and val > 0:
                em(eng, lambda e, s=sem.h, v=val: e.wait_ge(s, v))

    def dma(eng_name, sem, out, in_, transpose=False):
        v = sem.inc(16)
        em(eng_name, lambda e, o=out, i=in_, s=sem.h, tr=transpose:
           e.dma_start(out=o, in_=i, transpose=tr).then_inc(s, 16))
        return v

    def mark(eng):
        sem = ENG_SEM[eng]
        v = sem.inc(1)
        _, prev = ops[eng].pop()
        em(eng, lambda e, p=prev, s=sem.h: p(e).then_inc(s, 1))
        return v

    def op(eng, fn):
        em(eng, fn)

    def edge(eng):
        sem = ENG_SEM[eng]
        v = mark(eng)
        W(eng, (sem, v))
        return v

    bank_war = {}

    def use_bank(name):
        for sem, v in bank_war.get(name, ()):
            W("tensor", (sem, v))

    def done_bank(name, *marks):
        bank_war[name] = [(ENG_SEM[eng], v) for eng, v in marks]

    cpool = sb("cpool", [128, 2], F32)
    nc.const_aps.register(F32, 1e-5, cpool[:, 0:1])

    for dst, src in [(tri_s, tri), (idn128b_s, idn128b),
                     (idn32f_s, idn32f),
                     (onesr512b_s, ones_r512b), (onesc128b_s, ones_c128b),
                     (onesr128f_s, ones_r128f),
                     (onescol128f_s, ones_col128f),
                     (E2s, E2c), (E3s, E3c), (tempf, temp),
                     (flagc, flag)]:
        dma("sync", s_w0, dst, src)
    ld_small = s_w0.n

    op("vector", lambda e: e.memset(cpool[:, 0:1], 1e-5))
    ve_cpool = mark("vector")

    W("scalar", (s_w0, ld_small), (s_ve, ve_cpool))
    op("scalar", lambda e: e.activation(tprep[0:1, 0:1], tempf, AF.Abs))
    se_abs = mark("scalar")
    W("vector", (s_se, se_abs))
    op("vector", lambda e: e.tensor_scalar_max(tprep[0:1, 1:2], tprep[0:1, 0:1], 0.1))
    edge("vector")
    op("vector", lambda e: e.reciprocal(tprep[0:1, 2:3], tprep[0:1, 1:2]))
    edge("vector")
    op("vector", lambda e: e.tensor_scalar_mul(tprep[0:1, 3:4], tprep[0:1, 2:3], -0.5))
    ve_t = mark("vector")
    W("tensor", (s_ve, ve_t))
    use_bank("p_bc")
    op("tensor", lambda e: e.matmul(p_bc[0:128, 0:1], onesr128f_s, tprep[0:1, 3:4], start=True, stop=True))
    pe_t = mark("tensor")
    W("vector", (s_pe, pe_t))
    op("vector", lambda e: e.tensor_copy(tcol, p_bc[0:128, 0:1]))
    ve_tcol = mark("vector")
    done_bank("p_bc", ("vector", ve_tcol))

    W("gpsimd", (s_w0, ld_small))
    op("gpsimd", lambda e: e.engine_nop())
    gp_smallcast = mark("gpsimd")

    def cast_loads(pairs, sem):
        for dst_ap, src_sl in pairs:
            dma("gpsimd", sem, dst_ap, src_sl)
        return sem.n

    pairs = [(projs, projP), (bqcb, bqc), (bkcb, bkc), (bvr, bv)]
    for cb in range(8):
        pairs.append((wvb[cb][:, :], wv[cb * 128:(cb + 1) * 128, :]))
    gp_wcast_n = cast_loads(pairs, s_wcA)

    pe_wdone = [0]

    def emit_wfusion(qk):
        Wdst = W_QP if qk == 0 else W_KP
        wsrc = wqT if qk == 0 else wkT
        if pe_wdone[0]:
            W("gpsimd", (s_pe, pe_wdone[0]))
        stg = []
        for cb in range(8):
            stg.append((qp_fm[cb][:, 0:512], wsrc[cb * 128:(cb + 1) * 128, 0:512]))
            stg.append((kp_fm[cb][:, 0:512], wsrc[cb * 128:(cb + 1) * 128, 512:1024]))
        v_stg = cast_loads(stg, s_wcB)
        W("tensor", (s_wcB, v_stg))
        for cbc in range(8):
            buf = qp_fm if cbc < 4 else kp_fm
            for gq in range(2):
                rb = (cbc * 2 + gq) % 2
                bname = f"p_mm{rb}"
                use_bank(bname)
                for gi in range(4):
                    g = gq * 4 + gi
                    lhsT = buf[g][:, (cbc % 4) * 128:(cbc % 4) * 128 + 128]
                    rhs = projs[:, g * 128:(g + 1) * 128]
                    op("tensor", lambda e, o=p_mm[rb][:, gi * 128:gi * 128 + 128], l=lhsT, rr=rhs:
                       e.matmul(o, l, rr, start=True, stop=True))
                pe_w = mark("tensor")
                W("vector", (s_pe, pe_w))
                op("vector", lambda e, d=Wdst[cbc][:, gq * 512:gq * 512 + 512], s=p_mm[rb][:, 0:512]:
                   e.tensor_copy(d, s))
                ve_w = mark("vector")
                done_bank(bname, ("vector", ve_w))
        pe_wdone[0] = pe_w
        bcb = bqcb if qk == 0 else bkcb
        dstr = bqpr if qk == 0 else bkpr
        ve_b = 0
        for half in range(2):
            use_bank("p_mm1")
            for gg in range(4):
                g = half * 4 + gg
                lhsT = bcb[:, g:g + 1]
                rhs = projs[:, g * 128:(g + 1) * 128]
                o_ap = p_mm[1][0:1, gg * 128:gg * 128 + 128]
                op("tensor", lambda e, o=o_ap, l=lhsT, rr=rhs:
                   e.matmul(o, l, rr, start=True, stop=True))
            pe_b = mark("tensor")
            W("vector", (s_pe, pe_b))
            op("vector", lambda e, d=dstr[0:1, half * 512:half * 512 + 512]:
               e.tensor_copy(d, p_mm[1][0:1, 0:512]))
            ve_b = mark("vector")
            done_bank("p_mm1", ("vector", ve_b))
        return ve_b

    psall = grb.bitcast(F32)[:, 0:256]
    sqall = Asm[0].bitcast(F32)[:, 0:256]
    use_bank("p_bc")
    v = dma("sync", s_wr[0], psall.rearrange("p (cb c) -> p cb c", c=32),
            protosT.rearrange("(cb p) c -> p cb c", p=128))
    W("vector", (s_wr[0], v))
    op("vector", lambda e: e.tensor_scalar_mul(psall, psall, -2.0))
    edge("vector")
    op("vector", lambda e: e.tensor_copy(phTa, psall))
    edge("vector")
    op("vector", lambda e: e.tensor_tensor(plTa, psall, phTa, ALU.subtract))
    ve_lo2 = mark("vector")
    W("scalar", (s_ve, ve_lo2))
    op("scalar", lambda e: e.activation(sqall, psall, AF.Square, scale=-0.5))
    se_sq = mark("scalar")
    W("tensor", (s_se, se_sq))
    op("tensor", lambda e: e.matmul(p_bc[0:1, 0:256], onescol128f_s, sqall, start=True, stop=True))
    pe_p2 = mark("tensor")
    gp_protos = s_gp.n
    W("vector", (s_pe, pe_p2))
    op("vector", lambda e: e.tensor_reduce(p2r.rearrange("o (c u) -> o c u", u=1),
                                           p_bc[0:1, 0:256].rearrange("o (cb c) -> o c cb", c=32),
                                           AX.X, ALU.add))
    ve_p2a = mark("vector")
    W("gpsimd", (s_ve, ve_p2a))
    op("gpsimd", lambda e: e.tensor_copy(p2h, p2r))
    gp_p2h = mark("gpsimd")
    W("vector", (s_gp, gp_p2h))
    op("vector", lambda e: e.tensor_tensor(p2l, p2r, p2h, ALU.subtract))
    ve_p2 = mark("vector")
    done_bank("p_bc", ("vector", ve_p2))

    W("tensor", (s_wcA, gp_wcast_n), (s_gp, gp_smallcast))
    ve_bias_rows = emit_wfusion(1)



    scratch_war = [0, 0]
    xt_war = [0, 0, 0, 0]
    ln_pend = {}

    ln_vx = [0, 0, 0, 0]

    def emit_ln_load(x_src_rows, rot):
        if xt_war[rot]:
            W("sync", (s_ve, xt_war[rot]))
        ln_vx[rot] = dma("sync", s_xb[rot], xt[rot][:, :], x_src_rows)

    def emit_ln_stats(rot, want_x2=None, pre_wait=None):
        W("vector", (s_xb[rot], ln_vx[rot]))
        op("vector", lambda e, r=rot: e.bn_stats(stat6[r][:, 0:6], xt[r][:, 0:512]))
        op("vector", lambda e, r=rot: e.bn_stats(stat6[r][:, 6:12], xt[r][:, 512:1024]))
        edge("vector")
        op("vector", lambda e, r=rot: e.bn_aggr(mv[r], stat6[r]))
        ve_mv = mark("vector")
        W("scalar", (s_ve, ve_mv))
        op("scalar", lambda e, r=rot: e.activation(ln1[r], mv[r][:, 1:2], AF.Ln, bias=1e-5))
        edge("scalar")
        op("scalar", lambda e, r=rot: e.activation(msr[r][:, 1:2], ln1[r], AF.Exp, scale=-0.5))
        edge("scalar")
        op("scalar", lambda e, r=rot: e.copy(msr[r][:, 0:1], mv[r][:, 0:1]))
        se_msr = mark("scalar")
        if want_x2 is not None:
            op("scalar", lambda e, r=rot: e.activation(ln1[r], msr[r][:, 1:2], AF.Square))
            se_sq2 = mark("scalar")
        else:
            se_sq2 = se_msr
        ln_pend[rot] = (se_msr, se_sq2, want_x2, pre_wait)

    def emit_ln_A(x_src_rows, rot, want_x2=None, pre_wait=None):
        emit_ln_load(x_src_rows, rot)
        emit_ln_stats(rot, want_x2=want_x2, pre_wait=pre_wait)

    ln_bseq = [0]

    def emit_ln_B(rot, want_lo=True):
        se_msr, se_sq2, want_x2, pre_wait = ln_pend.pop(rot)
        brot = ln_bseq[0] % 2
        ln_bseq[0] += 1
        if pre_wait is not None:
            W("vector", *pre_wait)
        W("vector", (s_se, se_sq2))
        if want_x2 is not None:
            op("vector", lambda e, r=rot, d=want_x2: e.tensor_tensor(d, mv[r][:, 1:2], ln1[r], ALU.mult))
            edge("vector")
            op("vector", lambda e, d=want_x2: e.tensor_scalar_mul(d, d, 1024.0))
        op("vector", lambda e, r=rot, b=brot: e.tensor_scalar(xn32[b], xt[r], msr[r][:, 0:1], msr[r][:, 1:2],
                                                              ALU.subtract, ALU.mult))
        ve_xn = mark("vector")
        xt_war[rot] = ve_xn
        W("vector", (s_ve, ve_xn))
        if scratch_war[brot]:
            W("vector", (s_scr, s_scr.n), (s_scrA, s_scrA.n))
        op("vector", lambda e, b=brot: e.tensor_copy(hitm[b], xn32[b]))
        if want_lo:
            edge("vector")
            op("vector", lambda e, b=brot: e.tensor_tensor(lotm[b], xn32[b], hitm[b], ALU.subtract))
        ve_lo = mark("vector")
        return ve_lo, brot

    def emit_kv_tile(fm_slices, kdst, vdst):
        marks = {}
        v3 = vdst.rearrange("p (h e) -> p h e", e=65)
        for mh in range(2):
            bname = f"p_mm{mh}"
            rot = p_mm[mh]
            use_bank(bname)
            for cb in range(8):
                l = fm_slices[cb]
                rr = W_KP[cb][:, mh * 512:mh * 512 + 512]
                op("tensor", lambda e, o=rot, l_=l, r_=rr, st=(cb == 0): e.matmul(o, l_, r_, start=st, stop=False))
            bslice = bkpr[0:1, mh * 512:mh * 512 + 512]
            op("tensor", lambda e, o=rot, b_=bslice: e.matmul(o, onesc128b_s, b_, start=False, stop=True))
            pe_k = mark("tensor")
            W("scalar", (s_pe, pe_k))
            op("scalar", lambda e, d=kdst[:, mh * 512:mh * 512 + 512], s=rot:
               e.activation(d, s, AF.Relu, scale=0.125))
            se_k = mark("scalar")
            done_bank(bname, ("scalar", se_k))
            marks[("k", mh)] = se_k
        for mh in range(2):
            bname = f"p_mm{mh}"
            rot = p_mm[mh]
            use_bank(bname)
            for cb in range(8):
                l = fm_slices[cb]
                rr = wvb[cb][:, mh * 512:mh * 512 + 512]
                op("tensor", lambda e, o=rot, l_=l, r_=rr, st=(cb == 0): e.matmul(o, l_, r_, start=st, stop=False))
            bslice = bvr[0:1, mh * 512:mh * 512 + 512]
            op("tensor", lambda e, o=rot, b_=bslice: e.matmul(o, onesc128b_s, b_, start=False, stop=True))
            pe_v = mark("tensor")
            W("scalar", (s_pe, pe_v))
            dst8 = v3[:, mh * 8:mh * 8 + 8, 0:64]
            src8 = rot.rearrange("p (h d) -> p h d", d=64)
            op("scalar", lambda e, d=dst8, s=src8: e.copy(d, s))
            se_v = mark("scalar")
            done_bank(bname, ("scalar", se_v))
            marks[("v", mh)] = se_v
        return marks

    PH_DONE = [False]

    def s_banks(tt):
        return (p_ctx[0], p_ctx[1]) if tt % 2 == 0 else (p_sc, p_bc)

    def s_bname(tt):
        return ("p_ctx0", "p_ctx1") if tt % 2 == 0 else ("p_sc", "p_bc")

    def s_region(tt, h):
        g = h // 2
        half = (h % 2) * 64
        b0, b1 = s_banks(tt)
        if g < 7:
            return b0[half:half + 64, 65 * g:65 * g + 65]
        return b1[half:half + 64, 0:65]

    if ph <= 0:
        PH_DONE[0] = True
    for i in range(NT_S):
        v3i = vt[i].rearrange("p (h e) -> p h e", e=65)
        op("gpsimd", lambda e, d=v3i[:, :, 64:65]: e.memset(d, 1.0))
    op("gpsimd", lambda e: e.memset(S_acc, 0.0))
    gp_vones = mark("gpsimd")
    if PH_DONE[0]:
        NT_C_eff = 0
    else:
        NT_C_eff = NT_C
    fmA_war = [0, 0]
    scrA_done = []
    sacc_m = [0]
    peS_m = [0]

    def emit_A_group(g2):
        fmbuf = hi_fm if (g2 % 2) == 0 else lo_fm
        if fmA_war[g2 % 2]:
            W("sync", (s_pe, fmA_war[g2 % 2]))
        W("sync", (s_scrA, scrA_done[g2]))
        for cb in range(8):
            src = scr_c[g2 * 512:(g2 + 1) * 512, cb * 128:(cb + 1) * 128]
            dma("sync", s_fmA[g2 % 2], fmbuf[cb][:, 0:512], src, transpose=True)
        v_fmrd = s_fmA[g2 % 2].n
        for tt in range(g2 * 4, g2 * 4 + 4):
            ro = tt % 2
            col = (tt % 4) * 128
            fm_slices = [fmbuf[cb][:, col:col + 128] for cb in range(8)]
            W("tensor", (s_fmA[g2 % 2], v_fmrd))
            marks = emit_kv_tile(fm_slices, ktA[ro], vtA[ro])
            W("tensor", (s_se, marks[("k", 1)]), (s_se, marks[("v", 1)]))
            n0, n1 = s_bname(tt)
            use_bank(n0)
            use_bank(n1)
            for h in range(H):
                l = ktA[ro][:, h * 64:h * 64 + 64]
                rr = vtA[ro].rearrange("p (hh e) -> p hh e", e=65)[:, h, :]
                o = s_region(tt, h)
                op("tensor", lambda e, o_=o, l_=l, r_=rr:
                   e.matmul(o_, l_, r_, start=True, stop=True))
            pe_S = mark("tensor")
            peS_m[0] = pe_S
            b0, b1 = s_banks(tt)
            W("vector", (s_pe, pe_S))
            if sacc_m[0]:
                W("vector", (s_ve, sacc_m[0]))
            op("vector", lambda e, b_=b0: e.tensor_tensor(S_acc[:, 0:455], S_acc[:, 0:455], b_[:, 0:455], ALU.add))
            op("vector", lambda e, b_=b1: e.tensor_tensor(S_acc[:, 455:520], S_acc[:, 455:520], b_[:, 0:65], ALU.add))
            sacc_m[0] = mark("vector")
            done_bank(n0, ("vector", sacc_m[0]))
            done_bank(n1, ("vector", sacc_m[0]))
        fmA_war[g2 % 2] = peS_m[0]

    b1a_gp = [0, 0]
    prev_store = [0, 0]

    def emit_b1a_A(spx, t):
        basex = spx * SUB
        pw = ((s_out, max(prev_store)),) if (spx > 0 and t < 2) else None
        emit_ln_A(xm[basex + t * 128: basex + (t + 1) * 128, :], t % 4,
                  want_x2=x2cb[spx % 2][:, t:t + 1], pre_wait=pw)

    def emit_b1a_B(spx, t):
        basex = spx * SUB
        ve_lo, brot = emit_ln_B(t % 4)
        b1a_gp[brot] = ve_lo
        W("gpsimd", (s_ve, ve_lo))
        v1 = dma("gpsimd", s_scr, scr_hi[basex + t * 128: basex + (t + 1) * 128, :], hitm[brot])
        v2 = dma("gpsimd", s_scr, scr_lo[basex + t * 128: basex + (t + 1) * 128, :], lotm[brot])
        scratch_war[brot] = v2

    jobsA = [(xc[t * 128:(t + 1) * 128, :], None) for t in range(NT_C_eff)]
    if not PH_DONE[0]:
        jobsA += [(xm[t * 128:(t + 1) * 128, :], x2cb[0][:, t:t + 1]) for t in range(NT_S)]
    NJ = len(jobsA)

    def job_load(j):
        if j < NJ:
            emit_ln_load(jobsA[j][0], j % 4)

    def job_stats(j):
        if j < NJ:
            emit_ln_stats(j % 4, want_x2=jobsA[j][1])

    ve_bias_rows = 0
    if NT_C_eff:
        for j in range(min(4, NJ)):
            job_load(j)
        job_stats(0)
        job_stats(1)
        for t in range(NT_C_eff):
            ve_lo, brot = emit_ln_B(t % 4, want_lo=False)
            W("scalar", (s_ve, ve_lo))
            scratch_war[brot] = dma("scalar", s_scrA, scr_c[t * 128:(t + 1) * 128, :], hitm[brot])
            if t % 4 == 3:
                scrA_done.append(s_scrA.n)
            job_load(t + 4)
            job_stats(t + 2)
        for j in range(NT_C_eff + 2, NJ):
            job_stats(j)
        W("tensor", (s_wcA, gp_wcast_n), (s_gp, gp_smallcast))
        ve_bias_rows = emit_wfusion(1)
        W("tensor", (s_ve, ve_bias_rows), (s_gp, gp_vones))
        for t in range(NT_S):
            emit_b1a_B(0, t)
        for g2 in range(NT_C_eff // 4):
            emit_A_group(g2)
    W("vector", (s_w0, ld_small), (s_ve, sacc_m[0]))
    op("vector", lambda e: e.tensor_scalar_mul(S_seed, S_acc, flagc[:, 0:1]))
    ve_seed = mark("vector")
    pe_phaseA_last = peS_m[0]

    if ph <= 1:
        PH_DONE[0] = True

    prev_scan_pe = 0
    gp_cend = [0, 0]
    prev_ctxn_ve = 0
    prev_y_pe = pe_phaseA_last
    prev_hg_ve = 0
    prev_gates_pe = 0
    prev_stage_gd = [0, 0]
    prev_out_ve = [0, 0]
    ve_final = 0
    se_scan_prev = 0

    def emit_b1b_hi(spx):
        basex = spx * SUB
        W("sync", (s_pe, s_pe.n))
        W("sync", (s_scr, s_scr.n))
        for g4 in range(NT_S // 4):
            for cb in range(8):
                src = scr_hi[basex + g4 * 512: basex + (g4 + 1) * 512, cb * 128:(cb + 1) * 128]
                dma("sync", s_fm, hi_fm[cb][:, g4 * 512:g4 * 512 + 512], src, transpose=True)

    def emit_b1b_lo(spx, ve_guard):
        basex = spx * SUB
        if ve_guard:
            W("sync", (s_ve, ve_guard))
        W("sync", (s_scr, s_scr.n))
        for g4 in range(NT_S // 4):
            for cb in range(8):
                src = scr_lo[basex + g4 * 512: basex + (g4 + 1) * 512, cb * 128:(cb + 1) * 128]
                dma("sync", s_fm, lo_fm[cb][:, g4 * 512:g4 * 512 + 512], src, transpose=True)
        return s_fm.n

    if not PH_DONE[0]:
        emit_b1b_hi(0)
        v_fm_next = emit_b1b_lo(0, 0)

    W("gpsimd", (s_scrA, s_scrA.n), (s_scr, s_scr.n))
    emit_wfusion(0)
    ve_wq = s_ve.n
    pairs2 = [(fbiasb, fbias), (boutr, bout)]
    for cb in range(8):
        pairs2.append((woutb[cb][:, :], wout[cb * 128:(cb + 1) * 128, :]))
        pairs2.append((ballb[cb][:, :], ball[cb * 128:(cb + 1) * 128, :]))
    for p in range(4):
        pairs2.append((aflb[p][:, :], afl[p * 128:(p + 1) * 128, :]))
    gp_wcast2_n = cast_loads(pairs2, s_wcC)

    for sp in range(NSUB if not PH_DONE[0] else 0):
        base = sp * SUB
        v_fm_all = v_fm_next
        op("vector", lambda e, x2=x2cb[sp % 2]: e.tensor_copy(deng[:, 0:1], x2[:, 0:1]))
        ve_x2 = mark("vector")

        if ph <= 2:
            continue
        for tb in range(NTB_S):
            use_bank("p_sc")
            W("tensor", (s_fm, v_fm_all), (s_ve, ve_p2), (s_gp, gp_protos))
            first = True
            for cb in range(8):
                phs = phTa[:, cb * 32:(cb + 1) * 32]
                pls = plTa[:, cb * 32:(cb + 1) * 32]
                for (lh, rh) in ((phs, hi_fm[cb]), (pls, hi_fm[cb]), (phs, lo_fm[cb])):
                    rsl = rh[:, tb * 512:tb * 512 + 512]
                    op("tensor", lambda e, l=lh, rr=rsl, st=first:
                       e.matmul(p_sc[0:32, :], l, rr, start=st, stop=False))
                    first = False
            op("tensor", lambda e: e.matmul(p_sc[0:32, :], p2h, onesr512b_s, start=False, stop=False))
            op("tensor", lambda e: e.matmul(p_sc[0:32, :], p2l, onesr512b_s, start=False, stop=True))
            pe_d = mark("tensor")
            W("scalar", (s_pe, pe_d))
            op("scalar", lambda e, t_=tb: e.copy(da[:, t_ * 512:t_ * 512 + 512], p_sc[0:32, :]))
            se_dist = mark("scalar")
            done_bank("p_sc", ("scalar", se_dist))

        W("tensor", (s_se, se_dist))
        if sp > 0:
            W("tensor", (s_pe, prev_gates_pe))
        for r in range(NT_S // 4):
            use_bank("p_bc")
            for j in range(4):
                tt = r * 4 + j
                src = da[:, tt * 128:tt * 128 + 128]
                o = p_bc[0:128, j * 32:j * 32 + 32]
                op("tensor", lambda e, o_=o, s=src: e.transpose(o_, s, idn32f_s))
            pe_tr = mark("tensor")
            W("vector", (s_pe, pe_tr))
            op("vector", lambda e, r_=r: e.tensor_copy(dt1[:, r_ * 128:r_ * 128 + 128], p_bc[0:128, 0:128]))
            ve_d2 = mark("vector")
            done_bank("p_bc", ("vector", ve_d2))

        d3 = dt1.rearrange("p (j c) -> p j c", c=32)
        x2e = x2cb[sp % 2][:, 0:NT_S].rearrange("p (j o) -> p j o", o=1).broadcast_to([128, NT_S, 32])
        W("vector", (s_ve, ve_x2), (s_ve, ve_d2))
        op("vector", lambda e, d_=d3, x_=x2e: e.tensor_tensor(d_, d_, x_, ALU.add))
        ve_d2f = mark("vector")
        W("scalar", (s_ve, ve_d2f))
        op("scalar", lambda e: e.activation(dt2, dt1, AF.Relu))
        edge("scalar")
        op("scalar", lambda e: e.activation(dt3, dt2, AF.Sqrt))
        se_s0 = mark("scalar")
        W("vector", (s_se, se_s0), (s_ve, ve_tcol))
        op("vector", lambda e: e.reciprocal(dt1, dt3))
        edge("vector")
        op("vector", lambda e: e.tensor_tensor(dt1, dt2, dt1, ALU.mult))
        edge("vector")
        op("vector", lambda e: e.tensor_tensor(dt1, dt3, dt1, ALU.add))
        edge("vector")
        op("vector", lambda e: e.tensor_scalar_mul(Lg, dt1, tcol[:, 0:1]))
        ve_log = mark("vector")
        if sp + 1 < NSUB:
            emit_b1a_A(sp + 1, 0)
            emit_b1a_A(sp + 1, 1)

        L3 = Lg.rearrange("p (j c) -> p j c", c=32)
        m1o = m1g.rearrange("p (j o) -> p j o", o=1)
        m2o = m2g.rearrange("p (j o) -> p j o", o=1)
        dno = deng.rearrange("p (j o) -> p j o", o=1)
        m1e = m1o.broadcast_to([128, NT_S, 32])
        m2e = m2o.broadcast_to([128, NT_S, 32])
        rde = rdeng.rearrange("p (j o) -> p j o", o=1).broadcast_to([128, NT_S, 32])
        sel3 = sel1.rearrange("p (j c) -> p j c", c=32)
        l23 = l2b.rearrange("p (j c) -> p j c", c=32)
        ee3 = eex.rearrange("p (j c) -> p j c", c=32)
        eg3 = egb.rearrange("p (j c) -> p j c", c=32)
        gt3 = gtm.rearrange("p (j c) -> p j c", c=32)
        W("vector", (s_ve, ve_log))
        op("vector", lambda e: e.tensor_reduce(m1o, L3, AX.X, ALU.max))
        edge("vector")
        op("vector", lambda e: e.tensor_tensor(sel3, L3, m1e, ALU.is_ge))
        edge("vector")
        op("vector", lambda e: e.scalar_tensor_tensor(l23, sel3, -1e30, L3, ALU.mult, ALU.add))
        edge("vector")
        op("vector", lambda e: e.tensor_reduce(m2o, l23, AX.X, ALU.max))
        edge("vector")
        op("vector", lambda e: e.tensor_tensor(l23, L3, m1e, ALU.subtract))
        ve_g1 = mark("vector")
        W("scalar", (s_ve, ve_g1))
        op("scalar", lambda e: e.activation(eex, l2b, AF.Exp))
        se_ex = mark("scalar")
        W("vector", (s_se, se_ex))
        op("vector", lambda e: e.tensor_tensor(sel3, L3, m2e, ALU.is_ge))
        edge("vector")
        op("vector", lambda e: e.tensor_tensor(eg3, ee3, sel3, ALU.mult))
        edge("vector")
        op("vector", lambda e: e.tensor_reduce(dno, eg3, AX.X, ALU.add))
        edge("vector")
        op("vector", lambda e: e.reciprocal(rdeng, deng))
        edge("vector")
        op("vector", lambda e: e.tensor_tensor(gt3, eg3, rde, ALU.mult))
        ve_g = mark("vector")

        W("tensor", (s_ve, ve_g))
        use_bank("p_trb")

        def _bf16_transpose(e, o_, s):
            with nc.allow_low_precision(reason="bf16 gates transpose, values are bf16 by design"):
                return e.transpose(o_, s, idn128b_s)

        for tt in range(NT_S):
            src = gtm[:, tt * 32:tt * 32 + 32]
            o = p_trb[0:32, tt * 128:tt * 128 + 128]
            op("tensor", lambda e, o_=o, s=src: _bf16_transpose(e, o_, s))
        pe_gt = mark("tensor")
        prev_gates_pe = pe_gt
        W("vector", (s_pe, pe_gt))
        op("vector", lambda e: e.tensor_copy(gatesT[0:32, 0:SUB], p_trb[0:32, 0:SUB]))
        ve_gT = mark("vector")
        done_bank("p_trb", ("vector", ve_gT))
        if sp + 1 < NSUB:
            emit_b1a_B(sp + 1, 0)
            emit_b1a_A(sp + 1, 2)

        if ph <= 3:
            continue
        if sp == 0:
            W("tensor", (s_ve, ve_wq))
        banks9 = [("p_mm0", p_mm[0]), ("p_mm1", p_mm[1]), ("p_sc", p_sc), ("p_bc", p_bc)]
        for qk in range(2):
            Wsrc = W_QP if qk == 0 else W_KP
            bsrc = bqpr if qk == 0 else bkpr
            dstl = qp_fm if qk == 0 else kp_fm
            for jb in range(8):
                for tb in range(NTB_S):
                    rb = (jb * NTB_S + tb + qk * 8) % 4
                    bname, rot = banks9[rb]
                    use_bank(bname)
                    for cb in range(8):
                        l = Wsrc[cb][:, jb * 128:jb * 128 + 128]
                        rr = hi_fm[cb][:, tb * 512:tb * 512 + 512]
                        op("tensor", lambda e, o=rot, l_=l, r_=rr, st=(cb == 0):
                           e.matmul(o, l_, r_, start=st, stop=False))
                    bsl = bsrc[0:1, jb * 128:jb * 128 + 128]
                    op("tensor", lambda e, o=rot, b_=bsl: e.matmul(o, b_, onesr512b_s, start=False, stop=True))
                    pe_q = mark("tensor")
                    W("scalar", (s_pe, pe_q))
                    d = dstl[jb][:, tb * 512:tb * 512 + 512]
                    op("scalar", lambda e, d_=d, s=rot: e.activation(d_, s, AF.Relu, scale=0.125))
                    se_qk = mark("scalar")
                    done_bank(bname, ("scalar", se_qk))

        if sp + 1 < NSUB:
            emit_b1a_B(sp + 1, 1)
            emit_b1a_A(sp + 1, 3)
        if sp > 0:
            W("tensor", (s_pe, prev_scan_pe))
        for t in range(NT_S):
            g4 = t // 4
            col = g4 * 512 + (t % 4) * 128
            fm_slices = [hi_fm[cb][:, col:col + 128] for cb in range(8)]
            emit_kv_tile(fm_slices, kt[t], vt[t])
        gp_ones_last = s_gp.n
        se_kv_last = s_se.n

        if sp + 1 < NSUB:
            emit_b1a_B(sp + 1, 2)
            emit_b1a_B(sp + 1, 3)
        if sp == 0:
            W("tensor", (s_wcC, gp_wcast2_n))
        ve_hg = 0
        for prc in range(4):
            for tb in range(NTB_S):
                rb = (prc * NTB_S + tb) % 2
                bname = f"p_mm{rb}"
                rot = p_mm[rb]
                use_bank(bname)
                for cb in range(8):
                    l = ballb[cb][:, prc * 128:prc * 128 + 128]
                    rr = hi_fm[cb][:, tb * 512:tb * 512 + 512]
                    op("tensor", lambda e, o=rot, l_=l, r_=rr, st=(cb == 0), sp_=(cb == 7):
                       e.matmul(o, l_, r_, start=st, stop=sp_))
                pe_h = mark("tensor")
                W("tensor", (s_ve, ve_gT))
                use_bank("p_bc")
                l2 = E2s[:, prc * 128:prc * 128 + 128]
                r2 = gatesT[0:32, tb * 512:tb * 512 + 512]
                op("tensor", lambda e, l_=l2, r_=r2: e.matmul(p_bc, l_, r_, start=True, stop=True))
                pe_gr = mark("tensor")
                W("vector", (s_pe, pe_gr))
                if ve_hg:
                    W("vector", (s_ve, ve_hg))
                if hps_dbg is not None and prc == 0 and tb == 0 and sp == 0:
                    op("vector", lambda e, s=rot: e.tensor_copy(Asm[1], s))
                    ve_dbg = mark("vector")
                    W("gpsimd", (s_ve, ve_dbg))
                    dma("gpsimd", s_gd, hps_dbg, Asm[1])
                op("vector", lambda e: e.tensor_copy(grb, p_bc))
                ve_grb = mark("vector")
                done_bank("p_bc", ("vector", ve_grb))
                W("vector", (s_ve, ve_grb))
                if grb_dbg is not None and prc == 0 and tb == 0 and sp == 0:
                    edge("vector")
                    op("vector", lambda e: e.tensor_copy(Asm[0], grb))
                    ve_dbg2 = mark("vector")
                    W("gpsimd", (s_ve, ve_dbg2))
                    dma("gpsimd", s_gd, grb_dbg, Asm[0])
                    W("vector", (s_gd, s_gd.n))
                d = hgt[prc][:, tb * 512:tb * 512 + 512]
                op("vector", lambda e, d_=d, s=rot: e.tensor_tensor(d_, s, grb, ALU.mult))
                ve_hg = mark("vector")
                done_bank(bname, ("vector", ve_hg))
        prev_hg_ve = ve_hg
        if sp + 1 < NSUB:
            emit_b1b_hi(sp + 1)

        if ph <= 4:
            continue
        seed_src = S_seed if sp == 0 else Cend
        if sp == 0:
            W("tensor", (s_ve, ve_seed))
        else:
            W("tensor", (s_se, se_scan_prev))
        ve_asm = 0
        se_snap = 0
        pe_ctx_group = [0, 0]
        ve_ctx_evac = [0, 0]
        se_stage = [0, 0]
        pe_head_last = [0, 0]
        se_head_last = [0, 0]
        W("vector", (s_se, s_se.n))
        for h in range(H):
            hb = h // 2
            hp = (h % 2) * 64
            hr = h % 2
            ssrc = seed_src[hp:hp + 64, (h // 2) * 65:(h // 2) * 65 + 65]
            if pe_head_last[hr]:
                W("vector", (s_pe, pe_head_last[hr]))
            if sp > 0:
                W("vector", (s_se, se_scan_prev))
            if gp_cend[hr]:
                W("vector", (s_gp, gp_cend[hr]))
            op("vector", lambda e, s=qp_fm[hb][hp:hp + 64, :], d=qkst[hr][0:64, 0:SUB]:
               e.tensor_copy(d, s))
            op("vector", lambda e, s=kp_fm[hb][hp:hp + 64, :], d=qkst[hr][0:64, SUB:2 * SUB]:
               e.tensor_copy(d, s))
            op("vector", lambda e, s=ssrc, c0=hr * 130: e.tensor_copy(Csnap[0:64, c0:c0 + 65], s))
            ve_stage0 = mark("vector")
            qsl = qkst[hr][0:64, 0:SUB]
            ksl = qkst[hr][0:64, SUB:2 * SUB]
            use_bank("p_st")
            W("tensor", (s_ve, ve_stage0))
            for i in range(NCH_S):
                l6 = kt[i][:, h * 64:h * 64 + 64]
                r6 = vt[i].rearrange("p (hh e) -> p hh e", e=65)[:, h, :]
                op("tensor", lambda e, o_=p_st[0:64, i * 65:(i + 1) * 65], l_=l6, r_=r6:
                   e.matmul(o_, l_, r_, start=True, stop=True))
            pe_S4 = mark("tensor")
            W("scalar", (s_pe, pe_S4))
            if gp_cend[hr]:
                W("scalar", (s_gp, gp_cend[hr]))
            op("scalar", lambda e, d=Ssb[hr][0:64, 0:260]: e.copy(d, p_st[0:64, 0:260]))
            se_Ssb = mark("scalar")
            done_bank("p_st", ("scalar", se_Ssb))
            W("gpsimd", (s_se, se_Ssb), (s_gp, max(gp_cend)))
            op("gpsimd", lambda e, s=Csnap[0:64, hr * 130:hr * 130 + 65]:
               e.tensor_copy(cendt[0:64, 0:65], s))
            edge("gpsimd")
            for i in range(NCH_S):
                op("gpsimd", lambda e, s=Ssb[hr][0:64, i * 65:(i + 1) * 65]:
                   e.tensor_tensor(cendt[0:64, 0:65], cendt[0:64, 0:65], s, ALU.add))
                edge("gpsimd")
            dcend = Cend[hp:hp + 64, (h // 2) * 65:(h // 2) * 65 + 65]
            op("gpsimd", lambda e, d=dcend: e.tensor_copy(d, cendt[0:64, 0:65]))
            gp_cend[hr] = mark("gpsimd")
            for tb in range(NTB_S):
                use_bank("p_sc")
                for j in range(4):
                    ccol = tb * 512 + j * 128
                    l = ksl[:, ccol:ccol + 128]
                    rr = qsl[:, ccol:ccol + 128]
                    o = p_sc[:, j * 128:j * 128 + 128]
                    op("tensor", lambda e, o_=o, l_=l, r_=rr:
                       e.matmul(o_, l_, r_, start=True, stop=True))
                pe_sc = mark("tensor")
                W("vector", (s_pe, pe_sc))
                ar = Asm[tb]
                op("vector", lambda e, a=ar: e.tensor_tensor(a, p_sc, tri_s, ALU.mult))
                ve_asm = mark("vector")
                done_bank("p_sc", ("vector", ve_asm))
                cname = f"p_ctx{tb}"
                use_bank(cname)
                pc = p_ctx[tb]
                snp = Csnap[0:64, hr * 130:hr * 130 + 65]
                op("tensor", lambda e, l_=snp, r_=qsl[:, tb * 512:tb * 512 + 512]:
                   e.matmul(pc[0:65, 0:512], l_, r_, start=True, stop=False))
                W("tensor", (s_se, se_Ssb))
                for i in range(3):
                    sl = Ssb[hr][0:64, i * 65:(i + 1) * 65]
                    rtail = qsl[:, tb * 512 + (i + 1) * 128: tb * 512 + 512]
                    otail = pc[0:65, (i + 1) * 128:512]
                    op("tensor", lambda e, o_=otail, l_=sl, r_=rtail:
                       e.matmul(o_, l_, r_, start=False, stop=False))
                W("tensor", (s_ve, ve_asm))
                for j in range(4):
                    jj = tb * 4 + j
                    l4 = vt[jj].rearrange("p (hh e) -> p hh e", e=65)[:, h, :]
                    r4 = ar[:, j * 128:j * 128 + 128]
                    o4 = pc[0:65, j * 128:j * 128 + 128]
                    op("tensor", lambda e, o_=o4, l_=l4, r_=r4, sp_=(j == 3):
                       e.matmul(o_, l_, r_, start=False, stop=sp_))
                pe_ctx_mms = mark("tensor")
                pe_ctx_group[tb] = pe_ctx_mms
                W("scalar", (s_gd, s_gd.n), (s_pe, pe_ctx_group[tb]))
                dctx = ctxu[hb][hp:hp + 64, tb * 512:tb * 512 + 512]
                op("scalar", lambda e, d=dctx, t_=tb: e.copy(d, p_ctx[t_][0:64, :]))
                op("scalar", lambda e, t_=tb, r=tb % 2: e.copy(stage[r], p_ctx[t_][64:65, :]))
                se_stage[tb] = mark("scalar")
                done_bank(cname, ("scalar", se_stage[tb]))
                W("sync", (s_se, se_stage[tb]))
                dn = nall[h:h + 1, tb * 512:tb * 512 + 512]
                prev_stage_gd[tb] = dma("sync", s_gd, dn, stage[tb % 2])
            pe_head_last[hr] = s_pe.n
            se_head_last[hr] = s_se.n
        prev_scan_pe = s_pe.n
        se_scan_prev = s_se.n

        W("vector", (s_gd, prev_stage_gd[0]), (s_gd, prev_stage_gd[1]))

        def _lp(e, fn):
            with nc.allow_low_precision(reason="bf16 norm reciprocal, consistent with bf16 matmul path"):
                return fn(e)

        op("vector", lambda e: _lp(e, lambda e2: e2.tensor_scalar_add(nall, nall, 1e-6)))
        edge("vector")
        op("vector", lambda e: _lp(e, lambda e2: e2.reciprocal(rnb, nall)))
        ve_rn = mark("vector")
        ve_ctxn = 0
        for cb in range(8):
            for tb in range(NTB_S):
                W("tensor", (s_ve, ve_rn))
                use_bank("p_bc")
                l = E3s[:, cb * 128:cb * 128 + 128]
                rr = rnb[0:16, tb * 512:tb * 512 + 512]
                op("tensor", lambda e, l_=l, r_=rr: e.matmul(p_bc, l_, r_, start=True, stop=True))
                pe_rn = mark("tensor")
                W("vector", (s_pe, pe_rn))
                d = ctxn[cb][:, tb * 512:tb * 512 + 512]
                s2 = ctxu[cb][:, tb * 512:tb * 512 + 512]
                op("vector", lambda e, d_=d, s_=s2: e.tensor_tensor(d_, s_, p_bc, ALU.mult))
                ve_ctxn = mark("vector")
                done_bank("p_bc", ("vector", ve_ctxn))
        prev_ctxn_ve = ve_ctxn
        if sp + 1 < NSUB:
            v_fm_next = emit_b1b_lo(sp + 1, prev_ctxn_ve)

        if ph <= 5:
            continue
        W("tensor", (s_ve, prev_hg_ve), (s_ve, ve_gT))
        for t in range(NT_S):
            rot = t % 2
            xrot = t % 4
            if prev_out_ve[rot]:
                W("sync", (s_ve, prev_out_ve[rot]))
            if xt_war[xrot]:
                W("sync", (s_ve, xt_war[xrot]))
            vx = dma("sync", s_xb[4 + xrot], xt[xrot][:, :], xm[base + t * 128: base + (t + 1) * 128, :])
            for jh in range(2):
                bname = f"p_mm{jh}"
                pm = p_mm[jh]
                use_bank(bname)
                for prc in range(4):
                    l = hgt[prc][:, t * 128:t * 128 + 128]
                    rr = aflb[prc][:, jh * 512:jh * 512 + 512]
                    op("tensor", lambda e, o=pm, l_=l, r_=rr, st=(prc == 0):
                       e.matmul(o, l_, r_, start=st, stop=False))
                lgb = gatesT[0:32, t * 128:t * 128 + 128]
                rgb = fbiasb[0:32, jh * 512:jh * 512 + 512]
                op("tensor", lambda e, o=pm, l_=lgb, r_=rgb: e.matmul(o, l_, r_, start=False, stop=False))
                rb2 = boutr[0:1, jh * 512:jh * 512 + 512]
                op("tensor", lambda e, o=pm, b_=rb2: e.matmul(o, onesc128b_s, b_, start=False, stop=False))
                W("tensor", (s_ve, ve_ctxn))
                for cb in range(8):
                    l = ctxn[cb][:, t * 128:t * 128 + 128]
                    rr = woutb[cb][:, jh * 512:jh * 512 + 512]
                    op("tensor", lambda e, o=pm, l_=l, r_=rr, sp_=(cb == 7):
                       e.matmul(o, l_, r_, start=False, stop=sp_))
                pe_y = mark("tensor")
                W("vector", (s_pe, pe_y), (s_xb[4 + xrot], vx), (s_out, s_out.n))
                d = outb[rot][:, jh * 512:jh * 512 + 512]
                xs = xt[xrot][:, jh * 512:jh * 512 + 512]
                op("vector", lambda e, d_=d, s_=xs, p=pm: e.tensor_tensor(d_, s_, p, ALU.add))
                ve_final = mark("vector")
                done_bank(bname, ("vector", ve_final))
            xt_war[xrot] = ve_final
            prev_y_pe = s_pe.n
            W("sync", (s_ve, ve_final))
            prev_store[rot] = dma("sync", s_out, y_out[base + t * 128: base + (t + 1) * 128, :], outb[rot])
            prev_out_ve[rot] = ve_final

    W("sync", (s_out, s_out.n))

    dbg_outs = {}
    if debug:
        dbg_srcs = {
            "d_hi_fm0": (hi_fm[0], [128, SUB], BF16),
            "d_qp_fm0": (qp_fm[0], [128, SUB], BF16),
            "d_kt0": (kt[0], [128, C], BF16),
            "d_vt0": (vt[0], [128, 1040], BF16),
            "d_gatesT": (gatesT, [32, SUB], BF16),
            "d_Lg": (Lg, [128, NG], F32),
            "d_Sseed": (S_seed, [128, 520], BF16),
            "d_nall": (nall, [16, SUB], BF16),
            "d_ctxu0": (ctxu[0], [128, SUB], BF16),
            "d_hgt0": (hgt[0], [128, SUB], BF16),
            "d_x2c": (x2c, [128, NT_S], F32),
            "d_Cend": (Cend, [128, 520], BF16),
        }
        for name in debug:
            if name not in dbg_srcs:
                continue
            src, shape, dt = dbg_srcs[name]
            dv = nc.dram_tensor(name, shape, dt, kind="ExternalOutput").ap()
            dma("sync", s_out, dv, src)
        W("sync", (s_out, s_out.n))

    tmax = trunc if trunc is not None else seqc[0] + 1

    with nc.Block() as blk:
        @blk.sync
        def _(eng):
            for sq, f in ops["sync"]:
                if sq <= tmax:
                    f(eng)

        @blk.tensor
        def _(eng):
            for sq, f in ops["tensor"]:
                if sq <= tmax:
                    f(eng)

        @blk.vector
        def _(eng):
            for sq, f in ops["vector"]:
                if sq <= tmax:
                    f(eng)

        @blk.scalar
        def _(eng):
            for sq, f in ops["scalar"]:
                if sq <= tmax:
                    f(eng)

        @blk.gpsimd
        def _(eng):
            for sq, f in ops["gpsimd"]:
                if sq <= tmax:
                    f(eng)

    return nc


def _host_inputs(inputs, TM=2048, TC=2048):
    x = np.ascontiguousarray(np.asarray(inputs["L0_attn_x"], dtype=np.float32))
    w_qkv = np.asarray(inputs["L0_attn_w_qkv"], dtype=np.float32)
    b_qkv = np.asarray(inputs["L0_attn_b_qkv"], dtype=np.float32)
    w_out = np.asarray(inputs["L0_attn_w_out"], dtype=np.float32)
    b_out = np.asarray(inputs["L0_attn_b_out"], dtype=np.float32)
    proj = np.asarray(inputs["L0_attn_proj"], dtype=np.float32)
    protos = np.asarray(inputs["L0_ffn_prototypes"], dtype=np.float32)
    Bm = np.asarray(inputs["L0_ffn_B"], dtype=np.float32)
    Am = np.asarray(inputs["L0_ffn_A"], dtype=np.float32)
    bias = np.asarray(inputs["L0_ffn_bias"], dtype=np.float32)
    tempv = np.asarray(inputs["L0_ffn_temp"], dtype=np.float32).reshape(1, 1)

    common = {}
    common["wqT"] = np.ascontiguousarray(w_qkv[:, 0:C].T)
    common["wkT"] = np.ascontiguousarray(w_qkv[:, C:2 * C].T)
    common["wv"] = np.ascontiguousarray(w_qkv[:, 2 * C:3 * C])
    projP = np.zeros((128, 1024), np.float32)
    bqc = np.zeros((128, 8), np.float32)
    bkc = np.zeros((128, 8), np.float32)
    for g in range(8):
        projP[0:64, g * 128:g * 128 + 64] = proj[2 * g]
        projP[64:128, g * 128 + 64:g * 128 + 128] = proj[2 * g + 1]
        bqc[0:64, g] = b_qkv[2 * g * 64:(2 * g + 1) * 64]
        bqc[64:128, g] = b_qkv[(2 * g + 1) * 64:(2 * g + 2) * 64]
        bkc[0:64, g] = b_qkv[C + 2 * g * 64: C + (2 * g + 1) * 64]
        bkc[64:128, g] = b_qkv[C + (2 * g + 1) * 64: C + (2 * g + 2) * 64]
    common["projP"] = projP
    common["bqc"] = bqc
    common["bkc"] = bkc
    common["bv"] = np.ascontiguousarray(b_qkv[2 * C:3 * C].reshape(1, C))
    common["wout"] = w_out
    common["bout"] = b_out.reshape(1, C)
    common["protosT"] = np.ascontiguousarray(protos.T)
    common["ball"] = np.ascontiguousarray(Bm.transpose(2, 0, 1).reshape(C, 32 * 16))
    common["afl"] = np.ascontiguousarray(Am.transpose(0, 2, 1).reshape(32 * 16, C))
    common["fbias"] = bias
    common["temp"] = tempv
    common["tri"] = np.ascontiguousarray(np.tile(np.triu(np.ones((128, 128), np.float32)), (1, 4))).astype(BFNP)
    common["idn64b"] = np.eye(64, dtype=BFNP)
    common["idn128b"] = np.eye(128, dtype=BFNP)
    common["idn32f"] = np.eye(32, dtype=np.float32)
    common["ones_r512f"] = np.ones((1, 512), np.float32)
    common["ones_r512b"] = np.ones((1, 512), BFNP)
    common["ones_c128b"] = np.ones((1, 128), BFNP)
    common["ones_r128f"] = np.ones((1, 128), np.float32)
    common["ones_col128f"] = np.ones((128, 1), np.float32)
    E2 = np.zeros((32, 512), BFNP)
    for k in range(32):
        E2[k, k * 16:(k + 1) * 16] = 1
    common["E2c"] = E2
    E3 = np.zeros((16, C), BFNP)
    for r in range(8):
        E3[2 * r, r * 128:r * 128 + 64] = 1
        E3[2 * r + 1, r * 128 + 64:r * 128 + 128] = 1
    common["E3c"] = E3

    in_maps = []
    for core in range(NCORE):
        b = core // 2
        half = core % 2
        m = dict(common)
        m["xm"] = np.ascontiguousarray(x[b, half * TM:(half + 1) * TM, :])
        m["xc"] = np.ascontiguousarray(x[b, 0:TC, :])
        f = np.zeros((128, 1), np.float32)
        f[:, 0] = float(half)
        m["flag"] = f
        in_maps.append(m)
    return in_maps


def kernel(**inputs):
    TM, TC, SUB = 2048, 2048, 512
    nc = build(TM=TM, TC=TC, SUB=SUB)
    in_maps = _host_inputs(inputs, TM, TC)
    res = run_bass_kernel_spmd(nc, in_maps, core_ids=list(range(NCORE)))
    out = np.zeros((B, T, C), np.float32)
    for core in range(NCORE):
        b = core // 2
        half = core % 2
        out[b, half * TM:(half + 1) * TM, :] = res.results[core]["y"]
    return out


# version 33
# speedup vs baseline: 1.0462x; 1.0050x over previous
import numpy as np
import concourse.bass as bass
import concourse.mybir as mybir
from concourse.bass_utils import run_bass_kernel_spmd
import ml_dtypes

BFNP = ml_dtypes.bfloat16
F32 = mybir.dt.float32
BF16 = mybir.dt.bfloat16
AF = mybir.ActivationFunctionType
ALU = mybir.AluOpType
AX = mybir.AxisListType

B, T, C, H, D = 4, 4096, 1024, 16, 64
NCORE = 8


def build(TM=2048, TC=2048, SUB=512, debug=(), ph=99, trunc=None):
    NSUB = TM // SUB
    NT_S = SUB // 128
    NTB_S = SUB // 512
    NCH_S = NT_S
    NT_C = TC // 128
    NG = 32 * NT_S
    nc = bass.Bass("TRN2", target_bir_lowering=False)

    def din(name, shape, dt=F32):
        return nc.dram_tensor(name, shape, dt, kind="ExternalInput").ap()

    xm = din("xm", [TM, C])
    xc = din("xc", [TC, C])
    flag = din("flag", [128, 1])
    wqT = din("wqT", [C, C]); wkT = din("wkT", [C, C])
    wv = din("wv", [C, C])
    projP = din("projP", [128, 1024])
    bqc = din("bqc", [128, 8]); bkc = din("bkc", [128, 8])
    bv = din("bv", [1, C])
    wout = din("wout", [C, C]); bout = din("bout", [1, C])
    protosT = din("protosT", [C, 32])
    ball = din("ball", [C, 512])
    afl = din("afl", [512, C])
    fbias = din("fbias", [32, C])
    temp = din("temp", [1, 1])
    tri = din("tri", [128, 512], BF16)
    idn64b = din("idn64b", [64, 64], BF16)
    idn128b = din("idn128b", [128, 128], BF16)
    idn32f = din("idn32f", [32, 32])
    ones_r512f = din("ones_r512f", [1, 512])
    ones_r512b = din("ones_r512b", [1, 512], BF16)
    ones_c128b = din("ones_c128b", [1, 128], BF16)
    ones_r128f = din("ones_r128f", [1, 128])
    ones_col128f = din("ones_col128f", [128, 1])
    E2c = din("E2c", [32, 512], BF16)
    E3c = din("E3c", [16, C], BF16)

    y_out = nc.dram_tensor("y", [TM, C], F32, kind="ExternalOutput").ap()

    scr_hi = nc.dram_tensor("scr_hi", [TM, C], BF16, kind="Internal").ap()
    scr_lo = nc.dram_tensor("scr_lo", [TM, C], BF16, kind="Internal").ap()
    scr_c = nc.dram_tensor("scr_c", [TC, C], BF16, kind="Internal").ap()

    def sb(name, shape, dt):
        return nc.alloc_sbuf_tensor(name, shape, dt).ap()

    xt = [sb(f"xt{i}", [128, C], F32) for i in range(4)]
    xn32 = [sb(f"xn32_{i}", [128, C], F32) for i in range(2)]
    hitm = [sb(f"hitm{i}", [128, C], BF16) for i in range(2)]
    lotm = [sb(f"lotm{i}", [128, C], BF16) for i in range(2)]
    stat6 = [sb(f"stat6_{i}", [128, 12], F32) for i in range(4)]
    mv = [sb(f"mv{i}", [128, 2], F32) for i in range(4)]
    msr = [sb(f"msr{i}", [128, 2], F32) for i in range(4)]
    ln1 = [sb(f"ln1_{i}", [128, 1], F32) for i in range(4)]
    x2cb = [sb(f"x2c{i}", [128, NT_S], F32) for i in range(2)]

    hi_fm = [sb(f"hi_fm{cb}", [128, SUB], BF16) for cb in range(8)]
    lo_fm = [sb(f"lo_fm{cb}", [128, SUB], BF16) for cb in range(8)]
    ctxu = lo_fm
    qp_fm = [sb(f"qp_fm{cb}", [128, SUB], BF16) for cb in range(8)]
    kp_fm = [sb(f"kp_fm{cb}", [128, SUB], BF16) for cb in range(8)]

    kt = [sb(f"kt{i}", [128, C], BF16) for i in range(NT_S)]
    ctxn = [kt[cb // 2][:, (cb % 2) * 512:(cb % 2) * 512 + 512] for cb in range(8)]
    vt = [sb(f"vt{i}", [128, 1040], BF16) for i in range(NT_S)]
    hgt = [sb(f"hgt{p}", [128, SUB], BF16) for p in range(4)]
    grb = sb("grb", [128, 512], BF16)

    W_QP = [sb(f"W_QP{cb}", [128, C], BF16) for cb in range(8)]
    W_KP = [sb(f"W_KP{cb}", [128, C], BF16) for cb in range(8)]
    wvb = [sb(f"wvb{cb}", [128, C], BF16) for cb in range(8)]
    woutb = [sb(f"woutb{cb}", [128, C], BF16) for cb in range(8)]
    ballb = [sb(f"ballb{cb}", [128, 512], BF16) for cb in range(8)]
    aflb = [sb(f"aflb{p}", [128, C], BF16) for p in range(4)]
    fbiasb = sb("fbiasb", [32, C], BF16)
    boutr = sb("boutr", [1, C], BF16)
    bvr = sb("bvr", [1, C], BF16)
    bqpr = sb("bqpr", [1, C], BF16)
    bkpr = sb("bkpr", [1, C], BF16)
    projs = sb("projs", [128, 1024], BF16)
    bqcb = sb("bqcb", [128, 8], BF16)
    bkcb = sb("bkcb", [128, 8], BF16)
    qkst = [sb(f"qkst{i}", [64, 1024], BF16) for i in range(2)]

    phTa = sb("phTa", [128, 256], BF16)
    plTa = sb("plTa", [128, 256], BF16)
    p2r = sb("p2r", [1, 32], F32)
    p2h = sb("p2h", [1, 32], BF16)
    p2l = sb("p2l", [1, 32], BF16)
    tprep = sb("tprep", [1, 4], F32)
    tcol = sb("tcol", [128, 1], F32)
    flagc = sb("flagc", [128, 1], F32)
    S_acc = sb("S_acc", [128, 520], F32)
    S_seed = sb("S_seed", [128, 520], BF16)
    Cend = sb("Cend", [128, 520], BF16)
    Csnap = sb("Csnap", [64, 260], BF16)
    Ssb = [sb(f"Ssb{i}", [64, 260], BF16) for i in range(2)]
    cendt = sb("cendt", [64, 65], F32)

    da = sb("da", [32, SUB], F32)
    dt1 = sb("dt1", [128, NG], F32)

    Lg = sb("Lg", [128, NG], F32)
    m1g = sb("m1g", [128, NT_S], F32)
    m2g = sb("m2g", [128, NT_S], F32)
    sel1 = sb("sel1", [128, NG], F32)
    l2b = sb("l2b", [128, NG], F32)
    dt2 = sel1
    dt3 = l2b
    eex = sb("eex", [128, NG], F32)
    egb = eex
    deng = sb("deng", [128, NT_S], F32)
    rdeng = sb("rdeng", [128, NT_S], F32)
    gtm = sb("gtm", [128, NG], BF16)
    gatesT = sb("gatesT", [32, SUB], BF16)

    Asm = [sb(f"Asm{i}", [128, 512], BF16) for i in range(NTB_S)]
    hps_dbg = nc.dram_tensor("d_hps", [128, 512], BF16, kind="ExternalOutput").ap() if "d_hps" in debug else None
    grb_dbg = nc.dram_tensor("d_grb", [128, 512], BF16, kind="ExternalOutput").ap() if "d_hps" in debug else None
    nall = sb("nall", [16, SUB], BF16)
    rnb = sb("rnb", [16, SUB], BF16)
    stage = [sb(f"stage{i}", [1, 512], BF16) for i in range(2)]
    outb = xn32

    tri_s = sb("tri_s", [128, 512], BF16)
    idn128b_s = sb("idn128b_s", [128, 128], BF16)
    idn32f_s = sb("idn32f_s", [32, 32], F32)
    onesr512b_s = sb("onesr512b_s", [1, 512], BF16)
    onesc128b_s = sb("onesc128b_s", [1, 128], BF16)
    onesr128f_s = sb("onesr128f_s", [1, 128], F32)
    onescol128f_s = sb("onescol128f_s", [128, 1], F32)
    E2s = sb("E2s", [32, 512], BF16)
    E3s = sb("E3s", [16, C], BF16)
    tempf = sb("tempf", [1, 1], F32)

    ktA = [kt[0], kt[1]]
    vtA = [vt[0], vt[1]]

    def psn(name, shape, dt=F32):
        return nc.alloc_psum_tensor(name, shape, dt).ap()

    p_mm = [psn("p_mm0", [128, 512]), psn("p_mm1", [128, 512])]
    p_sc = psn("p_sc", [128, 512])
    p_ctx = [psn("p_ctx0", [128, 512]), psn("p_ctx1", [128, 512])]
    p_st = psn("p_st", [128, 512])
    p_bc = psn("p_bc", [128, 512])
    p_trb = psn("p_trb", [128, 1024], BF16)

    ops = {k: [] for k in ("sync", "tensor", "vector", "scalar", "gpsimd")}
    seqc = [0]

    def em(eng, fn):
        seqc[0] += 1
        ops[eng].append((seqc[0], fn))

    class Sem:
        def __init__(self, name):
            self.h = nc.alloc_semaphore(name)
            self.n = 0

        def inc(self, k=1):
            self.n += k
            return self.n

    s_w0 = Sem("s_w0")
    s_wcA = Sem("s_wcA")
    s_wcB = Sem("s_wcB")
    s_wcC = Sem("s_wcC")
    s_wr = [Sem("s_wrA"), Sem("s_wrB")]
    s_xb = [Sem(f"s_x{i}") for i in range(8)]
    s_scr = Sem("s_scr")
    s_scrA = Sem("s_scrA")
    s_fmA = [Sem("s_fmA0"), Sem("s_fmA1")]
    s_fm = Sem("s_fm")
    s_pe = Sem("s_pe")
    s_ve = Sem("s_ve")
    s_se = Sem("s_se")
    s_gp = Sem("s_gp")
    s_gd = Sem("s_gd")
    s_out = Sem("s_out")

    ENG_SEM = {"tensor": s_pe, "vector": s_ve, "scalar": s_se, "gpsimd": s_gp}

    def W(eng, *deps):
        for sem, val in deps:
            if val and val > 0:
                em(eng, lambda e, s=sem.h, v=val: e.wait_ge(s, v))

    def dma(eng_name, sem, out, in_, transpose=False):
        v = sem.inc(16)
        em(eng_name, lambda e, o=out, i=in_, s=sem.h, tr=transpose:
           e.dma_start(out=o, in_=i, transpose=tr).then_inc(s, 16))
        return v

    def mark(eng):
        sem = ENG_SEM[eng]
        v = sem.inc(1)
        _, prev = ops[eng].pop()
        em(eng, lambda e, p=prev, s=sem.h: p(e).then_inc(s, 1))
        return v

    def op(eng, fn):
        em(eng, fn)

    def edge(eng):
        sem = ENG_SEM[eng]
        v = mark(eng)
        W(eng, (sem, v))
        return v

    bank_war = {}

    def use_bank(name):
        for sem, v in bank_war.get(name, ()):
            W("tensor", (sem, v))

    def done_bank(name, *marks):
        bank_war[name] = [(ENG_SEM[eng], v) for eng, v in marks]

    cpool = sb("cpool", [128, 2], F32)
    nc.const_aps.register(F32, 1e-5, cpool[:, 0:1])

    for dst, src in [(tri_s, tri), (idn128b_s, idn128b),
                     (idn32f_s, idn32f),
                     (onesr512b_s, ones_r512b), (onesc128b_s, ones_c128b),
                     (onesr128f_s, ones_r128f),
                     (onescol128f_s, ones_col128f),
                     (E2s, E2c), (E3s, E3c), (tempf, temp),
                     (flagc, flag)]:
        dma("sync", s_w0, dst, src)
    ld_small = s_w0.n

    op("vector", lambda e: e.memset(cpool[:, 0:1], 1e-5))
    ve_cpool = mark("vector")

    W("scalar", (s_w0, ld_small), (s_ve, ve_cpool))
    op("scalar", lambda e: e.activation(tprep[0:1, 0:1], tempf, AF.Abs))
    se_abs = mark("scalar")
    W("vector", (s_se, se_abs))
    op("vector", lambda e: e.tensor_scalar_max(tprep[0:1, 1:2], tprep[0:1, 0:1], 0.1))
    edge("vector")
    op("vector", lambda e: e.reciprocal(tprep[0:1, 2:3], tprep[0:1, 1:2]))
    edge("vector")
    op("vector", lambda e: e.tensor_scalar_mul(tprep[0:1, 3:4], tprep[0:1, 2:3], -0.5))
    ve_t = mark("vector")
    W("tensor", (s_ve, ve_t))
    use_bank("p_bc")
    op("tensor", lambda e: e.matmul(p_bc[0:128, 0:1], onesr128f_s, tprep[0:1, 3:4], start=True, stop=True))
    pe_t = mark("tensor")
    W("vector", (s_pe, pe_t))
    op("vector", lambda e: e.tensor_copy(tcol, p_bc[0:128, 0:1]))
    ve_tcol = mark("vector")
    done_bank("p_bc", ("vector", ve_tcol))

    W("gpsimd", (s_w0, ld_small))
    op("gpsimd", lambda e: e.engine_nop())
    gp_smallcast = mark("gpsimd")

    def cast_loads(pairs, sem):
        for dst_ap, src_sl in pairs:
            dma("gpsimd", sem, dst_ap, src_sl)
        return sem.n

    pairs = [(projs, projP), (bqcb, bqc), (bkcb, bkc), (bvr, bv)]
    for cb in range(8):
        pairs.append((wvb[cb][:, :], wv[cb * 128:(cb + 1) * 128, :]))
    gp_wcast_n = cast_loads(pairs, s_wcA)

    pe_wdone = [0]

    def emit_wfusion(qk):
        Wdst = W_QP if qk == 0 else W_KP
        wsrc = wqT if qk == 0 else wkT
        if pe_wdone[0]:
            W("gpsimd", (s_pe, pe_wdone[0]))
        stg = []
        for cb in range(8):
            stg.append((qp_fm[cb][:, 0:512], wsrc[cb * 128:(cb + 1) * 128, 0:512]))
            stg.append((kp_fm[cb][:, 0:512], wsrc[cb * 128:(cb + 1) * 128, 512:1024]))
        v_stg = cast_loads(stg, s_wcB)
        W("tensor", (s_wcB, v_stg))
        for cbc in range(8):
            buf = qp_fm if cbc < 4 else kp_fm
            for gq in range(2):
                rb = (cbc * 2 + gq) % 2
                bname = f"p_mm{rb}"
                use_bank(bname)
                for gi in range(4):
                    g = gq * 4 + gi
                    lhsT = buf[g][:, (cbc % 4) * 128:(cbc % 4) * 128 + 128]
                    rhs = projs[:, g * 128:(g + 1) * 128]
                    op("tensor", lambda e, o=p_mm[rb][:, gi * 128:gi * 128 + 128], l=lhsT, rr=rhs:
                       e.matmul(o, l, rr, start=True, stop=True))
                pe_w = mark("tensor")
                W("vector", (s_pe, pe_w))
                op("vector", lambda e, d=Wdst[cbc][:, gq * 512:gq * 512 + 512], s=p_mm[rb][:, 0:512]:
                   e.tensor_copy(d, s))
                ve_w = mark("vector")
                done_bank(bname, ("vector", ve_w))
        pe_wdone[0] = pe_w
        bcb = bqcb if qk == 0 else bkcb
        dstr = bqpr if qk == 0 else bkpr
        ve_b = 0
        for half in range(2):
            use_bank("p_mm1")
            for gg in range(4):
                g = half * 4 + gg
                lhsT = bcb[:, g:g + 1]
                rhs = projs[:, g * 128:(g + 1) * 128]
                o_ap = p_mm[1][0:1, gg * 128:gg * 128 + 128]
                op("tensor", lambda e, o=o_ap, l=lhsT, rr=rhs:
                   e.matmul(o, l, rr, start=True, stop=True))
            pe_b = mark("tensor")
            W("vector", (s_pe, pe_b))
            op("vector", lambda e, d=dstr[0:1, half * 512:half * 512 + 512]:
               e.tensor_copy(d, p_mm[1][0:1, 0:512]))
            ve_b = mark("vector")
            done_bank("p_mm1", ("vector", ve_b))
        return ve_b

    psall = grb.bitcast(F32)[:, 0:256]
    sqall = Asm[0].bitcast(F32)[:, 0:256]
    use_bank("p_bc")
    v = dma("sync", s_wr[0], psall.rearrange("p (cb c) -> p cb c", c=32),
            protosT.rearrange("(cb p) c -> p cb c", p=128))
    W("vector", (s_wr[0], v))
    op("vector", lambda e: e.tensor_scalar_mul(psall, psall, -2.0))
    edge("vector")
    op("vector", lambda e: e.tensor_copy(phTa, psall))
    edge("vector")
    op("vector", lambda e: e.tensor_tensor(plTa, psall, phTa, ALU.subtract))
    ve_lo2 = mark("vector")
    W("scalar", (s_ve, ve_lo2))
    op("scalar", lambda e: e.activation(sqall, psall, AF.Square, scale=-0.5))
    se_sq = mark("scalar")
    W("tensor", (s_se, se_sq))
    op("tensor", lambda e: e.matmul(p_bc[0:1, 0:256], onescol128f_s, sqall, start=True, stop=True))
    pe_p2 = mark("tensor")
    gp_protos = s_gp.n
    W("vector", (s_pe, pe_p2))
    op("vector", lambda e: e.tensor_reduce(p2r.rearrange("o (c u) -> o c u", u=1),
                                           p_bc[0:1, 0:256].rearrange("o (cb c) -> o c cb", c=32),
                                           AX.X, ALU.add))
    ve_p2a = mark("vector")
    W("gpsimd", (s_ve, ve_p2a))
    op("gpsimd", lambda e: e.tensor_copy(p2h, p2r))
    gp_p2h = mark("gpsimd")
    W("vector", (s_gp, gp_p2h))
    op("vector", lambda e: e.tensor_tensor(p2l, p2r, p2h, ALU.subtract))
    ve_p2 = mark("vector")
    done_bank("p_bc", ("vector", ve_p2))

    W("tensor", (s_wcA, gp_wcast_n), (s_gp, gp_smallcast))
    ve_bias_rows = emit_wfusion(1)



    scratch_war = [0, 0]
    xt_war = [0, 0, 0, 0]
    ln_pend = {}

    ln_vx = [0, 0, 0, 0]

    def emit_ln_load(x_src_rows, rot):
        if xt_war[rot]:
            W("sync", (s_ve, xt_war[rot]))
        ln_vx[rot] = dma("sync", s_xb[rot], xt[rot][:, :], x_src_rows)

    def emit_ln_stats(rot, want_x2=None, pre_wait=None):
        W("vector", (s_xb[rot], ln_vx[rot]))
        op("vector", lambda e, r=rot: e.bn_stats(stat6[r][:, 0:6], xt[r][:, 0:512]))
        op("vector", lambda e, r=rot: e.bn_stats(stat6[r][:, 6:12], xt[r][:, 512:1024]))
        edge("vector")
        op("vector", lambda e, r=rot: e.bn_aggr(mv[r], stat6[r]))
        ve_mv = mark("vector")
        W("scalar", (s_ve, ve_mv))
        op("scalar", lambda e, r=rot: e.activation(ln1[r], mv[r][:, 1:2], AF.Ln, bias=1e-5))
        edge("scalar")
        op("scalar", lambda e, r=rot: e.activation(msr[r][:, 1:2], ln1[r], AF.Exp, scale=-0.5))
        edge("scalar")
        op("scalar", lambda e, r=rot: e.copy(msr[r][:, 0:1], mv[r][:, 0:1]))
        se_msr = mark("scalar")
        if want_x2 is not None:
            op("scalar", lambda e, r=rot: e.activation(ln1[r], msr[r][:, 1:2], AF.Square))
            se_sq2 = mark("scalar")
        else:
            se_sq2 = se_msr
        ln_pend[rot] = (se_msr, se_sq2, want_x2, pre_wait)

    def emit_ln_A(x_src_rows, rot, want_x2=None, pre_wait=None):
        emit_ln_load(x_src_rows, rot)
        emit_ln_stats(rot, want_x2=want_x2, pre_wait=pre_wait)

    ln_bseq = [0]

    def emit_ln_B(rot, want_lo=True):
        se_msr, se_sq2, want_x2, pre_wait = ln_pend.pop(rot)
        brot = ln_bseq[0] % 2
        ln_bseq[0] += 1
        if pre_wait is not None:
            W("vector", *pre_wait)
        W("vector", (s_se, se_sq2))
        if want_x2 is not None:
            op("vector", lambda e, r=rot, d=want_x2: e.tensor_tensor(d, mv[r][:, 1:2], ln1[r], ALU.mult))
            edge("vector")
            op("vector", lambda e, d=want_x2: e.tensor_scalar_mul(d, d, 1024.0))
        op("vector", lambda e, r=rot, b=brot: e.tensor_scalar(xn32[b], xt[r], msr[r][:, 0:1], msr[r][:, 1:2],
                                                              ALU.subtract, ALU.mult))
        ve_xn = mark("vector")
        xt_war[rot] = ve_xn
        W("vector", (s_ve, ve_xn))
        if scratch_war[brot]:
            W("vector", (s_scr, s_scr.n), (s_scrA, s_scrA.n))
        op("vector", lambda e, b=brot: e.tensor_copy(hitm[b], xn32[b]))
        if want_lo:
            edge("vector")
            op("vector", lambda e, b=brot: e.tensor_tensor(lotm[b], xn32[b], hitm[b], ALU.subtract))
        ve_lo = mark("vector")
        return ve_lo, brot

    def emit_kv_tile(fm_slices, kdst, vdst):
        marks = {}
        v3 = vdst.rearrange("p (h e) -> p h e", e=65)
        for mh in range(2):
            bname = f"p_mm{mh}"
            rot = p_mm[mh]
            use_bank(bname)
            for cb in range(8):
                l = fm_slices[cb]
                rr = W_KP[cb][:, mh * 512:mh * 512 + 512]
                op("tensor", lambda e, o=rot, l_=l, r_=rr, st=(cb == 0): e.matmul(o, l_, r_, start=st, stop=False))
            bslice = bkpr[0:1, mh * 512:mh * 512 + 512]
            op("tensor", lambda e, o=rot, b_=bslice: e.matmul(o, onesc128b_s, b_, start=False, stop=True))
            pe_k = mark("tensor")
            W("scalar", (s_pe, pe_k))
            op("scalar", lambda e, d=kdst[:, mh * 512:mh * 512 + 512], s=rot:
               e.activation(d, s, AF.Relu, scale=0.125))
            se_k = mark("scalar")
            done_bank(bname, ("scalar", se_k))
            marks[("k", mh)] = se_k
        for mh in range(2):
            bname = f"p_mm{mh}"
            rot = p_mm[mh]
            use_bank(bname)
            for cb in range(8):
                l = fm_slices[cb]
                rr = wvb[cb][:, mh * 512:mh * 512 + 512]
                op("tensor", lambda e, o=rot, l_=l, r_=rr, st=(cb == 0): e.matmul(o, l_, r_, start=st, stop=False))
            bslice = bvr[0:1, mh * 512:mh * 512 + 512]
            op("tensor", lambda e, o=rot, b_=bslice: e.matmul(o, onesc128b_s, b_, start=False, stop=True))
            pe_v = mark("tensor")
            W("scalar", (s_pe, pe_v))
            dst8 = v3[:, mh * 8:mh * 8 + 8, 0:64]
            src8 = rot.rearrange("p (h d) -> p h d", d=64)
            op("scalar", lambda e, d=dst8, s=src8: e.copy(d, s))
            se_v = mark("scalar")
            done_bank(bname, ("scalar", se_v))
            marks[("v", mh)] = se_v
        return marks

    PH_DONE = [False]

    def s_banks(tt):
        return (p_ctx[0], p_ctx[1]) if tt % 2 == 0 else (p_sc, p_bc)

    def s_bname(tt):
        return ("p_ctx0", "p_ctx1") if tt % 2 == 0 else ("p_sc", "p_bc")

    def s_region(tt, h):
        g = h // 2
        half = (h % 2) * 64
        b0, b1 = s_banks(tt)
        if g < 7:
            return b0[half:half + 64, 65 * g:65 * g + 65]
        return b1[half:half + 64, 0:65]

    if ph <= 0:
        PH_DONE[0] = True
    for i in range(NT_S):
        v3i = vt[i].rearrange("p (h e) -> p h e", e=65)
        op("gpsimd", lambda e, d=v3i[:, :, 64:65]: e.memset(d, 1.0))
    op("gpsimd", lambda e: e.memset(S_acc, 0.0))
    gp_vones = mark("gpsimd")
    if PH_DONE[0]:
        NT_C_eff = 0
    else:
        NT_C_eff = NT_C
    fmA_war = [0, 0]
    scrA_done = []
    sacc_m = [0]
    peS_m = [0]

    def emit_A_group(g2):
        fmbuf = hi_fm if (g2 % 2) == 0 else lo_fm
        if fmA_war[g2 % 2]:
            W("sync", (s_pe, fmA_war[g2 % 2]))
        W("sync", (s_scrA, scrA_done[g2]))
        for cb in range(8):
            src = scr_c[g2 * 512:(g2 + 1) * 512, cb * 128:(cb + 1) * 128]
            dma("sync", s_fmA[g2 % 2], fmbuf[cb][:, 0:512], src, transpose=True)
        v_fmrd = s_fmA[g2 % 2].n
        for tt in range(g2 * 4, g2 * 4 + 4):
            ro = tt % 2
            col = (tt % 4) * 128
            fm_slices = [fmbuf[cb][:, col:col + 128] for cb in range(8)]
            W("tensor", (s_fmA[g2 % 2], v_fmrd))
            marks = emit_kv_tile(fm_slices, ktA[ro], vtA[ro])
            W("tensor", (s_se, marks[("k", 1)]), (s_se, marks[("v", 1)]))
            n0, n1 = s_bname(tt)
            use_bank(n0)
            use_bank(n1)
            for h in range(H):
                l = ktA[ro][:, h * 64:h * 64 + 64]
                rr = vtA[ro].rearrange("p (hh e) -> p hh e", e=65)[:, h, :]
                o = s_region(tt, h)
                op("tensor", lambda e, o_=o, l_=l, r_=rr:
                   e.matmul(o_, l_, r_, start=True, stop=True))
            pe_S = mark("tensor")
            peS_m[0] = pe_S
            b0, b1 = s_banks(tt)
            W("vector", (s_pe, pe_S))
            if sacc_m[0]:
                W("vector", (s_ve, sacc_m[0]))
            op("vector", lambda e, b_=b0: e.tensor_tensor(S_acc[:, 0:455], S_acc[:, 0:455], b_[:, 0:455], ALU.add))
            op("vector", lambda e, b_=b1: e.tensor_tensor(S_acc[:, 455:520], S_acc[:, 455:520], b_[:, 0:65], ALU.add))
            sacc_m[0] = mark("vector")
            done_bank(n0, ("vector", sacc_m[0]))
            done_bank(n1, ("vector", sacc_m[0]))
        fmA_war[g2 % 2] = peS_m[0]

    b1a_gp = [0, 0]
    prev_store = [0, 0]

    def emit_b1a_A(spx, t):
        basex = spx * SUB
        pw = ((s_out, max(prev_store)),) if (spx > 0 and t < 2) else None
        emit_ln_A(xm[basex + t * 128: basex + (t + 1) * 128, :], t % 4,
                  want_x2=x2cb[spx % 2][:, t:t + 1], pre_wait=pw)

    def emit_b1a_B(spx, t):
        basex = spx * SUB
        ve_lo, brot = emit_ln_B(t % 4)
        b1a_gp[brot] = ve_lo
        W("gpsimd", (s_ve, ve_lo))
        v1 = dma("gpsimd", s_scr, scr_hi[basex + t * 128: basex + (t + 1) * 128, :], hitm[brot])
        v2 = dma("gpsimd", s_scr, scr_lo[basex + t * 128: basex + (t + 1) * 128, :], lotm[brot])
        scratch_war[brot] = v2

    jobsA = [(xc[t * 128:(t + 1) * 128, :], None) for t in range(NT_C_eff)]
    if not PH_DONE[0]:
        jobsA += [(xm[t * 128:(t + 1) * 128, :], x2cb[0][:, t:t + 1]) for t in range(NT_S)]
    NJ = len(jobsA)

    def job_load(j):
        if j < NJ:
            emit_ln_load(jobsA[j][0], j % 4)

    def job_stats(j):
        if j < NJ:
            emit_ln_stats(j % 4, want_x2=jobsA[j][1])

    ve_bias_rows = 0
    if NT_C_eff:
        for j in range(min(4, NJ)):
            job_load(j)
        job_stats(0)
        job_stats(1)
        for t in range(NT_C_eff):
            ve_lo, brot = emit_ln_B(t % 4, want_lo=False)
            W("scalar", (s_ve, ve_lo))
            scratch_war[brot] = dma("scalar", s_scrA, scr_c[t * 128:(t + 1) * 128, :], hitm[brot])
            if t % 4 == 3:
                scrA_done.append(s_scrA.n)
            job_load(t + 4)
            job_stats(t + 2)
        for j in range(NT_C_eff + 2, NJ):
            job_stats(j)
        W("tensor", (s_wcA, gp_wcast_n), (s_gp, gp_smallcast))
        ve_bias_rows = emit_wfusion(1)
        W("tensor", (s_ve, ve_bias_rows), (s_gp, gp_vones))
        for t in range(NT_S):
            emit_b1a_B(0, t)
        for g2 in range(NT_C_eff // 4):
            emit_A_group(g2)
    W("vector", (s_w0, ld_small), (s_ve, sacc_m[0]))
    op("vector", lambda e: e.tensor_scalar_mul(S_seed, S_acc, flagc[:, 0:1]))
    ve_seed = mark("vector")
    pe_phaseA_last = peS_m[0]

    if ph <= 1:
        PH_DONE[0] = True

    prev_scan_pe = 0
    gp_cend = [0, 0]
    prev_ctxn_ve = 0
    prev_y_pe = pe_phaseA_last
    prev_hg_ve = 0
    prev_gates_pe = 0
    prev_stage_gd = [0, 0]
    prev_out_ve = [0, 0]
    ve_final = 0
    se_scan_prev = 0

    def emit_b1b_hi(spx):
        basex = spx * SUB
        W("sync", (s_pe, s_pe.n))
        W("sync", (s_scr, s_scr.n))
        for g4 in range(NT_S // 4):
            for cb in range(8):
                src = scr_hi[basex + g4 * 512: basex + (g4 + 1) * 512, cb * 128:(cb + 1) * 128]
                dma("sync", s_fm, hi_fm[cb][:, g4 * 512:g4 * 512 + 512], src, transpose=True)

    def emit_b1b_lo(spx, ve_guard):
        basex = spx * SUB
        if ve_guard:
            W("sync", (s_ve, ve_guard))
        W("sync", (s_scr, s_scr.n))
        for g4 in range(NT_S // 4):
            for cb in range(8):
                src = scr_lo[basex + g4 * 512: basex + (g4 + 1) * 512, cb * 128:(cb + 1) * 128]
                dma("sync", s_fm, lo_fm[cb][:, g4 * 512:g4 * 512 + 512], src, transpose=True)
        return s_fm.n

    if not PH_DONE[0]:
        emit_b1b_hi(0)
        v_fm_next = emit_b1b_lo(0, 0)

    W("gpsimd", (s_scrA, s_scrA.n), (s_scr, s_scr.n))
    emit_wfusion(0)
    ve_wq = s_ve.n
    pairs2 = [(fbiasb, fbias), (boutr, bout)]
    for cb in range(8):
        pairs2.append((woutb[cb][:, :], wout[cb * 128:(cb + 1) * 128, :]))
        pairs2.append((ballb[cb][:, :], ball[cb * 128:(cb + 1) * 128, :]))
    for p in range(4):
        pairs2.append((aflb[p][:, :], afl[p * 128:(p + 1) * 128, :]))
    gp_wcast2_n = cast_loads(pairs2, s_wcC)

    for sp in range(NSUB if not PH_DONE[0] else 0):
        base = sp * SUB
        v_fm_all = v_fm_next
        op("vector", lambda e, x2=x2cb[sp % 2]: e.tensor_copy(deng[:, 0:1], x2[:, 0:1]))
        ve_x2 = mark("vector")

        if ph <= 2:
            continue
        for tb in range(NTB_S):
            use_bank("p_sc")
            W("tensor", (s_fm, v_fm_all), (s_ve, ve_p2), (s_gp, gp_protos))
            first = True
            for cb in range(8):
                phs = phTa[:, cb * 32:(cb + 1) * 32]
                pls = plTa[:, cb * 32:(cb + 1) * 32]
                for (lh, rh) in ((phs, hi_fm[cb]), (pls, hi_fm[cb]), (phs, lo_fm[cb])):
                    rsl = rh[:, tb * 512:tb * 512 + 512]
                    op("tensor", lambda e, l=lh, rr=rsl, st=first:
                       e.matmul(p_sc[0:32, :], l, rr, start=st, stop=False))
                    first = False
            op("tensor", lambda e: e.matmul(p_sc[0:32, :], p2h, onesr512b_s, start=False, stop=False))
            op("tensor", lambda e: e.matmul(p_sc[0:32, :], p2l, onesr512b_s, start=False, stop=True))
            pe_d = mark("tensor")
            W("scalar", (s_pe, pe_d))
            op("scalar", lambda e, t_=tb: e.copy(da[:, t_ * 512:t_ * 512 + 512], p_sc[0:32, :]))
            se_dist = mark("scalar")
            done_bank("p_sc", ("scalar", se_dist))

        W("tensor", (s_se, se_dist))
        if sp > 0:
            W("tensor", (s_pe, prev_gates_pe))
        for r in range(NT_S // 4):
            use_bank("p_bc")
            for j in range(4):
                tt = r * 4 + j
                src = da[:, tt * 128:tt * 128 + 128]
                o = p_bc[0:128, j * 32:j * 32 + 32]
                op("tensor", lambda e, o_=o, s=src: e.transpose(o_, s, idn32f_s))
            pe_tr = mark("tensor")
            W("vector", (s_pe, pe_tr))
            op("vector", lambda e, r_=r: e.tensor_copy(dt1[:, r_ * 128:r_ * 128 + 128], p_bc[0:128, 0:128]))
            ve_d2 = mark("vector")
            done_bank("p_bc", ("vector", ve_d2))

        d3 = dt1.rearrange("p (j c) -> p j c", c=32)
        x2e = x2cb[sp % 2][:, 0:NT_S].rearrange("p (j o) -> p j o", o=1).broadcast_to([128, NT_S, 32])
        W("vector", (s_ve, ve_x2), (s_ve, ve_d2))
        op("vector", lambda e, d_=d3, x_=x2e: e.tensor_tensor(d_, d_, x_, ALU.add))
        ve_d2f = mark("vector")
        W("scalar", (s_ve, ve_d2f))
        op("scalar", lambda e: e.activation(dt2, dt1, AF.Relu))
        edge("scalar")
        op("scalar", lambda e: e.activation(dt3, dt2, AF.Sqrt))
        se_s0 = mark("scalar")
        W("vector", (s_se, se_s0), (s_ve, ve_tcol))
        op("vector", lambda e: e.reciprocal(dt1, dt3))
        edge("vector")
        op("vector", lambda e: e.tensor_tensor(dt1, dt2, dt1, ALU.mult))
        edge("vector")
        op("vector", lambda e: e.tensor_tensor(dt1, dt3, dt1, ALU.add))
        edge("vector")
        op("vector", lambda e: e.tensor_scalar_mul(Lg, dt1, tcol[:, 0:1]))
        ve_log = mark("vector")
        if sp + 1 < NSUB:
            emit_b1a_A(sp + 1, 0)
            emit_b1a_A(sp + 1, 1)

        L3 = Lg.rearrange("p (j c) -> p j c", c=32)
        m1o = m1g.rearrange("p (j o) -> p j o", o=1)
        m2o = m2g.rearrange("p (j o) -> p j o", o=1)
        dno = deng.rearrange("p (j o) -> p j o", o=1)
        m1e = m1o.broadcast_to([128, NT_S, 32])
        m2e = m2o.broadcast_to([128, NT_S, 32])
        rde = rdeng.rearrange("p (j o) -> p j o", o=1).broadcast_to([128, NT_S, 32])
        sel3 = sel1.rearrange("p (j c) -> p j c", c=32)
        l23 = l2b.rearrange("p (j c) -> p j c", c=32)
        ee3 = eex.rearrange("p (j c) -> p j c", c=32)
        eg3 = egb.rearrange("p (j c) -> p j c", c=32)
        gt3 = gtm.rearrange("p (j c) -> p j c", c=32)
        W("vector", (s_ve, ve_log))
        op("vector", lambda e: e.tensor_reduce(m1o, L3, AX.X, ALU.max))
        edge("vector")
        op("vector", lambda e: e.tensor_tensor(sel3, L3, m1e, ALU.is_ge))
        edge("vector")
        op("vector", lambda e: e.scalar_tensor_tensor(l23, sel3, -1e30, L3, ALU.mult, ALU.add))
        edge("vector")
        op("vector", lambda e: e.tensor_reduce(m2o, l23, AX.X, ALU.max))
        edge("vector")
        op("vector", lambda e: e.tensor_tensor(l23, L3, m1e, ALU.subtract))
        ve_g1 = mark("vector")
        W("scalar", (s_ve, ve_g1))
        op("scalar", lambda e: e.activation(eex, l2b, AF.Exp))
        se_ex = mark("scalar")
        W("vector", (s_se, se_ex))
        op("vector", lambda e: e.tensor_tensor(sel3, L3, m2e, ALU.is_ge))
        edge("vector")
        op("vector", lambda e: e.tensor_tensor(eg3, ee3, sel3, ALU.mult))
        edge("vector")
        op("vector", lambda e: e.tensor_reduce(dno, eg3, AX.X, ALU.add))
        edge("vector")
        op("vector", lambda e: e.reciprocal(rdeng, deng))
        edge("vector")
        op("vector", lambda e: e.tensor_tensor(gt3, eg3, rde, ALU.mult))
        ve_g = mark("vector")

        W("tensor", (s_ve, ve_g))
        use_bank("p_trb")

        def _bf16_transpose(e, o_, s):
            with nc.allow_low_precision(reason="bf16 gates transpose, values are bf16 by design"):
                return e.transpose(o_, s, idn128b_s)

        for tt in range(NT_S):
            src = gtm[:, tt * 32:tt * 32 + 32]
            o = p_trb[0:32, tt * 128:tt * 128 + 128]
            op("tensor", lambda e, o_=o, s=src: _bf16_transpose(e, o_, s))
        pe_gt = mark("tensor")
        prev_gates_pe = pe_gt
        W("vector", (s_pe, pe_gt))
        op("vector", lambda e: e.tensor_copy(gatesT[0:32, 0:SUB], p_trb[0:32, 0:SUB]))
        ve_gT = mark("vector")
        done_bank("p_trb", ("vector", ve_gT))
        if sp + 1 < NSUB:
            emit_b1a_B(sp + 1, 0)
            emit_b1a_A(sp + 1, 2)

        if ph <= 3:
            continue
        if sp == 0:
            W("tensor", (s_ve, ve_wq))
        banks9 = [("p_mm0", p_mm[0]), ("p_mm1", p_mm[1]), ("p_sc", p_sc), ("p_bc", p_bc)]
        for qk in range(2):
            Wsrc = W_QP if qk == 0 else W_KP
            bsrc = bqpr if qk == 0 else bkpr
            dstl = qp_fm if qk == 0 else kp_fm
            for jb in range(8):
                for tb in range(NTB_S):
                    rb = (jb * NTB_S + tb + qk * 8) % 4
                    bname, rot = banks9[rb]
                    use_bank(bname)
                    for cb in range(8):
                        l = Wsrc[cb][:, jb * 128:jb * 128 + 128]
                        rr = hi_fm[cb][:, tb * 512:tb * 512 + 512]
                        op("tensor", lambda e, o=rot, l_=l, r_=rr, st=(cb == 0):
                           e.matmul(o, l_, r_, start=st, stop=False))
                    bsl = bsrc[0:1, jb * 128:jb * 128 + 128]
                    op("tensor", lambda e, o=rot, b_=bsl: e.matmul(o, b_, onesr512b_s, start=False, stop=True))
                    pe_q = mark("tensor")
                    W("scalar", (s_pe, pe_q))
                    d = dstl[jb][:, tb * 512:tb * 512 + 512]
                    op("scalar", lambda e, d_=d, s=rot: e.activation(d_, s, AF.Relu, scale=0.125))
                    se_qk = mark("scalar")
                    done_bank(bname, ("scalar", se_qk))

        if sp + 1 < NSUB:
            emit_b1a_B(sp + 1, 1)
            emit_b1a_A(sp + 1, 3)
        if sp > 0:
            W("tensor", (s_pe, prev_scan_pe))
        for t in range(NT_S):
            g4 = t // 4
            col = g4 * 512 + (t % 4) * 128
            fm_slices = [hi_fm[cb][:, col:col + 128] for cb in range(8)]
            emit_kv_tile(fm_slices, kt[t], vt[t])
        gp_ones_last = s_gp.n
        se_kv_last = s_se.n

        if sp + 1 < NSUB:
            emit_b1a_B(sp + 1, 2)
            emit_b1a_B(sp + 1, 3)
        if sp == 0:
            W("tensor", (s_wcC, gp_wcast2_n))
        ve_hg = 0
        for prc in range(4):
            for tb in range(NTB_S):
                rb = (prc * NTB_S + tb) % 2
                bname = f"p_mm{rb}"
                rot = p_mm[rb]
                use_bank(bname)
                for cb in range(8):
                    l = ballb[cb][:, prc * 128:prc * 128 + 128]
                    rr = hi_fm[cb][:, tb * 512:tb * 512 + 512]
                    op("tensor", lambda e, o=rot, l_=l, r_=rr, st=(cb == 0), sp_=(cb == 7):
                       e.matmul(o, l_, r_, start=st, stop=sp_))
                pe_h = mark("tensor")
                W("tensor", (s_ve, ve_gT))
                use_bank("p_bc")
                l2 = E2s[:, prc * 128:prc * 128 + 128]
                r2 = gatesT[0:32, tb * 512:tb * 512 + 512]
                op("tensor", lambda e, l_=l2, r_=r2: e.matmul(p_bc, l_, r_, start=True, stop=True))
                pe_gr = mark("tensor")
                W("vector", (s_pe, pe_gr))
                if ve_hg:
                    W("vector", (s_ve, ve_hg))
                if hps_dbg is not None and prc == 0 and tb == 0 and sp == 0:
                    op("vector", lambda e, s=rot: e.tensor_copy(Asm[1], s))
                    ve_dbg = mark("vector")
                    W("gpsimd", (s_ve, ve_dbg))
                    dma("gpsimd", s_gd, hps_dbg, Asm[1])
                op("vector", lambda e: e.tensor_copy(grb, p_bc))
                ve_grb = mark("vector")
                done_bank("p_bc", ("vector", ve_grb))
                W("vector", (s_ve, ve_grb))
                if grb_dbg is not None and prc == 0 and tb == 0 and sp == 0:
                    edge("vector")
                    op("vector", lambda e: e.tensor_copy(Asm[0], grb))
                    ve_dbg2 = mark("vector")
                    W("gpsimd", (s_ve, ve_dbg2))
                    dma("gpsimd", s_gd, grb_dbg, Asm[0])
                    W("vector", (s_gd, s_gd.n))
                d = hgt[prc][:, tb * 512:tb * 512 + 512]
                op("vector", lambda e, d_=d, s=rot: e.tensor_tensor(d_, s, grb, ALU.mult))
                ve_hg = mark("vector")
                done_bank(bname, ("vector", ve_hg))
        prev_hg_ve = ve_hg
        if sp + 1 < NSUB:
            emit_b1b_hi(sp + 1)

        if ph <= 4:
            continue
        seed_src = S_seed if sp == 0 else Cend
        if sp == 0:
            W("tensor", (s_ve, ve_seed))
        else:
            W("tensor", (s_se, se_scan_prev))
        ve_asm = 0
        se_snap = 0
        pe_ctx_group = [0, 0]
        ve_ctx_evac = [0, 0]
        se_stage = [0, 0]
        pe_head_last = [0, 0]
        se_head_last = [0, 0]
        W("vector", (s_se, s_se.n))
        for h in range(H):
            hb = h // 2
            hp = (h % 2) * 64
            hr = h % 2
            ssrc = seed_src[hp:hp + 64, (h // 2) * 65:(h // 2) * 65 + 65]
            if pe_head_last[hr]:
                W("vector", (s_pe, pe_head_last[hr]))
            if sp > 0:
                W("vector", (s_se, se_scan_prev))
            if gp_cend[hr]:
                W("vector", (s_gp, gp_cend[hr]))
            op("vector", lambda e, s=qp_fm[hb][hp:hp + 64, :], d=qkst[hr][0:64, 0:SUB]:
               e.tensor_copy(d, s))
            op("vector", lambda e, s=kp_fm[hb][hp:hp + 64, :], d=qkst[hr][0:64, SUB:2 * SUB]:
               e.tensor_copy(d, s))
            op("vector", lambda e, s=ssrc, c0=hr * 130: e.tensor_copy(Csnap[0:64, c0:c0 + 65], s))
            ve_stage0 = mark("vector")
            qsl = qkst[hr][0:64, 0:SUB]
            ksl = qkst[hr][0:64, SUB:2 * SUB]
            use_bank("p_st")
            W("tensor", (s_ve, ve_stage0))
            for i in range(NCH_S):
                l6 = kt[i][:, h * 64:h * 64 + 64]
                r6 = vt[i].rearrange("p (hh e) -> p hh e", e=65)[:, h, :]
                op("tensor", lambda e, o_=p_st[0:64, i * 65:(i + 1) * 65], l_=l6, r_=r6:
                   e.matmul(o_, l_, r_, start=True, stop=True))
            pe_S4 = mark("tensor")
            W("scalar", (s_pe, pe_S4))
            if gp_cend[hr]:
                W("scalar", (s_gp, gp_cend[hr]))
            op("scalar", lambda e, d=Ssb[hr][0:64, 0:260]: e.copy(d, p_st[0:64, 0:260]))
            se_Ssb = mark("scalar")
            done_bank("p_st", ("scalar", se_Ssb))
            W("gpsimd", (s_se, se_Ssb), (s_gp, max(gp_cend)))
            op("gpsimd", lambda e, s=Csnap[0:64, hr * 130:hr * 130 + 65]:
               e.tensor_copy(cendt[0:64, 0:65], s))
            edge("gpsimd")
            for i in range(NCH_S):
                op("gpsimd", lambda e, s=Ssb[hr][0:64, i * 65:(i + 1) * 65]:
                   e.tensor_tensor(cendt[0:64, 0:65], cendt[0:64, 0:65], s, ALU.add))
                edge("gpsimd")
            dcend = Cend[hp:hp + 64, (h // 2) * 65:(h // 2) * 65 + 65]
            op("gpsimd", lambda e, d=dcend: e.tensor_copy(d, cendt[0:64, 0:65]))
            gp_cend[hr] = mark("gpsimd")
            for tb in range(NTB_S):
                use_bank("p_sc")
                for j in range(4):
                    ccol = tb * 512 + j * 128
                    l = ksl[:, ccol:ccol + 128]
                    rr = qsl[:, ccol:ccol + 128]
                    o = p_sc[:, j * 128:j * 128 + 128]
                    op("tensor", lambda e, o_=o, l_=l, r_=rr:
                       e.matmul(o_, l_, r_, start=True, stop=True))
                pe_sc = mark("tensor")
                W("vector", (s_pe, pe_sc))
                ar = Asm[tb]
                op("vector", lambda e, a=ar: e.tensor_tensor(a, p_sc, tri_s, ALU.mult))
                ve_asm = mark("vector")
                done_bank("p_sc", ("vector", ve_asm))
                cname = f"p_ctx{tb}"
                use_bank(cname)
                pc = p_ctx[tb]
                snp = Csnap[0:64, hr * 130:hr * 130 + 65]
                op("tensor", lambda e, l_=snp, r_=qsl[:, tb * 512:tb * 512 + 512]:
                   e.matmul(pc[0:65, 0:512], l_, r_, start=True, stop=False))
                W("tensor", (s_se, se_Ssb))
                for i in range(3):
                    sl = Ssb[hr][0:64, i * 65:(i + 1) * 65]
                    rtail = qsl[:, tb * 512 + (i + 1) * 128: tb * 512 + 512]
                    otail = pc[0:65, (i + 1) * 128:512]
                    op("tensor", lambda e, o_=otail, l_=sl, r_=rtail:
                       e.matmul(o_, l_, r_, start=False, stop=False))
                W("tensor", (s_ve, ve_asm))
                for j in range(4):
                    jj = tb * 4 + j
                    l4 = vt[jj].rearrange("p (hh e) -> p hh e", e=65)[:, h, :]
                    r4 = ar[:, j * 128:j * 128 + 128]
                    o4 = pc[0:65, j * 128:j * 128 + 128]
                    op("tensor", lambda e, o_=o4, l_=l4, r_=r4, sp_=(j == 3):
                       e.matmul(o_, l_, r_, start=False, stop=sp_))
                pe_ctx_mms = mark("tensor")
                pe_ctx_group[tb] = pe_ctx_mms
                W("scalar", (s_gd, s_gd.n), (s_pe, pe_ctx_group[tb]))
                dctx = ctxu[hb][hp:hp + 64, tb * 512:tb * 512 + 512]
                op("scalar", lambda e, d=dctx, t_=tb: e.copy(d, p_ctx[t_][0:64, :]))
                op("scalar", lambda e, t_=tb, r=tb % 2: e.copy(stage[r], p_ctx[t_][64:65, :]))
                se_stage[tb] = mark("scalar")
                done_bank(cname, ("scalar", se_stage[tb]))
                W("sync", (s_se, se_stage[tb]))
                dn = nall[h:h + 1, tb * 512:tb * 512 + 512]
                prev_stage_gd[tb] = dma("sync", s_gd, dn, stage[tb % 2])
            pe_head_last[hr] = s_pe.n
            se_head_last[hr] = s_se.n
        prev_scan_pe = s_pe.n
        se_scan_prev = s_se.n

        W("vector", (s_gd, prev_stage_gd[0]), (s_gd, prev_stage_gd[1]))

        def _lp(e, fn):
            with nc.allow_low_precision(reason="bf16 norm reciprocal, consistent with bf16 matmul path"):
                return fn(e)

        op("vector", lambda e: _lp(e, lambda e2: e2.tensor_scalar_add(nall, nall, 1e-6)))
        edge("vector")
        op("vector", lambda e: _lp(e, lambda e2: e2.reciprocal(rnb, nall)))
        ve_rn = mark("vector")
        ve_ctxn = 0
        for cb in range(8):
            for tb in range(NTB_S):
                W("tensor", (s_ve, ve_rn))
                use_bank("p_bc")
                l = E3s[:, cb * 128:cb * 128 + 128]
                rr = rnb[0:16, tb * 512:tb * 512 + 512]
                op("tensor", lambda e, l_=l, r_=rr: e.matmul(p_bc, l_, r_, start=True, stop=True))
                pe_rn = mark("tensor")
                W("vector", (s_pe, pe_rn))
                d = ctxn[cb][:, tb * 512:tb * 512 + 512]
                s2 = ctxu[cb][:, tb * 512:tb * 512 + 512]
                op("vector", lambda e, d_=d, s_=s2: e.tensor_tensor(d_, s_, p_bc, ALU.mult))
                ve_ctxn = mark("vector")
                done_bank("p_bc", ("vector", ve_ctxn))
        prev_ctxn_ve = ve_ctxn
        if sp + 1 < NSUB:
            v_fm_next = emit_b1b_lo(sp + 1, prev_ctxn_ve)

        if ph <= 5:
            continue
        W("tensor", (s_ve, prev_hg_ve), (s_ve, ve_gT))
        for t in range(NT_S):
            rot = t % 2
            xrot = t % 4
            if prev_out_ve[rot]:
                W("sync", (s_ve, prev_out_ve[rot]))
            if xt_war[xrot]:
                W("sync", (s_ve, xt_war[xrot]))
            vx = dma("sync", s_xb[4 + xrot], xt[xrot][:, :], xm[base + t * 128: base + (t + 1) * 128, :])
            for jh in range(2):
                bname, pm = banks9[(t * 2 + jh) % 4]
                use_bank(bname)
                for prc in range(4):
                    l = hgt[prc][:, t * 128:t * 128 + 128]
                    rr = aflb[prc][:, jh * 512:jh * 512 + 512]
                    op("tensor", lambda e, o=pm, l_=l, r_=rr, st=(prc == 0):
                       e.matmul(o, l_, r_, start=st, stop=False))
                lgb = gatesT[0:32, t * 128:t * 128 + 128]
                rgb = fbiasb[0:32, jh * 512:jh * 512 + 512]
                op("tensor", lambda e, o=pm, l_=lgb, r_=rgb: e.matmul(o, l_, r_, start=False, stop=False))
                rb2 = boutr[0:1, jh * 512:jh * 512 + 512]
                op("tensor", lambda e, o=pm, b_=rb2: e.matmul(o, onesc128b_s, b_, start=False, stop=False))
                W("tensor", (s_ve, ve_ctxn))
                for cb in range(8):
                    l = ctxn[cb][:, t * 128:t * 128 + 128]
                    rr = woutb[cb][:, jh * 512:jh * 512 + 512]
                    op("tensor", lambda e, o=pm, l_=l, r_=rr, sp_=(cb == 7):
                       e.matmul(o, l_, r_, start=False, stop=sp_))
                pe_y = mark("tensor")
                W("vector", (s_pe, pe_y), (s_xb[4 + xrot], vx), (s_out, s_out.n))
                d = outb[rot][:, jh * 512:jh * 512 + 512]
                xs = xt[xrot][:, jh * 512:jh * 512 + 512]
                op("vector", lambda e, d_=d, s_=xs, p=pm: e.tensor_tensor(d_, s_, p, ALU.add))
                ve_final = mark("vector")
                done_bank(bname, ("vector", ve_final))
            xt_war[xrot] = ve_final
            prev_y_pe = s_pe.n
            W("sync", (s_ve, ve_final))
            prev_store[rot] = dma("sync", s_out, y_out[base + t * 128: base + (t + 1) * 128, :], outb[rot])
            prev_out_ve[rot] = ve_final

    W("sync", (s_out, s_out.n))

    dbg_outs = {}
    if debug:
        dbg_srcs = {
            "d_hi_fm0": (hi_fm[0], [128, SUB], BF16),
            "d_qp_fm0": (qp_fm[0], [128, SUB], BF16),
            "d_kt0": (kt[0], [128, C], BF16),
            "d_vt0": (vt[0], [128, 1040], BF16),
            "d_gatesT": (gatesT, [32, SUB], BF16),
            "d_Lg": (Lg, [128, NG], F32),
            "d_Sseed": (S_seed, [128, 520], BF16),
            "d_nall": (nall, [16, SUB], BF16),
            "d_ctxu0": (ctxu[0], [128, SUB], BF16),
            "d_hgt0": (hgt[0], [128, SUB], BF16),
            "d_x2c": (x2c, [128, NT_S], F32),
            "d_Cend": (Cend, [128, 520], BF16),
        }
        for name in debug:
            if name not in dbg_srcs:
                continue
            src, shape, dt = dbg_srcs[name]
            dv = nc.dram_tensor(name, shape, dt, kind="ExternalOutput").ap()
            dma("sync", s_out, dv, src)
        W("sync", (s_out, s_out.n))

    tmax = trunc if trunc is not None else seqc[0] + 1

    with nc.Block() as blk:
        @blk.sync
        def _(eng):
            for sq, f in ops["sync"]:
                if sq <= tmax:
                    f(eng)

        @blk.tensor
        def _(eng):
            for sq, f in ops["tensor"]:
                if sq <= tmax:
                    f(eng)

        @blk.vector
        def _(eng):
            for sq, f in ops["vector"]:
                if sq <= tmax:
                    f(eng)

        @blk.scalar
        def _(eng):
            for sq, f in ops["scalar"]:
                if sq <= tmax:
                    f(eng)

        @blk.gpsimd
        def _(eng):
            for sq, f in ops["gpsimd"]:
                if sq <= tmax:
                    f(eng)

    return nc


def _host_inputs(inputs, TM=2048, TC=2048):
    x = np.ascontiguousarray(np.asarray(inputs["L0_attn_x"], dtype=np.float32))
    w_qkv = np.asarray(inputs["L0_attn_w_qkv"], dtype=np.float32)
    b_qkv = np.asarray(inputs["L0_attn_b_qkv"], dtype=np.float32)
    w_out = np.asarray(inputs["L0_attn_w_out"], dtype=np.float32)
    b_out = np.asarray(inputs["L0_attn_b_out"], dtype=np.float32)
    proj = np.asarray(inputs["L0_attn_proj"], dtype=np.float32)
    protos = np.asarray(inputs["L0_ffn_prototypes"], dtype=np.float32)
    Bm = np.asarray(inputs["L0_ffn_B"], dtype=np.float32)
    Am = np.asarray(inputs["L0_ffn_A"], dtype=np.float32)
    bias = np.asarray(inputs["L0_ffn_bias"], dtype=np.float32)
    tempv = np.asarray(inputs["L0_ffn_temp"], dtype=np.float32).reshape(1, 1)

    common = {}
    common["wqT"] = np.ascontiguousarray(w_qkv[:, 0:C].T)
    common["wkT"] = np.ascontiguousarray(w_qkv[:, C:2 * C].T)
    common["wv"] = np.ascontiguousarray(w_qkv[:, 2 * C:3 * C])
    projP = np.zeros((128, 1024), np.float32)
    bqc = np.zeros((128, 8), np.float32)
    bkc = np.zeros((128, 8), np.float32)
    for g in range(8):
        projP[0:64, g * 128:g * 128 + 64] = proj[2 * g]
        projP[64:128, g * 128 + 64:g * 128 + 128] = proj[2 * g + 1]
        bqc[0:64, g] = b_qkv[2 * g * 64:(2 * g + 1) * 64]
        bqc[64:128, g] = b_qkv[(2 * g + 1) * 64:(2 * g + 2) * 64]
        bkc[0:64, g] = b_qkv[C + 2 * g * 64: C + (2 * g + 1) * 64]
        bkc[64:128, g] = b_qkv[C + (2 * g + 1) * 64: C + (2 * g + 2) * 64]
    common["projP"] = projP
    common["bqc"] = bqc
    common["bkc"] = bkc
    common["bv"] = np.ascontiguousarray(b_qkv[2 * C:3 * C].reshape(1, C))
    common["wout"] = w_out
    common["bout"] = b_out.reshape(1, C)
    common["protosT"] = np.ascontiguousarray(protos.T)
    common["ball"] = np.ascontiguousarray(Bm.transpose(2, 0, 1).reshape(C, 32 * 16))
    common["afl"] = np.ascontiguousarray(Am.transpose(0, 2, 1).reshape(32 * 16, C))
    common["fbias"] = bias
    common["temp"] = tempv
    common["tri"] = np.ascontiguousarray(np.tile(np.triu(np.ones((128, 128), np.float32)), (1, 4))).astype(BFNP)
    common["idn64b"] = np.eye(64, dtype=BFNP)
    common["idn128b"] = np.eye(128, dtype=BFNP)
    common["idn32f"] = np.eye(32, dtype=np.float32)
    common["ones_r512f"] = np.ones((1, 512), np.float32)
    common["ones_r512b"] = np.ones((1, 512), BFNP)
    common["ones_c128b"] = np.ones((1, 128), BFNP)
    common["ones_r128f"] = np.ones((1, 128), np.float32)
    common["ones_col128f"] = np.ones((128, 1), np.float32)
    E2 = np.zeros((32, 512), BFNP)
    for k in range(32):
        E2[k, k * 16:(k + 1) * 16] = 1
    common["E2c"] = E2
    E3 = np.zeros((16, C), BFNP)
    for r in range(8):
        E3[2 * r, r * 128:r * 128 + 64] = 1
        E3[2 * r + 1, r * 128 + 64:r * 128 + 128] = 1
    common["E3c"] = E3

    in_maps = []
    for core in range(NCORE):
        b = core // 2
        half = core % 2
        m = dict(common)
        m["xm"] = np.ascontiguousarray(x[b, half * TM:(half + 1) * TM, :])
        m["xc"] = np.ascontiguousarray(x[b, 0:TC, :])
        f = np.zeros((128, 1), np.float32)
        f[:, 0] = float(half)
        m["flag"] = f
        in_maps.append(m)
    return in_maps


def kernel(**inputs):
    TM, TC, SUB = 2048, 2048, 512
    nc = build(TM=TM, TC=TC, SUB=SUB)
    in_maps = _host_inputs(inputs, TM, TC)
    res = run_bass_kernel_spmd(nc, in_maps, core_ids=list(range(NCORE)))
    out = np.zeros((B, T, C), np.float32)
    for core in range(NCORE):
        b = core // 2
        half = core % 2
        out[b, half * TM:(half + 1) * TM, :] = res.results[core]["y"]
    return out
